# version 06
# baseline (speedup 1.0000x reference)
import os
import numpy as np
import ml_dtypes
import concourse.bass as bass
import concourse.mybir as mybir
from concourse.bass_utils import run_bass_kernel_spmd
from contextlib import ExitStack

F32 = mybir.dt.float32
BF16 = mybir.dt.bfloat16
I32 = mybir.dt.int32
AF = mybir.ActivationFunctionType
ALU = mybir.AluOpType
AX = mybir.AxisListType

B, T, D = 2, 8192, 1024
H, d = 16, 64
R = 2048
C = 128
NCH = R // C
NT = R // 128
BLK = 256
NBLK = R // BLK
DH = 4096
XROWS = 128 + R
WIN = 259


class Bld:
    def __init__(self, nc):
        self.nc = nc
        self.ops = {k: [] for k in ("sync", "gpsimd", "vector", "scalar", "tensor")}
        self.cnt = {k: 0 for k in ("sD", "sG", "sT", "sV", "sS", "sC", "sF")}
        self.sems = {}

    def _emit(self, eng, fn, inc=None):
        if inc is not None:
            semname, n = inc
            s = self.sems[semname]

            def g(e, fn=fn, s=s, n=n):
                fn(e).then_inc(s, n)
            self.ops[eng].append(g)
            self.cnt[semname] += n
        else:
            self.ops[eng].append(fn)

    def wait(self, eng, semname, val=None):
        s = self.sems[semname]
        v = self.cnt[semname] if val is None else val
        if v <= 0:
            return
        self.ops[eng].append(lambda e, s=s, v=v: e.wait_ge(s, v))

    def barrier(self, eng, sems=("sD", "sG", "sT", "sV", "sS")):
        for s in sems:
            self.wait(eng, s)

    def dma(self, q, out, in_):
        sn = "sD" if q == "sync" else "sG"
        self._emit(q, lambda e, o=out, i=in_: e.dma_start(out=o, in_=i, allow_slow_non_contiguous=True), (sn, 16))

    def mm(self, out, lhsT, rhs, start=True, stop=True, inc=None, tpos=None):
        self._emit("tensor",
                   lambda e, o=out, l=lhsT, r=rhs, s1=start, s2=stop, t=tpos:
                   e.matmul(o, l, r, start=s1, stop=s2, tile_position=t), inc)

    def tp(self, out, in_, ident, inc=None, tpos=None):
        self._emit("tensor",
                   lambda e, o=out, i=in_, dd=ident, t=tpos:
                   e.transpose(o, i, dd, tile_position=t), inc)

    def op(self, eng, meth, *args, inc=None, **kw):
        def f(e, m=meth, a=args, k=kw):
            return getattr(e, m)(*a, **k)
        fence = False
        if eng in ("vector", "gpsimd") and args:
            try:
                fence = args[0].free_size() < 64
            except Exception:
                fence = False
        if fence and inc is None:
            inc = ("sF", 1)
        self._emit(eng, f, inc)
        if fence:
            self.wait(eng, inc[0])

    def act(self, out, in_, func, scale=1.0, bias=0.0, accum_out=None, inc=None):
        kw = {}
        if accum_out is not None:
            kw["accum_out"] = accum_out
        self._emit("scalar",
                   lambda e, o=out, i=in_, f=func, s=scale, bb=bias, k=kw:
                   e.activation(out=o, in_=i, func=f, scale=s, bias=bb, **k), inc)


def bcl(ap, n):
    return ap.unsqueeze(ap.ndim).broadcast_to(list(ap.shape) + [n])


def build_nc():
    nc = bass.Bass("TRN2", target_bir_lowering=False, debug=False)

    def din(name, shape, dt=F32):
        return nc.dram_tensor(name, shape, dt, kind="ExternalInput").ap()

    io = {}
    io["x_seg"] = din("x_seg", [XROWS, D])
    for n in ("wq", "wk", "wv", "wg", "wo"):
        io[n] = din(n, [D, D])
    io["wb"] = din("wb", [D, H])
    for n in ("cqw", "ckw", "cvw"):
        io[n] = din(n, [D, 4])
    for n in ("cqb", "ckb", "cvb"):
        io[n] = din(n, [D, 1])
    io["nwrep"] = din("nwrep", [128, H * d])
    io["fw1"] = din("fw1", [DH, D])
    io["fw2"] = din("fw2", [D, DH])
    io["fcw"] = din("fcw", [DH, 3])
    io["fcb"] = din("fcb", [DH, 1])
    io["ident128"] = din("ident128", [128, 128])
    io["identpair"] = din("identpair", [128, 8 * d])
    io["m_trilS"] = din("m_trilS", [128, H * C])
    io["m_triuS"] = din("m_triuS", [128, H * C])
    io["m_triuI"] = din("m_triuI", [128, H * C])
    io["pmask"] = din("pmask", [128, 4])
    io["pmask1"] = din("pmask1", [128, 4])
    io["hmask"] = din("hmask", [128, 2])
    io["onescol"] = din("onescol", [128, 2])
    io["csel"] = din("csel", [4, 2])

    out = nc.dram_tensor("out", [R, D], F32, kind="ExternalOutput").ap()

    def scr(name, shape, dt=F32):
        return nc.dram_tensor(name, shape, dt).ap()

    sp = {}
    sp["lnh"] = scr("sp_lnh", [8, 128, XROWS], BF16)
    sp["lnl"] = scr("sp_lnl", [8, 128, XROWS], BF16)
    sp["q"] = scr("sp_q", [NCH, 128, 8 * C])
    sp["k"] = scr("sp_k", [NCH, 128, 8 * C])
    sp["vtm"] = scr("sp_vtm", [NCH, 128, H * d])
    sp["gate"] = scr("sp_gate", [NCH, 128, 8 * C])
    sp["zt"] = scr("sp_zt", [NCH, 128, 8 * C])
    sp["gg"] = scr("sp_gg", [NCH, 128, H * d])
    sp["mt"] = scr("sp_mt", [NCH, 128, 8 * d])
    sp["bt"] = scr("sp_bt", [NCH, 128, 8 * d])
    sp["x1"] = scr("sp_x1", [NT, 128, D])
    sp["xi"] = scr("sp_xi", [NT, 128, 1024], BF16)
    sp["hqf"] = scr("sp_hqf", [NT, 128, 32 * 128], BF16)
    sp["cc_in"] = scr("cc_in", [1, 2 * 128 * 8 * d])
    sp["cc_out"] = scr("cc_out", [4, 2 * 128 * 8 * d])
    sp["cb_in"] = scr("cb_in", [1, 2 * D])
    sp["cb_out"] = scr("cb_out", [4, 2 * D])

    bld = Bld(nc)

    with ExitStack() as ctx:
        def sb(name, shape, dt=F32):
            return ctx.enter_context(nc.sbuf_tensor(name, shape, dt))

        def psum(name, shape):
            return ctx.enter_context(nc.psum_tensor(name, shape, F32))

        for s in ("sD", "sG", "sT", "sV", "sS", "sC", "sF"):
            bld.sems[s] = ctx.enter_context(nc.semaphore(s))

        b = {}
        PF = sb("PF", [128, 28224])
        PB = sb("PB", [128, 43040], BF16)
        PI = sb("PI", [128, 2048], I32)

        def fs(a, n, p=128, p0=0):
            return PF[p0:p0 + p, a:a + n]

        def bs(a, n, p=128, p0=0):
            return PB[p0:p0 + p, a:a + n]

        b["lnt"] = fs(0, 1024); b["sqs"] = fs(1024, 1024)
        b["x1t"] = fs(2048, 1024); b["xt"] = fs(3072, 1024)
        b["lnst"] = fs(4096, 16); b["rowm"] = fs(4112, 8)
        b["scl"] = fs(4120, 128); b["asc"] = fs(4248, 256)
        b["ffsb"] = fs(4504, 4)
        b["onesrow"] = fs(8608, 128, p=1)
        b["ffs"] = fs(8736, 8, p=4)
        b["praw"] = fs(8744, 8 * WIN)
        b["pfm"] = fs(10816, 2048)
        b["beta"] = fs(4510, 2048, p=16)
        b["wstage"] = fs(8744, 8192)
        b["qfm"] = fs(8744, 1024); b["kfm"] = fs(9768, 1024)
        b["ktm"] = fs(10792, 1024); b["qtm"] = fs(11816, 1024)
        b["vtm"] = b["qtm"]
        b["kdd"] = fs(14912, 1024); b["kddf"] = fs(15936, 1024)
        b["vstage"] = fs(14912, 1024)
        b["Xb"] = fs(16960, 2048); b["Xtb"] = fs(19008, 2048)
        b["Pb"] = fs(21056, 2048); b["Ptb"] = fs(23104, 2048)
        b["rb"] = fs(25152, 2048)
        b["mrun"] = fs(27200, 512); b["brun"] = fs(27712, 512)
        b["Gtb"] = b["Pb"]
        b["stg"] = b["kdd"]; b["stg2"] = b["kddf"]
        b["Sb"] = fs(27200, 512); b["Mg"] = fs(27712, 512)
        b["c_nwrep"] = fs(14912, 1024)
        b["Bg"] = fs(11816, 512)
        b["yfm"] = fs(10792, 4096)
        b["hraw"] = fs(8744, 32 * 130)
        b["htt"] = fs(8744, 4096)
        b["hcs"] = fs(12904, 4096)
        b["hscp"] = fs(17000, 130); b["zscp"] = fs(17130, 128)
        b["hscr"] = fs(17258, 2050, p=1)
        b["zscr"] = fs(19308, 2048, p=1)
        b["cball"] = fs(21356, 2048, p=4)
        b["brow"] = fs(23404, 2048, p=1)
        b["zf"] = fs(25452, 1024)
        b["xib"] = PF[:].bitcast(BF16)[:, 52952:56024]
        b["WAh"] = bs(0, 8192); b["WAl"] = bs(8192, 8192)
        b["WBh"] = bs(16384, 8192); b["WBl"] = bs(24576, 8192)
        b["WCh"] = b["WAh"]; b["WCl"] = b["WAl"]
        b["WDh"] = b["WBh"]; b["WDl"] = b["WBl"]
        b["lnph"] = bs(32768, 8 * WIN); b["lnpl"] = bs(34840, 8 * WIN)
        b["hif"] = bs(36912, 1024); b["lof"] = bs(37936, 1024)
        b["c_trilS"] = bs(0, 2048); b["c_triuS"] = bs(2048, 2048)
        b["c_triuI"] = bs(4096, 2048)
        b["oghi"] = bs(16384, 8192); b["oglo"] = bs(24576, 8192)
        b["xiw"] = bs(32768, 8 * 130); b["xit"] = bs(33808, 1024)
        b["wq16"] = bs(32768, 8192)
        b["hqt"] = bs(34832, 4096); b["hqf"] = bs(38928, 4096)
        b["wi32"] = PI[:, 0:2048]
        b["xi32"] = PI[:, 0:1024]
        b["hi32"] = PI[:, 0:2048]
        b["c_id128"] = sb("c_id128", [128, 128])
        b["c_idpair"] = sb("c_idpair", [128, 8 * d])
        b["c_convw"] = sb("c_convw", [128, 8 * 12])
        b["c_convb"] = sb("c_convb", [128, 8 * 3])
        b["c_wbh"] = sb("c_wbh", [128, 8 * H], BF16)
        b["c_wbl"] = sb("c_wbl", [128, 8 * H], BF16)
        b["c_pm"] = sb("c_pm", [128, 4])
        b["c_pm1"] = sb("c_pm1", [128, 4])
        b["c_hm"] = sb("c_hm", [128, 2])
        b["c_ones"] = sb("c_ones", [128, 2])
        b["c_sel"] = sb("c_sel", [4, 2])
        b["hhalo"] = sb("hhalo", [128, 2 * 32])
        b["c_fcw"] = sb("c_fcw", [128, 32 * 3])
        b["c_id128b"] = sb("c_id128b", [128, 128], BF16)
        b["c_fcb"] = sb("c_fcb", [128, 32])
        b["xif_h"] = sb("xif_h", [128, 8], BF16)
        b["pX"] = psum("pX", [128, H * C])
        b["pY"] = psum("pY", [128, H * C])

        emit(nc, bld, b, io, sp, out)

        with nc.Block() as block:
            @block.sync
            def _(e):
                for f in bld.ops["sync"]:
                    f(e)

            @block.gpsimd
            def _(e):
                for f in bld.ops["gpsimd"]:
                    f(e)

            @block.vector
            def _(e):
                for f in bld.ops["vector"]:
                    f(e)

            @block.scalar
            def _(e):
                for f in bld.ops["scalar"]:
                    f(e)

            @block.tensor
            def _(e):
                for f in bld.ops["tensor"]:
                    f(e)

    return nc


_PH = int(os.environ.get("KPH", "99"))
_PB = int(os.environ.get("KPB", "99"))
_PS = int(os.environ.get("KPS", "99"))


def emit(nc, B_, b, io, sp, out):
    V, SC, G = "vector", "scalar", "gpsimd"

    def cut(k):
        if _PH < k:
            _dbg_tap()
            B_.wait("sync", "sD")
            B_.wait("sync", "sG")
            return True
        return False

    def _dbg_tap():
        dbg = os.environ.get("KDBG", "")
        if not dbg:
            return
        B_.barrier("sync")
        B_.barrier(G)
        if dbg in ("q", "k", "vtm", "gate", "zt", "gg", "x1"):
            for cc in range(16):
                B_.dma("sync", out[cc * 128:(cc + 1) * 128, :], sp[dbg][cc])
        elif dbg == "mtbt":
            for cc in range(16):
                B_.dma("sync", out[cc * 128:(cc + 1) * 128, 0:512], sp["mt"][cc])
                B_.dma("sync", out[cc * 128:(cc + 1) * 128, 512:1024], sp["bt"][cc])
        elif dbg == "xi":
            for cc in range(NT):
                B_.dma(G, out[cc * 128:(cc + 1) * 128, :], sp["xi"][cc])
        elif dbg == "lnh":
            for half_ in range(2):
                for fc in range(8):
                    B_.dma(G, b["praw"][:, 0:1024],
                           sp["lnh"][fc, :, half_ * 1024:(half_ + 1) * 1024])
                    B_.wait("sync", "sG")
                    B_.dma("sync", out[half_ * 1024 + fc * 128:
                                       half_ * 1024 + (fc + 1) * 128, :]
                           .transpose([1, 0]) if False else
                           out[half_ * 1024 + fc * 128:half_ * 1024 + (fc + 1) * 128, :],
                           b["praw"][0:128, 0:1024])
                    B_.barrier(G, ("sD",))
        elif dbg == "hif":
            B_.dma(G, out[0:128, :], b["hif"][:])
            B_.dma(G, out[128:256, :], b["lof"][:])
            B_.dma("sync", out[256:384, :], b["lnt"][:])
            B_.dma("sync", out[384:512, 0:16], b["lnst"][:])
        elif dbg == "pfm":
            B_.dma("sync", out[0:128, :], b["pfm"][:, 0:1024])
            B_.dma("sync", out[128:256, :], b["pfm"][:, 1024:2048])
            B_.dma("sync", out[256:384, :], b["praw"][:, 0:1024])
            B_.dma("sync", out[384:512, :], b["praw"][:, 1024:2048])
            B_.dma("sync", out[512:640, 0:72], b["praw"][:, 2000:2072])
        elif dbg == "p3o":
            B_.dma("sync", out[0:128, :], b["kfm"][:, 0:1024])
            B_.dma("sync", out[128:256, :], b["qfm"][:, 0:1024])
            B_.dma("sync", out[256:384, :], b["rb"][:, 0:1024])
            B_.dma("sync", out[384:512, 0:512], b["Sb"][:])
            B_.dma("sync", out[512:640, 0:256], b["asc"][:])
        elif dbg == "wo":
            B_.dma(G, out[0:128, :], b["WCh"][:, 0:1024])
            B_.dma(G, out[128:256, :], b["WCl"][:, 0:1024])
            B_.dma(G, out[256:384, :], b["WCh"][:, 1024:2048])
        elif dbg == "yfm":
            B_.dma("sync", out[0:128, :], b["yfm"][:, 0:1024])
            B_.dma("sync", out[128:256, :], b["yfm"][:, 1024:2048])
            B_.dma("sync", out[256:384, :], b["yfm"][:, 2048:3072])
            B_.dma("sync", out[384:512, :], b["yfm"][:, 3072:4096])
            B_.dma(G, out[512:640, :], b["oghi"][:, 0:1024])
            B_.dma(G, out[640:768, :], b["oghi"][:, 4096:5120])
            B_.dma(G, out[768:896, :], b["oglo"][:, 0:1024])
        elif dbg == "xseg":
            B_.dma("sync", out[:, :], io["x_seg"][128:128 + R, :])
        elif dbg == "beta":
            B_.dma("sync", out[0:16, :], b["beta"][:, 0:1024])
            B_.dma("sync", out[16:32, :], b["beta"][:, 1024:2048])
        elif dbg == "asc":
            B_.dma("sync", out[0:128, 0:256], b["asc"][:])
        elif dbg == "sb":
            B_.dma("sync", out[0:128, 0:512], b["Sb"][:])
        elif dbg == "hst":
            for g4 in range(4):
                B_.dma("sync", out[g4 * 128:(g4 + 1) * 128, :],
                       b["hcs"][:, g4 * 1024:(g4 + 1) * 1024])
            for g4 in range(4):
                B_.dma("sync", out[512 + g4 * 128:512 + (g4 + 1) * 128, :],
                       b["htt"][:, g4 * 1024:(g4 + 1) * 1024])
            B_.dma(G, out[1024:1152, :], b["xiw"][:, 0:1024])
            B_.dma(G, out[1152:1280, 0:16], b["xiw"][:, 1024:1040])
            B_.dma("sync", out[1280:1408, 0:130], b["hscp"][:])
            B_.dma(G, out[1408:1536, :], b["hqt"][:, 0:1024])
            B_.dma("sync", out[1536:1664, 0:96], b["c_fcw"][:])
            B_.dma("sync", out[1664:1792, 0:32], b["c_fcb"][:])
            B_.dma("sync", out[1792:1920, 0:64], b["hhalo"][:])
        elif dbg == "hqf":
            qq = int(os.environ.get("KDBGI", "0"))
            for cc in range(4):
                B_.dma(G, out[cc * 512:(cc + 1) * 512, :]
                       .rearrange("(a p) f -> a p f", p=128),
                       sp["hqf"][qq * 4 + cc].rearrange("p (a f) -> a p f", a=4))
    pX, pY = b["pX"], b["pY"]
    id128 = b["c_id128"][:]
    id128b = b["c_id128b"][:]
    pXb = pX[:].bitcast(BF16)
    pYb = pY[:].bitcast(BF16)
    lnst = b["lnst"]

    def half(h):
        return 64 * (h & 1)

    def rsqrt(t_ap, s_ap, w1, w2):
        B_.op(V, "reciprocal", w1, t_ap, inc=("sV", 1))
        B_.wait(SC, "sV")
        B_.act(s_ap, w1, AF.Sqrt, inc=("sS", 1))
        B_.wait(V, "sS")
        B_.op(V, "tensor_mul", w1, t_ap, s_ap)
        B_.op(V, "tensor_mul", w2, w1, s_ap)
        B_.op(V, "tensor_scalar", w1, w2, -0.5, 1.5, ALU.mult, ALU.add)
        B_.op(V, "tensor_mul", s_ap, s_ap, w1)

    def ln_core(x_ap, out_f32=None, hi_ap=None, lo_ap=None):
        P_ = x_ap.shape[0]
        st = lnst[0:P_, :]
        B_.barrier(SC)
        B_.act(b["sqs"][0:P_, :], x_ap, AF.Square,
               accum_out=st[:, 1:2], inc=("sS", 1))
        B_.wait(V, "sS")
        B_.op(V, "reduce_sum", st[:, 0:1], x_ap.unsqueeze(1), axis=AX.X,
              inc=("sF", 1))
        B_.wait(V, "sF")
        B_.op(V, "tensor_scalar_mul", st[:, 0:1], st[:, 0:1], 1.0 / D)
        B_.op(V, "tensor_mul", st[:, 3:4], st[:, 0:1], st[:, 0:1])
        B_.op(V, "tensor_scalar_mul", st[:, 2:3], st[:, 1:2], 1.0 / D)
        B_.op(V, "tensor_sub", st[:, 2:3], st[:, 2:3], st[:, 3:4])
        B_.op(V, "tensor_scalar_add", st[:, 2:3], st[:, 2:3], 1e-5)
        rsqrt(st[:, 2:3], st[:, 5:6], st[:, 6:7], st[:, 7:8])
        B_.op(V, "tensor_mul", st[:, 8:9], st[:, 0:1], st[:, 5:6])
        B_.op(V, "tensor_scalar_mul", st[:, 8:9], st[:, 8:9], -1.0,
              inc=("sV", 1))
        B_.wait(SC, "sV")
        if out_f32 is not None:
            B_.act(out_f32, x_ap, AF.Identity, scale=st[:, 5:6], bias=st[:, 8:9],
                   inc=("sS", 1))
        if hi_ap is not None:
            B_.act(hi_ap, x_ap, AF.Identity, scale=st[:, 5:6], bias=st[:, 8:9],
                   inc=("sS", 1))
            B_.wait(V, "sS")
            B_.op(V, "scalar_tensor_tensor", b["sqs"][0:P_, :], x_ap,
                  st[:, 5:6], hi_ap, ALU.mult, ALU.subtract)
            B_.op(V, "tensor_scalar", lo_ap, b["sqs"][0:P_, :],
                  st[:, 8:9], None, ALU.add, ALU.bypass, inc=("sV", 1))

    for name, buf in (("ident128", "c_id128"), ("identpair", "c_idpair"),
                      ("pmask", "c_pm"), ("pmask1", "c_pm1"),
                      ("hmask", "c_hm"), ("onescol", "c_ones"),
                      ("csel", "c_sel")):
        B_.dma("sync", b[buf][:], io[name][:])
    for i, (wn, bn) in enumerate((("cqw", "cqb"), ("ckw", "ckb"),
                                  ("cvw", "cvb"))):
        B_.dma("sync", b["c_convw"][:, i * 32:(i + 1) * 32]
               .rearrange("p (c f) -> p c f", c=8),
               io[wn].rearrange("(c p) f -> p c f", p=128))
        B_.dma("sync", b["c_convb"][:, i * 8:(i + 1) * 8]
               .rearrange("p (c f) -> p c f", c=8),
               io[bn].rearrange("(c p) f -> p c f", p=128))
    B_.dma(G, b["c_id128b"][:], io["ident128"][:])
    B_.dma(G, b["c_wbh"][:].rearrange("p (c f) -> p c f", c=8),
           io["wb"].rearrange("(c p) f -> p c f", p=128))
    B_.dma("sync", b["wstage"][:, :8 * H].rearrange("p (c f) -> p c f", c=8),
           io["wb"].rearrange("(c p) f -> p c f", p=128))
    B_.wait(V, "sD")
    B_.wait(V, "sG")
    B_.op(V, "tensor_sub", b["c_wbl"][:], b["wstage"][:, :8 * H],
          b["c_wbh"][:], inc=("sV", 1))
    B_.barrier("tensor")
    B_.mm(pX[0:1, 0:128], b["c_ones"][:, 0:1], id128, inc=("sT", 1))
    B_.wait(V, "sT")
    B_.op(V, "tensor_copy", b["onesrow"][:], pX[0:1, 0:128], inc=("sV", 1))

    def load_wpair(wname, hi, lo):
        B_.barrier(G, ("sV",))
        B_.barrier("sync", ("sV",))
        B_.dma(G, b[hi][:].rearrange("p (c f) -> p c f", c=8),
               io[wname].rearrange("(c p) f -> p c f", p=128))
        B_.dma("sync", b["wstage"][:].rearrange("p (c f) -> p c f", c=8),
               io[wname].rearrange("(c p) f -> p c f", p=128))
        B_.wait(V, "sD")
        B_.wait(V, "sG")
        B_.op(V, "tensor_sub", b[lo][:], b["wstage"][:], b[hi][:],
              inc=("sV", 1))

    load_wpair("wq", "WAh", "WAl")
    load_wpair("wk", "WBh", "WBl")

    for tt in range(17):
        B_.barrier("sync")
        B_.dma("sync", b["lnt"][:], io["x_seg"][tt * 128:(tt + 1) * 128, :])
        B_.wait(SC, "sD")
        B_.wait(V, "sD")
        ln_core(b["lnt"][:], hi_ap=b["hif"][:], lo_ap=b["lof"][:])
        B_.barrier("tensor")
        for fc in range(8):
            B_.tp(pXb[:, fc * 128:(fc + 1) * 128],
                  b["hif"][:, fc * 128:(fc + 1) * 128], id128b)
        for fc in range(8):
            B_.tp(pYb[:, fc * 128:(fc + 1) * 128],
                  b["lof"][:, fc * 128:(fc + 1) * 128], id128b,
                  inc=("sT", 1) if fc == 7 else None)
        B_.wait(V, "sT")
        B_.op(V, "tensor_copy", b["hif"][:], pXb[:, :1024])
        B_.op(V, "tensor_copy", b["lof"][:], pYb[:, :1024], inc=("sV", 1))
        B_.wait("sync", "sV")
        B_.dma("sync", sp["lnh"][:, :, tt * 128:(tt + 1) * 128].transpose([1, 0, 2]),
               b["hif"][:].rearrange("p (c f) -> p c f", c=8))
        B_.dma("sync", sp["lnl"][:, :, tt * 128:(tt + 1) * 128].transpose([1, 0, 2]),
               b["lof"][:].rearrange("p (c f) -> p c f", c=8))

    def project(hi, lo, psrc, ncols, dst, dstw, m_list=range(8), pstart=0):
        for mi, m in enumerate(m_list):
            bank = mi % 4
            po = pX[:, bank * 512:bank * 512 + ncols]
            B_.barrier("tensor", ("sV", "sS", "sD", "sG"))
            for kk in range(8):
                B_.mm(po, b[hi][:, kk * D + m * 128:kk * D + (m + 1) * 128],
                      b["lnph"][:, kk * WIN + psrc:kk * WIN + psrc + ncols],
                      start=(kk == 0), stop=False)
                B_.mm(po, b[lo][:, kk * D + m * 128:kk * D + (m + 1) * 128],
                      b["lnph"][:, kk * WIN + psrc:kk * WIN + psrc + ncols],
                      start=False, stop=False)
                B_.mm(po, b[hi][:, kk * D + m * 128:kk * D + (m + 1) * 128],
                      b["lnpl"][:, kk * WIN + psrc:kk * WIN + psrc + ncols],
                      start=False, stop=(kk == 7),
                      inc=("sT", 1) if kk == 7 else None)
            B_.wait(V, "sT")
            B_.op(V, "tensor_copy",
                  dst[:, m * dstw + pstart:m * dstw + pstart + ncols], po,
                  inc=("sV", 1))

    def conv_silu(tensor_idx, do_silu=True):
        cw = b["c_convw"]
        cb_ = b["c_convb"]
        for m in range(8):
            wofs = tensor_idx * 32 + m * 4
            B_.barrier(SC, ("sV", "sD", "sG"))
            B_.act(b["pfm"][:, m * BLK:(m + 1) * BLK],
                   b["praw"][:, m * WIN + 3:m * WIN + 3 + BLK], AF.Identity,
                   scale=cw[:, wofs + 3:wofs + 4],
                   bias=cb_[:, tensor_idx * 8 + m:tensor_idx * 8 + m + 1],
                   inc=("sS", 1))
        B_.wait(V, "sS")
        for j in range(3):
            for m in range(8):
                wofs = tensor_idx * 32 + m * 4
                B_.op(V, "scalar_tensor_tensor",
                      b["pfm"][:, m * BLK:(m + 1) * BLK],
                      b["praw"][:, m * WIN + j:m * WIN + j + BLK],
                      cw[:, wofs + j:wofs + j + 1],
                      b["pfm"][:, m * BLK:(m + 1) * BLK],
                      ALU.mult, ALU.add,
                      inc=("sV", 1) if (j == 2 and m == 7) else None)
        if do_silu:
            B_.wait(SC, "sV")
            B_.act(b["pfm"][:], b["pfm"][:], AF.Silu, inc=("sS", 1))

    def load_window(blk):
        base = 125 + BLK * blk
        B_.barrier("sync")
        B_.dma("sync", b["lnph"][:].rearrange("p (c f) -> p c f", c=8),
               sp["lnh"][:, :, base:base + WIN].transpose([1, 0, 2]))
        B_.dma("sync", b["lnpl"][:].rearrange("p (c f) -> p c f", c=8),
               sp["lnl"][:, :, base:base + WIN].transpose([1, 0, 2]))

    def spill_fm(spname, blk, src):
        for hh in range(2):
            cc = blk * 2 + hh
            B_.dma("sync", sp[spname][cc].rearrange("p (c f) -> p c f", c=8),
                   src.rearrange("p (c f) -> p c f", c=8)[:, :, hh * 128:(hh + 1) * 128])

    if cut(2):
        return
    for blk in range(NBLK):
        load_window(blk)
        B_.wait("tensor", "sD")
        project("WAh", "WAl", 0, WIN, b["praw"], WIN)
        conv_silu(0)
        B_.wait("sync", "sS")
        spill_fm("q", blk, b["pfm"])
        project("WBh", "WBl", 0, WIN, b["praw"], WIN)
        conv_silu(1)
        B_.wait("sync", "sS")
        spill_fm("k", blk, b["pfm"])
        B_.barrier("tensor")
        po = pY[0:16, 0:BLK]
        for kk in range(8):
            B_.mm(po, b["c_wbh"][:, kk * H:(kk + 1) * H],
                  b["lnph"][:, kk * WIN + 3:kk * WIN + 3 + BLK],
                  start=(kk == 0), stop=False)
            B_.mm(po, b["c_wbl"][:, kk * H:(kk + 1) * H],
                  b["lnph"][:, kk * WIN + 3:kk * WIN + 3 + BLK],
                  start=False, stop=False)
            B_.mm(po, b["c_wbh"][:, kk * H:(kk + 1) * H],
                  b["lnpl"][:, kk * WIN + 3:kk * WIN + 3 + BLK],
                  start=False, stop=(kk == 7),
                  inc=("sT", 1) if kk == 7 else None)
        B_.wait(SC, "sT")
        B_.act(b["beta"][:, blk * BLK:(blk + 1) * BLK], po, AF.Sigmoid,
               inc=("sS", 1))

    if cut(3):
        return
    load_wpair("wv", "WCh", "WCl")
    load_wpair("wg", "WDh", "WDl")
    for blk in range(NBLK):
        load_window(blk)
        B_.wait("tensor", "sD")
        if _PB < 2:
            continue
        project("WCh", "WCl", 0, WIN, b["praw"], WIN)
        conv_silu(2)
        if _PB < 3:
            continue
        for hh in range(2):
            cc = blk * 2 + hh
            B_.barrier("tensor")
            for i_, h in enumerate([x for x in range(H) if x % 2 == 0] + [x for x in range(H) if x % 2 == 1]):
                B_.tp(pX[:, h * d:(h + 1) * d],
                      b["pfm"][half(h):half(h) + 64,
                               (h >> 1) * BLK + hh * 128:(h >> 1) * BLK + (hh + 1) * 128],
                      b["c_idpair"][half(h):half(h) + 64, 0:64],
                      inc=("sT", 1) if i_ in (7, H - 1) else None,
                      tpos=(half(h), 0))
                if i_ == 7:
                    B_.wait("tensor", "sT")
            B_.wait(V, "sT")
            B_.op(V, "tensor_copy", b["vstage"][:], pX[:, :H * d],
                  inc=("sV", 1))
            B_.wait("sync", "sV")
            B_.dma("sync", sp["vtm"][cc], b["vstage"][:])
        if _PB < 4:
            continue
        project("WDh", "WDl", 3, BLK, b["pfm"], BLK)
        B_.wait("sync", "sV")
        spill_fm("gate", blk, b["pfm"])

    if cut(4):
        return
    B_.barrier(G)
    for name, buf in (("m_trilS", "c_trilS"), ("m_triuS", "c_triuS"),
                      ("m_triuI", "c_triuI")):
        B_.dma(G, b[buf][:], io[name][:])
    sc_bcol = b["scl"][:, 0:16]
    sc_ssk = b["scl"][:, 16:32]
    sc_ssq = b["scl"][:, 32:48]
    sc_tmp = b["scl"][:, 48:64]
    sc_bsc = b["scl"][:, 64:80]
    sc_bb = b["scl"][:, 80:96]
    sc_bb2 = b["scl"][:, 96:112]
    sc_w2 = b["scl"][:, 112:128]
    B_.barrier(V)
    B_.op(V, "tensor_copy", b["mrun"][:], b["c_idpair"][:])
    B_.op(V, "memset", b["brun"][:], 0.0, inc=("sV", 1))

    for cc in range(NCH):
        B_.barrier("sync")
        B_.dma("sync", b["qfm"][:], sp["q"][cc])
        B_.dma("sync", b["kfm"][:], sp["k"][cc])
        B_.barrier("tensor")
        for i_, h in enumerate([x for x in range(H) if x % 2 == 0] + [x for x in range(H) if x % 2 == 1]):
            B_.tp(pX[:, h * d:(h + 1) * d],
                  b["kfm"][half(h):half(h) + 64, (h >> 1) * C:(h >> 1) * C + C],
                  b["c_idpair"][half(h):half(h) + 64, 0:64], tpos=(half(h), 0),
                  inc=("sT", 1) if i_ in (7, H - 1) else None)
            if i_ == 7:
                B_.wait("tensor", "sT")
        for i_, h in enumerate([x for x in range(H) if x % 2 == 0] + [x for x in range(H) if x % 2 == 1]):
            B_.tp(pY[:, h * d:(h + 1) * d],
                  b["qfm"][half(h):half(h) + 64, (h >> 1) * C:(h >> 1) * C + C],
                  b["c_idpair"][half(h):half(h) + 64, 0:64],
                  inc=("sT", 1) if i_ in (7, H - 1) else None, tpos=(half(h), 0))
            if i_ == 7:
                B_.wait("tensor", "sT")
        B_.wait(V, "sT")
        B_.op(V, "tensor_copy", b["ktm"][:], pX[:, :H * d])
        B_.op(V, "tensor_copy", b["qtm"][:], pY[:, :H * d], inc=("sV", 1))
        B_.barrier("tensor", ("sV",))
        B_.tp(pX[:, 0:16], b["beta"][:, cc * C:(cc + 1) * C],
              b["c_idpair"][0:16, 0:16], inc=("sT", 1))
        B_.wait(V, "sT")
        B_.op(V, "tensor_copy", sc_bcol, pX[:, 0:16], inc=("sV", 1))
        B_.wait(SC, "sV")
        B_.act(b["sqs"][:], b["ktm"][:], AF.Square, inc=("sS", 1))
        B_.wait(V, "sS")
        B_.op(V, "reduce_sum", sc_ssk.unsqueeze(2),
              b["sqs"][:].rearrange("p (h j) -> p h j", h=H), axis=AX.X,
              inc=("sV", 1))
        B_.wait(V, "sV")
        B_.wait(SC, "sV")
        B_.act(b["sqs"][:], b["qtm"][:], AF.Square, inc=("sS", 1))
        B_.wait(V, "sS")
        B_.op(V, "reduce_sum", sc_ssq.unsqueeze(2),
              b["sqs"][:].rearrange("p (h j) -> p h j", h=H), axis=AX.X,
              inc=("sF", 1))
        B_.wait(V, "sF")
        B_.op(V, "tensor_scalar_add", sc_tmp, sc_ssk, 1e-6)
        rsqrt(sc_tmp, sc_bsc, sc_bb, sc_w2)
        B_.op(V, "tensor_scalar", sc_tmp, sc_ssq, 64.0, 64e-6, ALU.mult,
              ALU.add)
        rsqrt(sc_tmp, b["asc"][:, cc * H:(cc + 1) * H], sc_bb, sc_w2)
        B_.wait("sync", "sV")
        B_.wait("sync", "sS")
        B_.dma("sync", b["vtm"][:], sp["vtm"][cc])
        B_.op(V, "tensor_mul", sc_bb, sc_bcol, sc_bsc)
        B_.op(V, "tensor_mul", sc_bb2, sc_bb, sc_bsc)
        for h in range(H):
            B_.op(V, "tensor_scalar_mul", b["kdd"][:, h * d:(h + 1) * d],
                  b["ktm"][:, h * d:(h + 1) * d], sc_bb2[:, h:h + 1])
        B_.wait(V, "sD")
        for h in range(H):
            B_.op(V, "tensor_scalar_mul", b["rb"][:, h * C:h * C + 64],
                  b["vtm"][:, h * d:(h + 1) * d], sc_bb[:, h:h + 1])
        B_.op(V, "tensor_copy",
              b["rb"][:].rearrange("p (h t) -> p h t", h=H)[:, :, 64:128],
              b["kdd"][:].rearrange("p (h j) -> p h j", h=H), inc=("sV", 1))
        B_.barrier("tensor", ("sV",))
        for h in range(H):
            B_.mm(pX[half(h):half(h) + 64, (h >> 1) * C:(h >> 1) * C + C],
                  b["kdd"][:, h * d:(h + 1) * d], id128,
                  inc=("sT", 1) if h == H - 1 else None,
                  tpos=(0, half(h)))
        B_.wait(V, "sT")
        B_.op(V, "tensor_copy", b["kddf"][:], pX[:, :8 * C], inc=("sV", 1))
        B_.barrier("tensor", ("sV",))
        for i_, h in enumerate([x for x in range(H) if x % 2 == 0] + [x for x in range(H) if x % 2 == 1]):
            B_.mm(pX[:, h * C:(h + 1) * C],
                  b["kddf"][half(h):half(h) + 64, (h >> 1) * C:(h >> 1) * C + C],
                  b["kfm"][half(h):half(h) + 64, (h >> 1) * C:(h >> 1) * C + C],
                  tpos=(half(h), 0),
                  inc=("sT", 1) if i_ in (7, H - 1) else None)
            if i_ == 7:
                B_.wait("tensor", "sT")
        for i_, h in enumerate([x for x in range(H) if x % 2 == 0] + [x for x in range(H) if x % 2 == 1]):
            B_.mm(pY[:, h * C:(h + 1) * C],
                  b["kfm"][half(h):half(h) + 64, (h >> 1) * C:(h >> 1) * C + C],
                  b["kddf"][half(h):half(h) + 64, (h >> 1) * C:(h >> 1) * C + C],
                  inc=("sT", 1) if i_ in (7, H - 1) else None,
                  tpos=(half(h), 0))
            if i_ == 7:
                B_.wait("tensor", "sT")
        B_.wait(V, "sT")
        B_.op(V, "tensor_mul", b["Xb"][:], pX[:, :], b["c_trilS"][:])
        B_.op(V, "tensor_mul", b["Xtb"][:], pY[:, :], b["c_triuS"][:],
              inc=("sV", 1))
        B_.barrier("tensor", ("sV",))
        for h in range(H):
            B_.mm(pX[:, h * C:(h + 1) * C], b["Xtb"][:, h * C:(h + 1) * C],
                  b["rb"][:, h * C:(h + 1) * C],
                  inc=("sT", 1) if h == H - 1 else None)
        B_.wait(V, "sT")
        B_.op(V, "tensor_sub", b["rb"][:], b["rb"][:], pX[:, :], inc=("sV", 1))
        cur, curT, nxt, nxtT = "Xb", "Xtb", "Pb", "Ptb"
        for lvl in range(6):
            B_.barrier("tensor", ("sV",))
            for h in range(H):
                B_.mm(pX[:, h * C:(h + 1) * C], b[curT][:, h * C:(h + 1) * C],
                      b[cur][:, h * C:(h + 1) * C])
            for h in range(H):
                B_.mm(pY[:, h * C:(h + 1) * C], b[cur][:, h * C:(h + 1) * C],
                      b[curT][:, h * C:(h + 1) * C],
                      inc=("sT", 1) if h == H - 1 else None)
            B_.wait(V, "sT")
            B_.op(V, "tensor_copy", b[nxt][:], pX[:, :])
            B_.op(V, "tensor_copy", b[nxtT][:], pY[:, :], inc=("sV", 1))
            B_.barrier("tensor", ("sV",))
            for h in range(H):
                B_.mm(pX[:, h * C:(h + 1) * C], b[nxtT][:, h * C:(h + 1) * C],
                      b["rb"][:, h * C:(h + 1) * C],
                      inc=("sT", 1) if h == H - 1 else None)
            B_.wait(V, "sT")
            B_.op(V, "tensor_add", b["rb"][:], b["rb"][:], pX[:, :],
                  inc=("sV", 1))
            cur, curT, nxt, nxtT = nxt, nxtT, cur, curT

        def rV(h):
            return b["rb"][:, h * C:h * C + 64]

        def rK(h):
            return b["rb"][:, h * C + 64:h * C + 128]

        B_.barrier("tensor", ("sV",))
        for i_, h in enumerate([x for x in range(H) if x % 2 == 0] + [x for x in range(H) if x % 2 == 1]):
            B_.mm(pX[:, h * C:(h + 1) * C],
                  b["kfm"][half(h):half(h) + 64, (h >> 1) * C:(h >> 1) * C + C],
                  b["qfm"][half(h):half(h) + 64, (h >> 1) * C:(h >> 1) * C + C],
                  inc=("sT", 1) if i_ in (7, H - 1) else None,
                  tpos=(half(h), 0))
            if i_ == 7:
                B_.wait("tensor", "sT")
        B_.wait(V, "sT")
        B_.op(V, "tensor_mul", b["Gtb"][:], pX[:, :], b["c_triuI"][:],
              inc=("sV", 1))
        B_.barrier("tensor", ("sV",))
        for h in range(H):
            B_.mm(pY[:, h * d:(h + 1) * d], b["Gtb"][:, h * C:(h + 1) * C],
                  rV(h))
        for h in range(H):
            B_.mm(pX[half(h):half(h) + 64, (h >> 1) * C:(h >> 1) * C + C],
                  rK(h), b["Gtb"][:, h * C:(h + 1) * C],
                  inc=("sT", 1) if h == H - 1 else None,
                  tpos=(0, half(h)))
        B_.wait(V, "sT")
        B_.op(V, "tensor_copy", b["stg"][:], pY[:, :H * d])
        B_.op(V, "tensor_sub", b["stg2"][:], b["qfm"][:], pX[:, :8 * C],
              inc=("sV", 1))
        B_.wait("sync", "sV")
        B_.dma("sync", sp["gg"][cc], b["stg"][:])
        B_.dma("sync", sp["zt"][cc], b["stg2"][:])
        B_.barrier("tensor")
        for h in range(H):
            B_.mm(pX[half(h):half(h) + 64, (h >> 1) * d:(h >> 1) * d + d],
                  rK(h), b["ktm"][:, h * d:(h + 1) * d], tpos=(0, half(h)))
        for h in range(H):
            B_.mm(pY[half(h):half(h) + 64, (h >> 1) * d:(h >> 1) * d + d],
                  rV(h), b["ktm"][:, h * d:(h + 1) * d],
                  inc=("sT", 1) if h == H - 1 else None, tpos=(0, half(h)))
        B_.wait(V, "sT")
        B_.op(V, "tensor_sub", b["stg"][:, :8 * d], b["c_idpair"][:],
              pX[:, :8 * d])
        B_.op(V, "tensor_copy", b["stg2"][:, :8 * d], pY[:, :8 * d],
              inc=("sV", 1))
        B_.wait("sync", "sV")
        B_.dma("sync", sp["mt"][cc], b["stg"][:, :8 * d])
        B_.dma("sync", sp["bt"][cc], b["stg2"][:, :8 * d])
        B_.barrier("tensor", ("sV",))
        for i_, h in enumerate([x for x in range(H) if x % 2 == 0] + [x for x in range(H) if x % 2 == 1]):
            hb, hp = half(h), (h >> 1) * d
            B_.mm(pX[hb:hb + 64, hp:hp + d], b["stg"][hb:hb + 64, hp:hp + d],
                  b["mrun"][hb:hb + 64, hp:hp + d], tpos=(hb, hb))
            B_.mm(pY[hb:hb + 64, hp:hp + d], b["stg"][hb:hb + 64, hp:hp + d],
                  b["brun"][hb:hb + 64, hp:hp + d], start=True, stop=False,
                  tpos=(hb, hb))
            B_.mm(pY[hb:hb + 64, hp:hp + d], b["stg2"][hb:hb + 64, hp:hp + d],
                  b["c_idpair"][hb:hb + 64, hp:hp + d], start=False, stop=True,
                  inc=("sT", 1) if i_ in (7, H - 1) else None, tpos=(hb, hb))
            if i_ == 7:
                B_.wait("tensor", "sT")
        B_.wait(V, "sT")
        B_.op(V, "tensor_copy", b["mrun"][:], pX[:, :8 * d])
        B_.op(V, "tensor_copy", b["brun"][:], pY[:, :8 * d], inc=("sV", 1))

    if cut(5):
        return
    B_.wait("sync", "sV")
    B_.dma("sync", sp["cc_in"][0:1, 0:65536]
           .rearrange("a (p f) -> (a p) f", p=128), b["mrun"][:])
    B_.dma("sync", sp["cc_in"][0:1, 65536:131072]
           .rearrange("a (p f) -> (a p) f", p=128), b["brun"][:])
    B_.wait(G, "sD")
    B_._emit(G, lambda e: e.collective_compute(
        "AllGather", ALU.bypass, replica_groups=[[0, 1, 2, 3], [4, 5, 6, 7]],
        ins=[sp["cc_in"][:].opt()], outs=[sp["cc_out"][:].opt()]),
        ("sC", 1))

    def wquant_stream(src_ap_quarters, nquarters, sw_col, absacc=True):
        for qq in range(nquarters):
            B_.barrier("sync")
            B_.dma("sync", b["wstage"][:].rearrange("p (c f) -> p c f", c=8),
                   src_ap_quarters(qq))
            B_.wait(V, "sD")
            B_.op(V, "reduce_sum", b["rowm"][:, qq:qq + 1].unsqueeze(1),
                  b["wstage"][:].unsqueeze(1), axis=AX.X,
                  apply_absolute_value=True, inc=("sV", 1))

    if cut(6):
        return
    load_wpair("wo", "WCh", "WCl")

    B_.wait("sync", "sC", 1)
    B_.barrier(V)
    B_.op(V, "memset", b["Sb"][:], 0.0, inc=("sV", 1))
    for j in range(3):
        B_.barrier("sync", ("sV", "sT"))
        B_.dma("sync", b["Mg"][:], sp["cc_out"][j:j + 1, 0:65536]
               .rearrange("a (p f) -> (a p) f", p=128))
        B_.dma("sync", b["Bg"][:], sp["cc_out"][j:j + 1, 65536:131072]
               .rearrange("a (p f) -> (a p) f", p=128))
        B_.wait(V, "sD")
        B_.op(V, "tensor_scalar", b["Mg"][:], b["Mg"][:], b["c_pm"][:, j:j + 1],
              None, ALU.mult, ALU.bypass)
        B_.op(V, "scalar_tensor_tensor", b["Mg"][:], b["c_idpair"][:],
              b["c_pm1"][:, j:j + 1], b["Mg"][:], ALU.mult, ALU.add)
        B_.op(V, "tensor_scalar", b["Bg"][:], b["Bg"][:], b["c_pm"][:, j:j + 1],
              None, ALU.mult, ALU.bypass, inc=("sV", 1))
        B_.barrier("tensor", ("sV",))
        for i_, h in enumerate([x for x in range(H) if x % 2 == 0] + [x for x in range(H) if x % 2 == 1]):
            hb, hp = half(h), (h >> 1) * d
            off = 0 if hb == 0 else 512
            B_.mm(pX[hb:hb + 64, off + hp:off + hp + d],
                  b["Mg"][hb:hb + 64, hp:hp + d],
                  b["c_idpair"][hb:hb + 64, 0:64],
                  inc=("sT", 1) if i_ in (7, H - 1) else None, tpos=(hb, hb))
        B_.wait(V, "sT")
        B_.op(V, "tensor_copy", b["Mg"][0:64, :], pX[0:64, 0:512])
        B_.op(V, "tensor_copy", b["Mg"][64:128, :], pX[64:128, 512:1024],
              inc=("sV", 1))
        B_.barrier("tensor", ("sV",))
        for i_, h in enumerate([x for x in range(H) if x % 2 == 0] + [x for x in range(H) if x % 2 == 1]):
            hb, hp = half(h), (h >> 1) * d
            off = 0 if hb == 0 else 512
            B_.mm(pY[hb:hb + 64, off + hp:off + hp + d],
                  b["Mg"][hb:hb + 64, hp:hp + d],
                  b["Sb"][hb:hb + 64, hp:hp + d], start=True, stop=False,
                  tpos=(hb, hb))
            B_.mm(pY[hb:hb + 64, off + hp:off + hp + d],
                  b["c_idpair"][hb:hb + 64, hp:hp + d],
                  b["Bg"][hb:hb + 64, hp:hp + d], start=False, stop=True,
                  inc=("sT", 1) if i_ in (7, H - 1) else None, tpos=(hb, hb))
        B_.wait(V, "sT")
        B_.op(V, "tensor_copy", b["Sb"][0:64, :], pY[0:64, 0:512])
        B_.op(V, "tensor_copy", b["Sb"][64:128, :], pY[64:128, 512:1024],
              inc=("sV", 1))

    if cut(7):
        return
    B_.barrier("sync", ("sV", "sT"))
    B_.dma("sync", b["c_nwrep"][:], io["nwrep"][:])
    sc_rs = b["scl"][:, 0:16]
    sc_t3 = b["scl"][:, 16:32]
    sc_w3 = b["scl"][:, 32:48]
    sc_w4 = b["scl"][:, 48:64]
    for cc in range(int(os.environ.get("KCC", NCH))):
        ztb = b["Xb"]
        ggb = b["Xtb"]
        mtb = b["Pb"]
        gfb = b["Ptb"]
        B_.barrier("sync")
        B_.dma("sync", ztb[:, :8 * C], sp["zt"][cc])
        B_.dma("sync", ggb[:, :H * d], sp["gg"][cc])
        B_.dma("sync", mtb[:, 0:8 * d], sp["mt"][cc])
        B_.dma("sync", mtb[:, 8 * d:16 * d], sp["bt"][cc])
        B_.dma("sync", gfb[:, :8 * C], sp["gate"][cc])
        B_.wait("tensor", "sD")
        B_.barrier("tensor", ("sV", "sS"))
        for i_, h in enumerate([x for x in range(H) if x % 2 == 0] + [x for x in range(H) if x % 2 == 1]):
            hb = half(h)
            B_.mm(pX[:, h * d:(h + 1) * d],
                  ztb[hb:hb + 64, (h >> 1) * C:(h >> 1) * C + C],
                  b["Sb"][hb:hb + 64, (h >> 1) * d:(h >> 1) * d + d],
                  inc=("sT", 1) if i_ in (7, H - 1) else None, tpos=(hb, 0))
            if i_ == 7:
                B_.wait("tensor", "sT")
        for i_, h in enumerate([x for x in range(H) if x % 2 == 0] + [x for x in range(H) if x % 2 == 1]):
            hb, hp = half(h), (h >> 1) * d
            off = 0 if hb == 0 else 512
            B_.mm(pY[hb:hb + 64, off + hp:off + hp + d],
                  mtb[hb:hb + 64, hp:hp + d],
                  b["Sb"][hb:hb + 64, hp:hp + d], start=True, stop=False,
                  tpos=(hb, hb))
            B_.mm(pY[hb:hb + 64, off + hp:off + hp + d],
                  mtb[hb:hb + 64, 8 * d + hp:8 * d + hp + d],
                  b["c_idpair"][hb:hb + 64, 0:64], start=False, stop=True,
                  inc=("sT", 1) if i_ in (7, H - 1) else None, tpos=(hb, hb))
        B_.wait(V, "sT")
        B_.op(V, "tensor_add", b["rb"][:, :H * d], pX[:, :H * d], ggb[:, :H * d])
        B_.op(V, "tensor_copy", b["Sb"][0:64, :], pY[0:64, 0:512])
        B_.op(V, "tensor_copy", b["Sb"][64:128, :], pY[64:128, 512:1024],
              inc=("sV", 1))
        for h in range(H):
            B_.op(V, "tensor_scalar_mul", b["rb"][:, h * d:(h + 1) * d],
                  b["rb"][:, h * d:(h + 1) * d],
                  b["asc"][:, cc * H + h:cc * H + h + 1],
                  inc=("sV", 1) if h == H - 1 else None)
        B_.wait(SC, "sV")
        B_.act(b["sqs"][:], b["rb"][:, :H * d], AF.Square, inc=("sS", 1))
        B_.wait(V, "sS")
        B_.op(V, "reduce_sum", sc_t3.unsqueeze(2),
              b["sqs"][:].rearrange("p (h j) -> p h j", h=H), axis=AX.X,
              inc=("sF", 1))
        B_.wait(V, "sF")
        B_.op(V, "tensor_scalar", sc_t3, sc_t3, 1.0 / d, 1e-5, ALU.mult,
              ALU.add)
        rsqrt(sc_t3, sc_rs, sc_w3, sc_w4)
        for h in range(H):
            B_.op(V, "scalar_tensor_tensor", b["rb"][:, h * d:(h + 1) * d],
                  b["rb"][:, h * d:(h + 1) * d], sc_rs[:, h:h + 1],
                  b["c_nwrep"][:, h * d:(h + 1) * d], ALU.mult, ALU.mult,
                  inc=("sV", 1) if h == H - 1 else None)
        B_.barrier("tensor", ("sV",))
        for fc in range(8):
            B_.tp(pX[:, fc * 128:(fc + 1) * 128],
                  gfb[:, fc * 128:(fc + 1) * 128], id128,
                  inc=("sT", 1) if fc == 7 else None)
        B_.wait(SC, "sT")
        B_.act(b["qfm"][:, :1024], pX[:, :1024], AF.Silu, inc=("sS", 1))
        B_.wait(V, "sS")
        B_.op(V, "tensor_mul", b["kfm"][:, :1024], b["rb"][:, :H * d],
              b["qfm"][:, :1024])
        B_.op(V, "tensor_copy", b["hif"][:], b["kfm"][:, :1024])
        B_.op(V, "tensor_sub", b["lof"][:], b["kfm"][:, :1024], b["hif"][:],
              inc=("sV", 1))
        B_.barrier("tensor", ("sV",))
        for fc in range(8):
            B_.tp(pXb[:, fc * 128:(fc + 1) * 128],
                  b["hif"][:, fc * 128:(fc + 1) * 128], id128b)
        for fc in range(8):
            B_.tp(pYb[:, fc * 128:(fc + 1) * 128],
                  b["lof"][:, fc * 128:(fc + 1) * 128], id128b,
                  inc=("sT", 1) if fc == 7 else None)
        B_.wait(V, "sT")
        B_.op(V, "tensor_copy",
              b["oghi"][:].rearrange("p (c f) -> p c f", c=16)[:, 0:8, (cc % 4) * 128:(cc % 4 + 1) * 128],
              pXb[:, :1024].rearrange("p (c f) -> p c f", c=8))
        B_.op(V, "tensor_copy",
              b["oglo"][:].rearrange("p (c f) -> p c f", c=16)[:, 0:8, (cc % 4) * 128:(cc % 4 + 1) * 128],
              pYb[:, :1024].rearrange("p (c f) -> p c f", c=8), inc=("sV", 1))
        if cc % 4 == 3:
            sb4 = cc // 4
            for m in range(8):
                po = pX[:, (m % 4) * 512:(m % 4) * 512 + 512]
                B_.barrier("tensor", ("sV",))
                for kk in range(8):
                    B_.mm(po, b["WCh"][:, kk * 1024 + m * 128:kk * 1024 + (m + 1) * 128],
                          b["oghi"][:, kk * 512:(kk + 1) * 512],
                          start=(kk == 0), stop=False)
                    B_.mm(po, b["WCl"][:, kk * 1024 + m * 128:kk * 1024 + (m + 1) * 128],
                          b["oghi"][:, kk * 512:(kk + 1) * 512],
                          start=False, stop=False)
                    B_.mm(po, b["WCh"][:, kk * 1024 + m * 128:kk * 1024 + (m + 1) * 128],
                          b["oglo"][:, kk * 512:(kk + 1) * 512],
                          start=False, stop=(kk == 7),
                          inc=("sT", 1) if kk == 7 else None)
                B_.wait(V, "sT")
                B_.op(V, "tensor_copy", b["yfm"][:, m * 512:(m + 1) * 512], po,
                      inc=("sV", 1))
            for tt4 in range(4):
                gt = sb4 * 4 + tt4
                B_.barrier("tensor", ("sV", "sS"))
                for mc in range(8):
                    B_.tp(pY[:, mc * 128:(mc + 1) * 128],
                          b["yfm"][:, mc * 512 + tt4 * 128:mc * 512 + (tt4 + 1) * 128],
                          id128, inc=("sT", 1) if mc == 7 else None)
                B_.wait(V, "sT")
                B_.wait(V, "sD")
                B_.op(V, "tensor_copy", b["x1t"][:], pY[:, :1024], inc=("sV", 1))
                ln_core(b["x1t"][:], out_f32=b["lnt"][:])
                B_.barrier("sync", ("sV", "sS"))
                B_.dma("sync", b["xt"][:],
                       io["x_seg"][128 + gt * 128:128 + (gt + 1) * 128, :])
                B_.wait(V, "sD")
                B_.wait(V, "sS")
                B_.op(V, "tensor_add", b["x1t"][:], b["lnt"][:], b["xt"][:],
                      inc=("sV", 1))
                B_.wait("sync", "sV")
                B_.dma("sync", sp["x1"][gt], b["x1t"][:])
                if gt == 0:
                    B_.dma("sync", sp["cb_in"][0:1, 0:D], b["x1t"][0:1, :])
                if gt == NT - 1:
                    B_.dma("sync", sp["cb_in"][0:1, D:2 * D],
                           b["x1t"][127:128, :])

    B_.wait(G, "sD")
    B_._emit(G, lambda e: e.collective_compute(
        "AllGather", ALU.bypass, replica_groups=[[0, 1, 2, 3], [4, 5, 6, 7]],
        ins=[sp["cb_in"][:].opt()], outs=[sp["cb_out"][:].opt()]),
        ("sC", 1))

    if cut(8):
        return
    def w1q(qq):
        return io["fw1"][qq * 1024:(qq + 1) * 1024, :] \
            .rearrange("(c p) f -> p c f", p=128)

    wquant_stream(w1q, 4, 0)
    B_.barrier("tensor", ("sV",))
    B_.mm(pX[0:1, 0:4], b["c_ones"][:, 0:1], b["rowm"][:, 0:4], inc=("sT", 1))
    B_.wait(V, "sT")
    B_.op(V, "reduce_sum", b["ffs"][0:1, 0:1].unsqueeze(1),
          pX[0:1, 0:4].unsqueeze(1), axis=AX.X, inc=("sF", 1))
    B_.wait(V, "sF")
    B_.op(V, "tensor_scalar_mul", b["ffs"][0:1, 1:2], b["ffs"][0:1, 0:1],
          1.0 / (DH * D))
    B_.op(V, "tensor_scalar_max", b["ffs"][0:1, 1:2], b["ffs"][0:1, 1:2],
          1e-5)
    B_.op(V, "reciprocal", b["ffs"][0:1, 2:3], b["ffs"][0:1, 1:2],
          inc=("sV", 1))
    B_.barrier("tensor", ("sV",))
    B_.mm(pX[:, 0:1], b["onesrow"][:], b["ffs"][0:1, 2:3], inc=("sT", 1))
    B_.mm(pX[:, 1:2], b["onesrow"][:], b["ffs"][0:1, 1:2], inc=("sT", 1))
    B_.wait(V, "sT")
    B_.op(V, "tensor_copy", b["ffsb"][:, 0:2], pX[:, 0:2], inc=("sV", 1))

    W1T = [("WAh", 0), ("WAh", 1), ("WAl", 0), ("WAl", 1),
           ("WBh", 0), ("WBh", 1), ("WBl", 0), ("WBl", 1)]

    for qq in range(4):
        B_.barrier("sync")
        B_.dma("sync", b["wstage"][:].rearrange("p (c f) -> p c f", c=8),
               w1q(qq))
        B_.wait(V, "sD")
        for hh2 in range(4):
            sl = slice(hh2 * 2048, (hh2 + 1) * 2048)
            B_.op(V, "tensor_scalar", b["wstage"][:, sl], b["wstage"][:, sl],
                  b["ffsb"][:, 0:1], None, ALU.mult, ALU.bypass)
            B_.op(V, "tensor_scalar", b["wstage"][:, sl], b["wstage"][:, sl],
                  -1.0, 1.0, ALU.max, ALU.min)
            B_.op(V, "tensor_copy", b["wi32"][:], b["wstage"][:, sl])
            B_.op(V, "tensor_copy", b["wq16"][:, sl], b["wi32"][:],
                  inc=("sV", 1))
        for dc in range(8):
            buf, plane = W1T[dc]
            B_.barrier("tensor", ("sV",))
            for jc in range(8):
                B_.tp(pXb[:, jc * 128:(jc + 1) * 128],
                      b["wq16"][:, jc * 1024 + dc * 128:jc * 1024 + (dc + 1) * 128],
                      id128b, inc=("sT", 1) if jc == 7 else None)
            B_.wait(V, "sT")
            B_.op(V, "tensor_copy",
                  b[buf][:, plane * 4096 + qq * 1024:plane * 4096 + (qq + 1) * 1024],
                  pXb[:, 0:1024], inc=("sV", 1))

    def w1qT_sl(dc, jc):
        buf, plane = W1T[dc]
        return b[buf][:, plane * 4096 + jc * 128:plane * 4096 + (jc + 1) * 128]


    FB = 128
    B_.barrier(V)
    B_.op(V, "memset", b["hscr"][:], 1.0)
    B_.op(V, "memset", b["zscr"][:], 1.0, inc=("sV", 1))

    if cut(8):
        return
    B_.dma("sync", b["c_fcw"][:].rearrange("p (j t) -> p j t", j=32),
           io["fcw"][:].rearrange("(j p) t -> p j t", p=128))
    B_.dma("sync", b["c_fcb"][:].unsqueeze(2),
           io["fcb"][:].rearrange("(j p) o -> p j o", p=128))
    B_.wait(SC, "sD")
    for tt in range(NT):
        B_.barrier("sync")
        B_.dma("sync", b["x1t"][:], sp["x1"][tt])
        B_.wait(V, "sD")
        B_.wait(SC, "sD")
        ln_core(b["x1t"][:], out_f32=b["lnt"][:])
        B_.wait(V, "sS")
        B_.op(V, "reduce_max", b["rowm"][:, 0:1].unsqueeze(1),
              b["lnt"][:].unsqueeze(1), axis=AX.X, apply_absolute_value=True,
              inc=("sF", 1))
        B_.wait(V, "sF")
        B_.op(V, "tensor_scalar_max", b["rowm"][:, 1:2], b["rowm"][:, 0:1],
              1e-5)
        B_.op(V, "reciprocal", b["rowm"][:, 2:3], b["rowm"][:, 1:2],
              inc=("sF", 1))
        B_.wait(V, "sF")
        B_.op(V, "tensor_scalar", b["sqs"][:], b["lnt"][:],
              b["rowm"][:, 2:3], 127.0, ALU.mult, ALU.mult)
        B_.op(V, "tensor_copy", b["xi32"][:], b["sqs"][:])
        B_.op(V, "tensor_copy", b["xit"][:], b["xi32"][:])
        B_.op(V, "tensor_scalar", b["rowm"][:, 3:4], b["rowm"][:, 1:2],
              b["ffsb"][:, 1:2], 1.0 / 127.0, ALU.mult, ALU.mult,
              inc=("sV", 1))
        B_.barrier("tensor", ("sV",))
        for fc in range(8):
            B_.tp(pXb[:, fc * 128:(fc + 1) * 128],
                  b["xit"][:, fc * 128:(fc + 1) * 128], id128b)
        B_.mm(pY[0:1, 0:128], b["rowm"][:, 3:4], id128, inc=("sT", 1))
        B_.wait(V, "sT")
        B_.op(V, "tensor_copy", b["hqt"][:, 0:1024], pXb[:, 0:1024])
        B_.op(V, "tensor_copy", b["hscr"][0:1, 1 + tt * 128:1 + (tt + 1) * 128],
              pY[0:1, 0:128], inc=("sV", 1))
        B_.wait("sync", "sV")
        B_.dma("sync", sp["xi"][tt], b["hqt"][:, 0:1024])

    B_.wait("sync", "sC", 2)
    B_.barrier("sync", ("sV", "sS", "sT"))
    B_.dma("sync", b["cball"][:], sp["cb_out"][:])
    B_.wait("tensor", "sD")
    B_.mm(pX[0:1, 0:512], b["c_sel"][:, 0:1], b["cball"][:, D:D + 512])
    B_.mm(pX[0:1, 512:1024], b["c_sel"][:, 0:1], b["cball"][:, D + 512:2 * D])
    B_.mm(pX[0:1, 1024:1536], b["c_sel"][:, 1:2], b["cball"][:, 0:512])
    B_.mm(pX[0:1, 1536:2048], b["c_sel"][:, 1:2], b["cball"][:, 512:D],
          inc=("sT", 1))
    B_.wait(V, "sT")
    B_.op(V, "tensor_copy", b["brow"][:], pX[0:1, 0:2048], inc=("sV", 1))
    for s in range(2):
        xrow = b["brow"][0:1, s * D:(s + 1) * D]
        ln_core(xrow, out_f32=b["lnt"][0:1, :])
        B_.wait(V, "sS")
        B_.op(V, "reduce_max", b["rowm"][0:1, 4:5].unsqueeze(1),
              b["lnt"][0:1, :].unsqueeze(1), axis=AX.X,
              apply_absolute_value=True, inc=("sF", 1))
        B_.wait(V, "sF")
        B_.op(V, "tensor_scalar_max", b["rowm"][0:1, 4:5], b["rowm"][0:1, 4:5],
              1e-5)
        B_.op(V, "reciprocal", b["rowm"][0:1, 5:6], b["rowm"][0:1, 4:5],
              inc=("sF", 1))
        B_.wait(V, "sF")
        B_.op(V, "tensor_scalar", b["sqs"][0:1, :], b["lnt"][0:1, :],
              b["rowm"][0:1, 5:6], 127.0, ALU.mult, ALU.mult)
        B_.op(V, "tensor_copy", b["xi32"][0:1, :], b["sqs"][0:1, :])
        B_.op(V, "tensor_copy", b["xit"][0:1, :], b["xi32"][0:1, :])
        B_.op(V, "tensor_scalar", b["rowm"][0:1, 6:7], b["rowm"][0:1, 4:5],
              b["ffs"][0:1, 1:2], 1.0 / 127.0, ALU.mult, ALU.mult,
              inc=("sV", 1))
        B_.barrier("tensor", ("sV",))
        for fc in range(8):
            B_.mm(pY[:, fc:fc + 1], b["xit"][0:1, fc * 128:(fc + 1) * 128],
                  id128b[0:1, 0:1],
                  inc=("sT", 1) if fc == 7 else None)
        B_.mm(pY[:, 8:9], b["onesrow"][:], b["rowm"][0:1, 6:7], inc=("sT", 1))
        B_.wait(V, "sT")
        B_.op(V, "tensor_copy", b["xif_h"][:], pY[:, 0:8])
        B_.op(V, "tensor_copy", b["rowm"][:, 6:7], pY[:, 8:9], inc=("sV", 1))
        B_.barrier("tensor", ("sV",))
        for jc in range(32):
            for dc in range(8):
                B_.mm(pX[:, jc:jc + 1], w1qT_sl(dc, jc),
                      b["xif_h"][:, dc:dc + 1], start=(dc == 0),
                      stop=(dc == 7),
                      inc=("sT", 1) if (jc == 31 and dc == 7) else None)
        B_.wait(V, "sT")
        B_.op(V, "tensor_scalar", b["hhalo"][:, s * 32:(s + 1) * 32],
              pX[:, 0:32], b["rowm"][:, 6:7], b["c_hm"][:, s:s + 1],
              ALU.mult, ALU.mult, inc=("sV", 1))

    if cut(9):
        return
    for fb in range(int(os.environ.get("KFB", NT))):
        B_.barrier("tensor", ("sV", "sS"))
        B_.mm(pX[:, 0:FB + 2], b["onesrow"][:],
              b["hscr"][0:1, fb * FB:fb * FB + FB + 2], inc=("sT", 1))
        B_.wait(V, "sT")
        B_.op(V, "tensor_copy", b["hscp"][:], pX[:, 0:FB + 2], inc=("sV", 1))
        B_.barrier("sync", ("sV",))
        if fb == 0:
            B_.dma("sync", b["xib"][:, 0:1024], sp["xi"][0])
        if fb + 1 < NT:
            B_.dma("sync", b["xib"][:, ((fb + 1) % 3) * 1024:
                                     ((fb + 1) % 3) * 1024 + 1024],
                   sp["xi"][fb + 1])
        B_.wait(V, "sD")
        xiwr = b["xiw"][:].rearrange("p (c f) -> p c f", c=8)
        cur = b["xib"][:, (fb % 3) * 1024:(fb % 3) * 1024 + 1024] \
            .rearrange("p (c f) -> p c f", c=8)
        B_.op(V, "tensor_copy", xiwr[:, :, 1:FB + 1], cur)
        if fb == 0:
            B_.op(V, "memset", xiwr[:, :, 0:1], 0.0)
        else:
            prev = b["xib"][:, ((fb - 1) % 3) * 1024:
                            ((fb - 1) % 3) * 1024 + 1024] \
                .rearrange("p (c f) -> p c f", c=8)
            B_.op(V, "tensor_copy", xiwr[:, :, 0:1], prev[:, :, 127:128])
        if fb == NT - 1:
            B_.op(V, "memset", xiwr[:, :, FB + 1:FB + 2], 0.0, inc=("sV", 1))
        else:
            nxt = b["xib"][:, ((fb + 1) % 3) * 1024:
                           ((fb + 1) % 3) * 1024 + 1024] \
                .rearrange("p (c f) -> p c f", c=8)
            B_.op(V, "tensor_copy", xiwr[:, :, FB + 1:FB + 2],
                  nxt[:, :, 0:1], inc=("sV", 1))
        B_.barrier("tensor", ("sV",))
        for grp in range(4):
            B_.barrier("tensor", ("sV",))
            for j2 in range(8):
                jc = grp * 8 + j2
                po = (pX if j2 < 4 else pY)[:, (j2 % 4) * 512:(j2 % 4) * 512 + FB + 2]
                for dc in range(8):
                    B_.mm(po, w1qT_sl(dc, jc),
                          b["xiw"][:, dc * (FB + 2):(dc + 1) * (FB + 2)],
                          start=(dc == 0), stop=(dc == 7),
                          inc=("sT", 1) if (j2 == 7 and dc == 7) else None)
            B_.wait(V, "sT")
            for j2 in range(8):
                jc = grp * 8 + j2
                po = (pX if j2 < 4 else pY)[:, (j2 % 4) * 512:(j2 % 4) * 512 + FB + 2]
                B_.op(V, "tensor_mul",
                      b["hraw"][:, jc * (FB + 2):(jc + 1) * (FB + 2)],
                      po, b["hscp"][:],
                      inc=("sV", 1) if j2 == 7 else None)
        if fb == 0:
            B_.op(V, "tensor_copy",
                  b["hraw"][:].rearrange("p (j f) -> p j f", j=32)[:, :, 0:1],
                  b["hhalo"][:, 0:32].unsqueeze(2), inc=("sV", 1))
        if fb == NT - 1:
            B_.op(V, "tensor_copy",
                  b["hraw"][:].rearrange("p (j f) -> p j f", j=32)
                  [:, :, FB + 1:FB + 2],
                  b["hhalo"][:, 32:64].unsqueeze(2), inc=("sV", 1))
        B_.barrier(SC, ("sV",))
        for jc in range(32):
            B_.act(b["hcs"][:, jc * FB:(jc + 1) * FB],
                   b["hraw"][:, jc * (FB + 2) + 1:jc * (FB + 2) + 1 + FB],
                   AF.Identity, scale=b["c_fcw"][:, jc * 3 + 1:jc * 3 + 2],
                   bias=b["c_fcb"][:, jc:jc + 1],
                   inc=("sS", 1) if jc == 31 else None)
        B_.wait(V, "sS")
        for j in (0, 2):
            for jc in range(32):
                B_.op(V, "scalar_tensor_tensor",
                      b["hcs"][:, jc * FB:(jc + 1) * FB],
                      b["hraw"][:, jc * (FB + 2) + j:jc * (FB + 2) + j + FB],
                      b["c_fcw"][:, jc * 3 + j:jc * 3 + j + 1],
                      b["hcs"][:, jc * FB:(jc + 1) * FB],
                      ALU.mult, ALU.add,
                      inc=("sV", 1) if (j == 2 and jc == 31) else None)
        B_.wait(SC, "sV")
        B_.act(b["hcs"][:], b["hcs"][:], AF.Silu, inc=("sS", 1))
        for rnd in range(2):
            B_.barrier("tensor", ("sV", "sS"))
            for jj in range(16):
                jc = rnd * 16 + jj
                po = pX if jj < 8 else pY
                B_.tp(po[:, (jj % 8) * 128:(jj % 8 + 1) * 128],
                      b["hcs"][:, jc * FB:(jc + 1) * FB], id128,
                      inc=("sT", 1) if jj == 15 else None)
            B_.wait(V, "sT")
            B_.op(V, "tensor_copy",
                  b["htt"][:, rnd * 2048:rnd * 2048 + 1024], pX[:, 0:1024])
            B_.op(V, "tensor_copy",
                  b["htt"][:, rnd * 2048 + 1024:rnd * 2048 + 2048],
                  pY[:, 0:1024], inc=("sV", 1))
        B_.op(V, "reduce_max", b["rowm"][:, 4:5].unsqueeze(1),
              b["htt"][:].unsqueeze(1), axis=AX.X, apply_absolute_value=True,
              inc=("sF", 1))
        B_.wait(V, "sF")
        B_.op(V, "tensor_scalar_max", b["rowm"][:, 5:6], b["rowm"][:, 4:5],
              1e-5)
        B_.op(V, "reciprocal", b["rowm"][:, 6:7], b["rowm"][:, 5:6],
              inc=("sF", 1))
        B_.wait(V, "sF")
        B_.op(V, "tensor_scalar", b["htt"][:], b["htt"][:],
              b["rowm"][:, 6:7], 127.0, ALU.mult, ALU.mult)
        for g in range(2):
            B_.op(V, "tensor_copy", b["hi32"][:],
                  b["htt"][:, g * 2048:(g + 1) * 2048])
            B_.op(V, "tensor_copy", b["hqt"][:, g * 2048:(g + 1) * 2048],
                  b["hi32"][:])
        B_.op(V, "tensor_scalar_mul", b["rowm"][:, 7:8], b["rowm"][:, 5:6],
              1.0 / 127.0, inc=("sV", 1))
        B_.barrier("tensor", ("sV",))
        B_.mm(pX[0:1, 1536:1664], b["rowm"][:, 7:8], id128, inc=("sT", 1))
        B_.wait(V, "sT")
        B_.op(V, "tensor_copy",
              b["zscr"][0:1, fb * FB:(fb + 1) * FB], pX[0:1, 1536:1664],
              inc=("sV", 1))
        for rnd in range(2):
            B_.barrier("tensor", ("sV",))
            for jj in range(16):
                jc = rnd * 16 + jj
                pob = pXb if jj < 8 else pYb
                B_.tp(pob[:, (jj % 8) * 128:(jj % 8 + 1) * 128],
                      b["hqt"][:, jc * 128:(jc + 1) * 128], id128b,
                      inc=("sT", 1) if jj == 15 else None)
            B_.wait(V, "sT")
            for g in range(2):
                pob = pXb if g == 0 else pYb
                B_.op(V, "tensor_copy",
                      b["hqf"][:].rearrange("p (j f) -> p j f", j=32)
                      [:, rnd * 16 + g * 8:rnd * 16 + (g + 1) * 8, :],
                      pob[:, 0:1024].rearrange("p (j f) -> p j f", j=8),
                      inc=("sV", 1) if g == 1 else None)
        B_.wait("sync", "sV")
        B_.dma("sync", sp["hqf"][fb], b["hqf"][:])

    def w2chunk(mc):
        return io["fw2"][mc * 128:(mc + 1) * 128, :]

    for mc in range(8):
        B_.barrier("sync")
        B_.dma("sync", b["wstage"][:, :DH], w2chunk(mc))
        B_.wait(V, "sD")
        B_.op(V, "reduce_sum", b["rowm"][:, mc:mc + 1].unsqueeze(1),
              b["wstage"][:, :DH].unsqueeze(1), axis=AX.X,
              apply_absolute_value=True, inc=("sV", 1))
    B_.barrier("tensor", ("sV",))
    B_.mm(pX[0:1, 0:8], b["c_ones"][:, 0:1], b["rowm"][:, 0:8], inc=("sT", 1))
    B_.wait(V, "sT")
    B_.op(V, "reduce_sum", b["ffs"][0:1, 3:4].unsqueeze(1),
          pX[0:1, 0:8].unsqueeze(1), axis=AX.X, inc=("sF", 1))
    B_.wait(V, "sF")
    B_.op(V, "tensor_scalar_mul", b["ffs"][0:1, 4:5], b["ffs"][0:1, 3:4],
          1.0 / (DH * D))
    B_.op(V, "tensor_scalar_max", b["ffs"][0:1, 4:5], b["ffs"][0:1, 4:5], 1e-5)
    B_.op(V, "reciprocal", b["ffs"][0:1, 5:6], b["ffs"][0:1, 4:5],
          inc=("sV", 1))
    B_.barrier("tensor", ("sV",))
    B_.mm(pX[:, 0:1], b["onesrow"][:], b["ffs"][0:1, 5:6])
    B_.mm(pX[:, 1:2], b["onesrow"][:], b["ffs"][0:1, 4:5], inc=("sT", 1))
    B_.wait(V, "sT")
    B_.op(V, "tensor_copy", b["ffsb"][:, 2:4], pX[:, 0:2], inc=("sV", 1))

    W2T = [("WCh", 0), ("WCh", 1), ("WCl", 0), ("WCl", 1),
           ("WDh", 0), ("WDh", 1), ("WDl", 0), ("WDl", 1)]

    def w2qT_sl(jc, mc):
        bufn = ("WCh", "WCl", "WDh", "WDl")[jc // 8]
        off = (jc % 8) * 1024 + mc * 128
        return b[bufn][:, off:off + 128]

    for mc in range(8):
        B_.barrier("sync")
        B_.dma("sync", b["wstage"][:, :DH], w2chunk(mc))
        B_.wait(V, "sD")
        for hh2 in range(2):
            sl = slice(hh2 * 2048, (hh2 + 1) * 2048)
            B_.op(V, "tensor_scalar", b["wstage"][:, sl], b["wstage"][:, sl],
                  b["ffsb"][:, 2:3], None, ALU.mult, ALU.bypass)
            B_.op(V, "tensor_scalar", b["wstage"][:, sl], b["wstage"][:, sl],
                  -1.0, 1.0, ALU.max, ALU.min)
            B_.op(V, "tensor_copy", b["wi32"][:], b["wstage"][:, sl])
            B_.op(V, "tensor_copy", b["hqt"][:, sl], b["wi32"][:],
                  inc=("sV", 1))
        for rnd in range(2):
            B_.barrier("tensor", ("sV",))
            for jj in range(16):
                jc = rnd * 16 + jj
                pob = pXb if jj < 8 else pYb
                B_.tp(pob[:, (jj % 8) * 128:(jj % 8 + 1) * 128],
                      b["hqt"][:, jc * 128:(jc + 1) * 128], id128b,
                      inc=("sT", 1) if jj == 15 else None)
            B_.wait(V, "sT")
            for g4 in range(2):
                pob = pXb if g4 == 0 else pYb
                base_jc = rnd * 16 + g4 * 8
                bufn = ("WCh", "WCl", "WDh", "WDl")[base_jc // 8]
                B_.op(V, "tensor_copy",
                      b[bufn][:].rearrange("p (j f) -> p j f", j=8)
                      [:, :, mc * 128:(mc + 1) * 128],
                      pob[:, 0:1024].rearrange("p (j f) -> p j f", j=8),
                      inc=("sV", 1) if g4 == 1 else None)

    if cut(10):
        return
    for fb in range(NT):
        B_.barrier("sync")
        B_.dma("sync", b["hqf"][:], sp["hqf"][fb])
        B_.barrier("tensor", ("sV", "sS"))
        B_.mm(pX[:, 0:FB], b["onesrow"][:],
              b["zscr"][0:1, fb * FB:(fb + 1) * FB], inc=("sT", 1))
        B_.wait(V, "sT")
        B_.op(V, "tensor_scalar", b["zscp"][:], pX[:, 0:FB],
              b["ffsb"][:, 3:4], None, ALU.mult, ALU.bypass, inc=("sV", 1))
        if _PS < 2:
            continue
        B_.barrier("tensor", ("sV", "sD"))
        for mc in range(8):
            po = (pX if mc < 4 else pY)[:, (mc % 4) * 512:(mc % 4) * 512 + FB]
            for jc in range(32):
                B_.mm(po, w2qT_sl(jc, mc), b["hqf"][:, jc * FB:(jc + 1) * FB],
                      start=(jc == 0), stop=(jc == 31),
                      inc=("sT", 1) if (mc == 7 and jc == 31) else None)
        B_.wait(V, "sT")
        for mc in range(8):
            po = (pX if mc < 4 else pY)[:, (mc % 4) * 512:(mc % 4) * 512 + FB]
            B_.op(V, "tensor_mul", b["zf"][:, mc * FB:(mc + 1) * FB], po,
                  b["zscp"][:], inc=("sV", 1) if mc == 7 else None)
        if _PS < 3:
            continue
        B_.barrier("tensor", ("sV", "sS"))
        for mc in range(8):
            B_.tp(pX[:, mc * 128:(mc + 1) * 128],
                  b["zf"][:, mc * FB:(mc + 1) * FB], id128,
                  inc=("sT", 1) if mc == 7 else None)
        B_.wait(V, "sT")
        B_.op(V, "tensor_copy", b["xt"][:], pX[:, 0:1024], inc=("sV", 1))
        ln_core(b["xt"][:], out_f32=b["lnt"][:])
        B_.barrier("sync", ("sV", "sS"))
        B_.dma("sync", b["x1t"][:], sp["x1"][fb])
        B_.wait(V, "sD")
        B_.wait(V, "sS")
        B_.op(V, "tensor_add", b["praw"][:, 0:1024], b["lnt"][:],
              b["x1t"][:], inc=("sV", 1))
        B_.wait("sync", "sV")
        B_.dma("sync", out[fb * 128:(fb + 1) * 128, :], b["praw"][:, 0:1024])

    _dbg_tap()

    B_.wait("sync", "sD")
    B_.wait("sync", "sG")


_NC_CACHE = {}


def _get_nc():
    if "nc" not in _NC_CACHE:
        _NC_CACHE["nc"] = build_nc()
    return _NC_CACHE["nc"]


def kernel(**inputs):
    x = np.asarray(inputs["L0_attn_x"], dtype=np.float32)
    f32 = np.float32

    def c_(a):
        return np.ascontiguousarray(np.asarray(a, dtype=f32))

    base = {
        "wq": c_(inputs["L0_attn_Wq"]), "wk": c_(inputs["L0_attn_Wk"]),
        "wv": c_(inputs["L0_attn_Wv"]), "wg": c_(inputs["L0_attn_Wg"]),
        "wo": c_(inputs["L0_attn_Wo"]), "wb": c_(inputs["L0_attn_Wb"]),
        "cqw": c_(inputs["L0_attn_cq_w"]),
        "cqb": c_(inputs["L0_attn_cq_b"]).reshape(D, 1),
        "ckw": c_(inputs["L0_attn_ck_w"]),
        "ckb": c_(inputs["L0_attn_ck_b"]).reshape(D, 1),
        "cvw": c_(inputs["L0_attn_cv_w"]),
        "cvb": c_(inputs["L0_attn_cv_b"]).reshape(D, 1),
        "nwrep": np.ascontiguousarray(
            np.tile(np.asarray(inputs["L0_attn_nw"], f32).reshape(1, d),
                    (128, H))),
        "fw1": c_(inputs["L0_ffn_w1"]),
        "fw2": c_(inputs["L0_ffn_w2"]),
        "fcw": c_(inputs["L0_ffn_conv_w"]).reshape(DH, 3),
        "fcb": c_(inputs["L0_ffn_conv_b"]).reshape(DH, 1),
        "ident128": np.eye(128, dtype=f32),
        "identpair": np.ascontiguousarray(
            np.tile(np.concatenate([np.eye(d, dtype=f32)] * 2, axis=0),
                    (1, 8))),
        "m_trilS": np.ascontiguousarray(
            np.tile(np.tril(np.ones((128, 128), f32), -1), (1, H))),
        "m_triuS": np.ascontiguousarray(
            np.tile(np.triu(np.ones((128, 128), f32), 1), (1, H))),
        "m_triuI": np.ascontiguousarray(
            np.tile(np.triu(np.ones((128, 128), f32), 0), (1, H))),
        "onescol": np.ones((128, 2), f32),
    }
    in_maps = []
    for core in range(8):
        bidx, pos = core // 4, core % 4
        seg0 = pos * R
        halo3 = (x[bidx, seg0 - 3:seg0] if pos > 0
                 else np.zeros((3, D), f32))
        x_seg = np.concatenate(
            [np.zeros((125, D), f32), halo3, x[bidx, seg0:seg0 + R]], axis=0)
        pm = np.zeros((128, 4), f32)
        pm[:, :3] = [1.0 if j < pos else 0.0 for j in range(3)]
        pm1 = np.zeros((128, 4), f32)
        pm1[:, :3] = 1.0 - pm[:, :3]
        hm = np.zeros((128, 2), f32)
        hm[:, 0] = 1.0 if pos > 0 else 0.0
        hm[:, 1] = 1.0 if pos < 3 else 0.0
        cs = np.zeros((4, 2), f32)
        if pos > 0:
            cs[pos - 1, 0] = 1.0
        if pos < 3:
            cs[pos + 1, 1] = 1.0
        m = dict(base)
        m["x_seg"] = np.ascontiguousarray(x_seg)
        m["pmask"] = pm
        m["pmask1"] = pm1
        m["hmask"] = hm
        m["csel"] = cs
        in_maps.append(m)

    nc = _get_nc()
    res = run_bass_kernel_spmd(nc, in_maps, core_ids=list(range(8)))
    outp = np.empty((B, T, D), dtype=np.float32)
    for core in range(8):
        bidx, pos = core // 4, core % 4
        outp[bidx, pos * R:(pos + 1) * R, :] = res.results[core]["out"]
    return outp


# version 07 (final)
# speedup vs baseline: 1.0027x; 1.0027x over previous
import os
import numpy as np
import ml_dtypes
import concourse.bass as bass
import concourse.mybir as mybir
from concourse.bass_utils import run_bass_kernel_spmd
from contextlib import ExitStack

F32 = mybir.dt.float32
BF16 = mybir.dt.bfloat16
I32 = mybir.dt.int32
AF = mybir.ActivationFunctionType
ALU = mybir.AluOpType
AX = mybir.AxisListType

B, T, D = 2, 8192, 1024
H, d = 16, 64
R = 2048
C = 128
NCH = R // C
NT = R // 128
BLK = 256
NBLK = R // BLK
DH = 4096
XROWS = 128 + R
WIN = 259


class Bld:
    def __init__(self, nc):
        self.nc = nc
        self.ops = {k: [] for k in ("sync", "gpsimd", "vector", "scalar", "tensor")}
        self.cnt = {k: 0 for k in ("sD", "sG", "sT", "sV", "sS", "sC", "sF")}
        self.sems = {}

    def _emit(self, eng, fn, inc=None):
        if inc is not None:
            semname, n = inc
            s = self.sems[semname]

            def g(e, fn=fn, s=s, n=n):
                fn(e).then_inc(s, n)
            self.ops[eng].append(g)
            self.cnt[semname] += n
        else:
            self.ops[eng].append(fn)

    def wait(self, eng, semname, val=None):
        s = self.sems[semname]
        v = self.cnt[semname] if val is None else val
        if v <= 0:
            return
        self.ops[eng].append(lambda e, s=s, v=v: e.wait_ge(s, v))

    def barrier(self, eng, sems=("sD", "sG", "sT", "sV", "sS")):
        for s in sems:
            self.wait(eng, s)

    def dma(self, q, out, in_):
        sn = "sD" if q == "sync" else "sG"
        self._emit(q, lambda e, o=out, i=in_: e.dma_start(out=o, in_=i, allow_slow_non_contiguous=True), (sn, 16))

    def mm(self, out, lhsT, rhs, start=True, stop=True, inc=None, tpos=None):
        self._emit("tensor",
                   lambda e, o=out, l=lhsT, r=rhs, s1=start, s2=stop, t=tpos:
                   e.matmul(o, l, r, start=s1, stop=s2, tile_position=t), inc)

    def tp(self, out, in_, ident, inc=None, tpos=None):
        self._emit("tensor",
                   lambda e, o=out, i=in_, dd=ident, t=tpos:
                   e.transpose(o, i, dd, tile_position=t), inc)

    def op(self, eng, meth, *args, inc=None, **kw):
        def f(e, m=meth, a=args, k=kw):
            return getattr(e, m)(*a, **k)
        fence = False
        if eng in ("vector", "gpsimd") and args:
            try:
                fence = args[0].free_size() < 64
            except Exception:
                fence = False
        if fence and inc is None:
            inc = ("sF", 1)
        self._emit(eng, f, inc)
        if fence:
            self.wait(eng, inc[0])

    def act(self, out, in_, func, scale=1.0, bias=0.0, accum_out=None, inc=None):
        kw = {}
        if accum_out is not None:
            kw["accum_out"] = accum_out
        self._emit("scalar",
                   lambda e, o=out, i=in_, f=func, s=scale, bb=bias, k=kw:
                   e.activation(out=o, in_=i, func=f, scale=s, bias=bb, **k), inc)


def bcl(ap, n):
    return ap.unsqueeze(ap.ndim).broadcast_to(list(ap.shape) + [n])


def build_nc():
    nc = bass.Bass("TRN2", target_bir_lowering=False, debug=False)

    def din(name, shape, dt=F32):
        return nc.dram_tensor(name, shape, dt, kind="ExternalInput").ap()

    io = {}
    io["x_seg"] = din("x_seg", [XROWS, D])
    for n in ("wq", "wk", "wv", "wg", "wo"):
        io[n] = din(n, [D, D])
    io["wb"] = din("wb", [D, H])
    for n in ("cqw", "ckw", "cvw"):
        io[n] = din(n, [D, 4])
    for n in ("cqb", "ckb", "cvb"):
        io[n] = din(n, [D, 1])
    io["nwrep"] = din("nwrep", [128, H * d])
    io["fw1"] = din("fw1", [DH, D])
    io["fw2"] = din("fw2", [D, DH])
    io["fcw"] = din("fcw", [DH, 3])
    io["fcb"] = din("fcb", [DH, 1])
    io["ident128"] = din("ident128", [128, 128])
    io["identpair"] = din("identpair", [128, 8 * d])
    io["m_trilS"] = din("m_trilS", [128, H * C])
    io["m_triuS"] = din("m_triuS", [128, H * C])
    io["m_triuI"] = din("m_triuI", [128, H * C])
    io["pmask"] = din("pmask", [128, 4])
    io["pmask1"] = din("pmask1", [128, 4])
    io["hmask"] = din("hmask", [128, 2])
    io["onescol"] = din("onescol", [128, 2])
    io["csel"] = din("csel", [4, 2])

    out = nc.dram_tensor("out", [R, D], F32, kind="ExternalOutput").ap()

    def scr(name, shape, dt=F32):
        return nc.dram_tensor(name, shape, dt).ap()

    sp = {}
    sp["lnh"] = scr("sp_lnh", [8, 128, XROWS], BF16)
    sp["lnl"] = scr("sp_lnl", [8, 128, XROWS], BF16)
    sp["q"] = scr("sp_q", [NCH, 128, 8 * C])
    sp["k"] = scr("sp_k", [NCH, 128, 8 * C])
    sp["vtm"] = scr("sp_vtm", [NCH, 128, H * d])
    sp["gate"] = scr("sp_gate", [NCH, 128, 8 * C])
    sp["zt"] = scr("sp_zt", [NCH, 128, 8 * C])
    sp["gg"] = scr("sp_gg", [NCH, 128, H * d])
    sp["mt"] = scr("sp_mt", [NCH, 128, 8 * d])
    sp["bt"] = scr("sp_bt", [NCH, 128, 8 * d])
    sp["x1"] = scr("sp_x1", [NT, 128, D])
    sp["xi"] = scr("sp_xi", [NT, 128, 1024], BF16)
    sp["hqf"] = scr("sp_hqf", [NT, 128, 32 * 128], BF16)
    sp["cc_in"] = scr("cc_in", [1, 2 * 128 * 8 * d])
    sp["cc_out"] = scr("cc_out", [4, 2 * 128 * 8 * d])
    sp["cb_in"] = scr("cb_in", [1, 2 * D])
    sp["cb_out"] = scr("cb_out", [4, 2 * D])

    bld = Bld(nc)

    with ExitStack() as ctx:
        def sb(name, shape, dt=F32):
            return ctx.enter_context(nc.sbuf_tensor(name, shape, dt))

        def psum(name, shape):
            return ctx.enter_context(nc.psum_tensor(name, shape, F32))

        for s in ("sD", "sG", "sT", "sV", "sS", "sC", "sF"):
            bld.sems[s] = ctx.enter_context(nc.semaphore(s))

        b = {}
        PF = sb("PF", [128, 28224])
        PB = sb("PB", [128, 43040], BF16)
        PI = sb("PI", [128, 2048], I32)

        def fs(a, n, p=128, p0=0):
            return PF[p0:p0 + p, a:a + n]

        def bs(a, n, p=128, p0=0):
            return PB[p0:p0 + p, a:a + n]

        b["lnt"] = fs(0, 1024); b["sqs"] = fs(1024, 1024)
        b["x1t"] = fs(2048, 1024); b["xt"] = fs(3072, 1024)
        b["lnst"] = fs(4096, 16); b["rowm"] = fs(4112, 8)
        b["scl"] = fs(4120, 128); b["asc"] = fs(4248, 256)
        b["ffsb"] = fs(4504, 4)
        b["onesrow"] = fs(8608, 128, p=1)
        b["ffs"] = fs(8736, 8, p=4)
        b["praw"] = fs(8744, 8 * WIN)
        b["pfm"] = fs(10816, 2048)
        b["beta"] = fs(4510, 2048, p=16)
        b["wstage"] = fs(8744, 8192)
        b["qfm"] = fs(8744, 1024); b["kfm"] = fs(9768, 1024)
        b["ktm"] = fs(10792, 1024); b["qtm"] = fs(11816, 1024)
        b["vtm"] = b["qtm"]
        b["kdd"] = fs(14912, 1024); b["kddf"] = fs(15936, 1024)
        b["vstage"] = fs(14912, 1024)
        b["Xb"] = fs(16960, 2048); b["Xtb"] = fs(19008, 2048)
        b["Pb"] = fs(21056, 2048); b["Ptb"] = fs(23104, 2048)
        b["rb"] = fs(25152, 2048)
        b["mrun"] = fs(27200, 512); b["brun"] = fs(27712, 512)
        b["Gtb"] = b["Pb"]
        b["stg"] = b["kdd"]; b["stg2"] = b["kddf"]
        b["Sb"] = fs(27200, 512); b["Mg"] = fs(27712, 512)
        b["c_nwrep"] = fs(14912, 1024)
        b["Bg"] = fs(11816, 512)
        b["yfm"] = fs(10792, 4096)
        b["hraw"] = fs(8744, 32 * 130)
        b["htt"] = fs(8744, 4096)
        b["hcs"] = fs(12904, 4096)
        b["hscp"] = fs(17000, 130); b["zscp"] = fs(17130, 128)
        b["hscr"] = fs(17258, 2050, p=1)
        b["zscr"] = fs(19308, 2048, p=1)
        b["cball"] = fs(21356, 2048, p=4)
        b["brow"] = fs(23404, 2048, p=1)
        b["zf"] = fs(25452, 1024)
        b["xib"] = PF[:].bitcast(BF16)[:, 52952:56024]
        b["WAh"] = bs(0, 8192); b["WAl"] = bs(8192, 8192)
        b["WBh"] = bs(16384, 8192); b["WBl"] = bs(24576, 8192)
        b["WCh"] = b["WAh"]; b["WCl"] = b["WAl"]
        b["WDh"] = b["WBh"]; b["WDl"] = b["WBl"]
        b["lnph"] = bs(32768, 8 * WIN); b["lnpl"] = bs(34840, 8 * WIN)
        b["hif"] = bs(36912, 1024); b["lof"] = bs(37936, 1024)
        b["c_trilS"] = bs(0, 2048); b["c_triuS"] = bs(2048, 2048)
        b["c_triuI"] = bs(4096, 2048)
        b["oghi"] = bs(16384, 8192); b["oglo"] = bs(24576, 8192)
        b["xiw"] = bs(32768, 8 * 130); b["xit"] = bs(33808, 1024)
        b["wq16"] = bs(32768, 8192)
        b["hqt"] = bs(34832, 4096); b["hqf"] = bs(38928, 4096)
        b["wi32"] = PI[:, 0:2048]
        b["xi32"] = PI[:, 0:1024]
        b["hi32"] = PI[:, 0:2048]
        b["c_id128"] = sb("c_id128", [128, 128])
        b["c_idpair"] = sb("c_idpair", [128, 8 * d])
        b["c_convw"] = sb("c_convw", [128, 8 * 12])
        b["c_convb"] = sb("c_convb", [128, 8 * 3])
        b["c_wbh"] = sb("c_wbh", [128, 8 * H], BF16)
        b["c_wbl"] = sb("c_wbl", [128, 8 * H], BF16)
        b["c_pm"] = sb("c_pm", [128, 4])
        b["c_pm1"] = sb("c_pm1", [128, 4])
        b["c_hm"] = sb("c_hm", [128, 2])
        b["c_ones"] = sb("c_ones", [128, 2])
        b["c_sel"] = sb("c_sel", [4, 2])
        b["hhalo"] = sb("hhalo", [128, 2 * 32])
        b["c_fcw"] = sb("c_fcw", [128, 32 * 3])
        b["c_id128b"] = sb("c_id128b", [128, 128], BF16)
        b["c_fcb"] = sb("c_fcb", [128, 32])
        b["xif_h"] = sb("xif_h", [128, 8], BF16)
        b["pX"] = psum("pX", [128, H * C])
        b["pY"] = psum("pY", [128, H * C])

        emit(nc, bld, b, io, sp, out)

        with nc.Block() as block:
            @block.sync
            def _(e):
                for f in bld.ops["sync"]:
                    f(e)

            @block.gpsimd
            def _(e):
                for f in bld.ops["gpsimd"]:
                    f(e)

            @block.vector
            def _(e):
                for f in bld.ops["vector"]:
                    f(e)

            @block.scalar
            def _(e):
                for f in bld.ops["scalar"]:
                    f(e)

            @block.tensor
            def _(e):
                for f in bld.ops["tensor"]:
                    f(e)

    return nc


_PH = int(os.environ.get("KPH", "99"))
_PB = int(os.environ.get("KPB", "99"))
_PS = int(os.environ.get("KPS", "99"))


def emit(nc, B_, b, io, sp, out):
    V, SC, G = "vector", "scalar", "gpsimd"

    def cut(k):
        if _PH < k:
            _dbg_tap()
            B_.wait("sync", "sD")
            B_.wait("sync", "sG")
            return True
        return False

    def _dbg_tap():
        dbg = os.environ.get("KDBG", "")
        if not dbg:
            return
        B_.barrier("sync")
        B_.barrier(G)
        if dbg in ("q", "k", "vtm", "gate", "zt", "gg", "x1"):
            for cc in range(16):
                B_.dma("sync", out[cc * 128:(cc + 1) * 128, :], sp[dbg][cc])
        elif dbg == "mtbt":
            for cc in range(16):
                B_.dma("sync", out[cc * 128:(cc + 1) * 128, 0:512], sp["mt"][cc])
                B_.dma("sync", out[cc * 128:(cc + 1) * 128, 512:1024], sp["bt"][cc])
        elif dbg == "xi":
            for cc in range(NT):
                B_.dma(G, out[cc * 128:(cc + 1) * 128, :], sp["xi"][cc])
        elif dbg == "lnh":
            for half_ in range(2):
                for fc in range(8):
                    B_.dma(G, b["praw"][:, 0:1024],
                           sp["lnh"][fc, :, half_ * 1024:(half_ + 1) * 1024])
                    B_.wait("sync", "sG")
                    B_.dma("sync", out[half_ * 1024 + fc * 128:
                                       half_ * 1024 + (fc + 1) * 128, :]
                           .transpose([1, 0]) if False else
                           out[half_ * 1024 + fc * 128:half_ * 1024 + (fc + 1) * 128, :],
                           b["praw"][0:128, 0:1024])
                    B_.barrier(G, ("sD",))
        elif dbg == "hif":
            B_.dma(G, out[0:128, :], b["hif"][:])
            B_.dma(G, out[128:256, :], b["lof"][:])
            B_.dma("sync", out[256:384, :], b["lnt"][:])
            B_.dma("sync", out[384:512, 0:16], b["lnst"][:])
        elif dbg == "pfm":
            B_.dma("sync", out[0:128, :], b["pfm"][:, 0:1024])
            B_.dma("sync", out[128:256, :], b["pfm"][:, 1024:2048])
            B_.dma("sync", out[256:384, :], b["praw"][:, 0:1024])
            B_.dma("sync", out[384:512, :], b["praw"][:, 1024:2048])
            B_.dma("sync", out[512:640, 0:72], b["praw"][:, 2000:2072])
        elif dbg == "p3o":
            B_.dma("sync", out[0:128, :], b["kfm"][:, 0:1024])
            B_.dma("sync", out[128:256, :], b["qfm"][:, 0:1024])
            B_.dma("sync", out[256:384, :], b["rb"][:, 0:1024])
            B_.dma("sync", out[384:512, 0:512], b["Sb"][:])
            B_.dma("sync", out[512:640, 0:256], b["asc"][:])
        elif dbg == "wo":
            B_.dma(G, out[0:128, :], b["WCh"][:, 0:1024])
            B_.dma(G, out[128:256, :], b["WCl"][:, 0:1024])
            B_.dma(G, out[256:384, :], b["WCh"][:, 1024:2048])
        elif dbg == "yfm":
            B_.dma("sync", out[0:128, :], b["yfm"][:, 0:1024])
            B_.dma("sync", out[128:256, :], b["yfm"][:, 1024:2048])
            B_.dma("sync", out[256:384, :], b["yfm"][:, 2048:3072])
            B_.dma("sync", out[384:512, :], b["yfm"][:, 3072:4096])
            B_.dma(G, out[512:640, :], b["oghi"][:, 0:1024])
            B_.dma(G, out[640:768, :], b["oghi"][:, 4096:5120])
            B_.dma(G, out[768:896, :], b["oglo"][:, 0:1024])
        elif dbg == "xseg":
            B_.dma("sync", out[:, :], io["x_seg"][128:128 + R, :])
        elif dbg == "beta":
            B_.dma("sync", out[0:16, :], b["beta"][:, 0:1024])
            B_.dma("sync", out[16:32, :], b["beta"][:, 1024:2048])
        elif dbg == "asc":
            B_.dma("sync", out[0:128, 0:256], b["asc"][:])
        elif dbg == "sb":
            B_.dma("sync", out[0:128, 0:512], b["Sb"][:])
        elif dbg == "hst":
            for g4 in range(4):
                B_.dma("sync", out[g4 * 128:(g4 + 1) * 128, :],
                       b["hcs"][:, g4 * 1024:(g4 + 1) * 1024])
            for g4 in range(4):
                B_.dma("sync", out[512 + g4 * 128:512 + (g4 + 1) * 128, :],
                       b["htt"][:, g4 * 1024:(g4 + 1) * 1024])
            B_.dma(G, out[1024:1152, :], b["xiw"][:, 0:1024])
            B_.dma(G, out[1152:1280, 0:16], b["xiw"][:, 1024:1040])
            B_.dma("sync", out[1280:1408, 0:130], b["hscp"][:])
            B_.dma(G, out[1408:1536, :], b["hqt"][:, 0:1024])
            B_.dma("sync", out[1536:1664, 0:96], b["c_fcw"][:])
            B_.dma("sync", out[1664:1792, 0:32], b["c_fcb"][:])
            B_.dma("sync", out[1792:1920, 0:64], b["hhalo"][:])
        elif dbg == "hqf":
            qq = int(os.environ.get("KDBGI", "0"))
            for cc in range(4):
                B_.dma(G, out[cc * 512:(cc + 1) * 512, :]
                       .rearrange("(a p) f -> a p f", p=128),
                       sp["hqf"][qq * 4 + cc].rearrange("p (a f) -> a p f", a=4))
    pX, pY = b["pX"], b["pY"]
    id128 = b["c_id128"][:]
    id128b = b["c_id128b"][:]
    pXb = pX[:].bitcast(BF16)
    pYb = pY[:].bitcast(BF16)
    lnst = b["lnst"]

    def half(h):
        return 64 * (h & 1)

    def rsqrt(t_ap, s_ap, w1, w2):
        B_.op(V, "reciprocal", w1, t_ap, inc=("sV", 1))
        B_.wait(SC, "sV")
        B_.act(s_ap, w1, AF.Sqrt, inc=("sS", 1))
        B_.wait(V, "sS")
        B_.op(V, "tensor_mul", w1, t_ap, s_ap)
        B_.op(V, "tensor_mul", w2, w1, s_ap)
        B_.op(V, "tensor_scalar", w1, w2, -0.5, 1.5, ALU.mult, ALU.add)
        B_.op(V, "tensor_mul", s_ap, s_ap, w1)

    def ln_core(x_ap, out_f32=None, hi_ap=None, lo_ap=None):
        P_ = x_ap.shape[0]
        st = lnst[0:P_, :]
        B_.barrier(SC)
        B_.act(b["sqs"][0:P_, :], x_ap, AF.Square,
               accum_out=st[:, 1:2], inc=("sS", 1))
        B_.wait(V, "sS")
        B_.op(V, "reduce_sum", st[:, 0:1], x_ap.unsqueeze(1), axis=AX.X,
              inc=("sF", 1))
        B_.wait(V, "sF")
        B_.op(V, "tensor_scalar_mul", st[:, 0:1], st[:, 0:1], 1.0 / D)
        B_.op(V, "tensor_mul", st[:, 3:4], st[:, 0:1], st[:, 0:1])
        B_.op(V, "tensor_scalar_mul", st[:, 2:3], st[:, 1:2], 1.0 / D)
        B_.op(V, "tensor_sub", st[:, 2:3], st[:, 2:3], st[:, 3:4])
        B_.op(V, "tensor_scalar_add", st[:, 2:3], st[:, 2:3], 1e-5)
        rsqrt(st[:, 2:3], st[:, 5:6], st[:, 6:7], st[:, 7:8])
        B_.op(V, "tensor_mul", st[:, 8:9], st[:, 0:1], st[:, 5:6])
        B_.op(V, "tensor_scalar_mul", st[:, 8:9], st[:, 8:9], -1.0,
              inc=("sV", 1))
        B_.wait(SC, "sV")
        if out_f32 is not None:
            B_.act(out_f32, x_ap, AF.Identity, scale=st[:, 5:6], bias=st[:, 8:9],
                   inc=("sS", 1))
        if hi_ap is not None:
            B_.act(hi_ap, x_ap, AF.Identity, scale=st[:, 5:6], bias=st[:, 8:9],
                   inc=("sS", 1))
            B_.wait(V, "sS")
            B_.op(V, "scalar_tensor_tensor", b["sqs"][0:P_, :], x_ap,
                  st[:, 5:6], hi_ap, ALU.mult, ALU.subtract)
            B_.op(V, "tensor_scalar", lo_ap, b["sqs"][0:P_, :],
                  st[:, 8:9], None, ALU.add, ALU.bypass, inc=("sV", 1))

    for name, buf in (("ident128", "c_id128"), ("identpair", "c_idpair"),
                      ("pmask", "c_pm"), ("pmask1", "c_pm1"),
                      ("hmask", "c_hm"), ("onescol", "c_ones"),
                      ("csel", "c_sel")):
        B_.dma("sync", b[buf][:], io[name][:])
    for i, (wn, bn) in enumerate((("cqw", "cqb"), ("ckw", "ckb"),
                                  ("cvw", "cvb"))):
        B_.dma("sync", b["c_convw"][:, i * 32:(i + 1) * 32]
               .rearrange("p (c f) -> p c f", c=8),
               io[wn].rearrange("(c p) f -> p c f", p=128))
        B_.dma("sync", b["c_convb"][:, i * 8:(i + 1) * 8]
               .rearrange("p (c f) -> p c f", c=8),
               io[bn].rearrange("(c p) f -> p c f", p=128))
    B_.dma(G, b["c_id128b"][:], io["ident128"][:])
    B_.dma(G, b["c_wbh"][:].rearrange("p (c f) -> p c f", c=8),
           io["wb"].rearrange("(c p) f -> p c f", p=128))
    B_.dma("sync", b["wstage"][:, :8 * H].rearrange("p (c f) -> p c f", c=8),
           io["wb"].rearrange("(c p) f -> p c f", p=128))
    B_.wait(V, "sD")
    B_.wait(V, "sG")
    B_.op(V, "tensor_sub", b["c_wbl"][:], b["wstage"][:, :8 * H],
          b["c_wbh"][:], inc=("sV", 1))
    B_.barrier("tensor")
    B_.mm(pX[0:1, 0:128], b["c_ones"][:, 0:1], id128, inc=("sT", 1))
    B_.wait(V, "sT")
    B_.op(V, "tensor_copy", b["onesrow"][:], pX[0:1, 0:128], inc=("sV", 1))

    def load_wpair(wname, hi, lo):
        B_.barrier(G, ("sV",))
        B_.barrier("sync", ("sV",))
        B_.dma(G, b[hi][:].rearrange("p (c f) -> p c f", c=8),
               io[wname].rearrange("(c p) f -> p c f", p=128))
        B_.dma("sync", b["wstage"][:].rearrange("p (c f) -> p c f", c=8),
               io[wname].rearrange("(c p) f -> p c f", p=128))
        B_.wait(V, "sD")
        B_.wait(V, "sG")
        B_.op(V, "tensor_sub", b[lo][:], b["wstage"][:], b[hi][:],
              inc=("sV", 1))

    load_wpair("wq", "WAh", "WAl")
    load_wpair("wk", "WBh", "WBl")

    for tt in range(17):
        B_.barrier("sync")
        B_.dma("sync", b["lnt"][:], io["x_seg"][tt * 128:(tt + 1) * 128, :])
        B_.wait(SC, "sD")
        B_.wait(V, "sD")
        ln_core(b["lnt"][:], hi_ap=b["hif"][:], lo_ap=b["lof"][:])
        B_.barrier("tensor")
        for fc in range(8):
            B_.tp(pXb[:, fc * 128:(fc + 1) * 128],
                  b["hif"][:, fc * 128:(fc + 1) * 128], id128b)
        for fc in range(8):
            B_.tp(pYb[:, fc * 128:(fc + 1) * 128],
                  b["lof"][:, fc * 128:(fc + 1) * 128], id128b,
                  inc=("sT", 1) if fc == 7 else None)
        B_.wait(V, "sT")
        B_.op(V, "tensor_copy", b["hif"][:], pXb[:, :1024])
        B_.op(V, "tensor_copy", b["lof"][:], pYb[:, :1024], inc=("sV", 1))
        B_.wait("sync", "sV")
        B_.dma("sync", sp["lnh"][:, :, tt * 128:(tt + 1) * 128].transpose([1, 0, 2]),
               b["hif"][:].rearrange("p (c f) -> p c f", c=8))
        B_.dma("sync", sp["lnl"][:, :, tt * 128:(tt + 1) * 128].transpose([1, 0, 2]),
               b["lof"][:].rearrange("p (c f) -> p c f", c=8))

    def project(hi, lo, psrc, ncols, dst, dstw, m_list=range(8), pstart=0):
        for mi, m in enumerate(m_list):
            bank = mi % 4
            po = pX[:, bank * 512:bank * 512 + ncols]
            B_.barrier("tensor", ("sV", "sS", "sD", "sG"))
            for kk in range(8):
                B_.mm(po, b[hi][:, kk * D + m * 128:kk * D + (m + 1) * 128],
                      b["lnph"][:, kk * WIN + psrc:kk * WIN + psrc + ncols],
                      start=(kk == 0), stop=False)
                B_.mm(po, b[lo][:, kk * D + m * 128:kk * D + (m + 1) * 128],
                      b["lnph"][:, kk * WIN + psrc:kk * WIN + psrc + ncols],
                      start=False, stop=False)
                B_.mm(po, b[hi][:, kk * D + m * 128:kk * D + (m + 1) * 128],
                      b["lnpl"][:, kk * WIN + psrc:kk * WIN + psrc + ncols],
                      start=False, stop=(kk == 7),
                      inc=("sT", 1) if kk == 7 else None)
            B_.wait(V, "sT")
            B_.op(V, "tensor_copy",
                  dst[:, m * dstw + pstart:m * dstw + pstart + ncols], po,
                  inc=("sV", 1))

    def conv_silu(tensor_idx, do_silu=True):
        cw = b["c_convw"]
        cb_ = b["c_convb"]
        for m in range(8):
            wofs = tensor_idx * 32 + m * 4
            B_.barrier(SC, ("sV", "sD", "sG"))
            B_.act(b["pfm"][:, m * BLK:(m + 1) * BLK],
                   b["praw"][:, m * WIN + 3:m * WIN + 3 + BLK], AF.Identity,
                   scale=cw[:, wofs + 3:wofs + 4],
                   bias=cb_[:, tensor_idx * 8 + m:tensor_idx * 8 + m + 1],
                   inc=("sS", 1))
        B_.wait(V, "sS")
        for j in range(3):
            for m in range(8):
                wofs = tensor_idx * 32 + m * 4
                B_.op(V, "scalar_tensor_tensor",
                      b["pfm"][:, m * BLK:(m + 1) * BLK],
                      b["praw"][:, m * WIN + j:m * WIN + j + BLK],
                      cw[:, wofs + j:wofs + j + 1],
                      b["pfm"][:, m * BLK:(m + 1) * BLK],
                      ALU.mult, ALU.add,
                      inc=("sV", 1) if (j == 2 and m == 7) else None)
        if do_silu:
            B_.wait(SC, "sV")
            B_.act(b["pfm"][:], b["pfm"][:], AF.Silu, inc=("sS", 1))

    def load_window(blk):
        base = 125 + BLK * blk
        B_.barrier("sync")
        B_.dma("sync", b["lnph"][:].rearrange("p (c f) -> p c f", c=8),
               sp["lnh"][:, :, base:base + WIN].transpose([1, 0, 2]))
        B_.dma("sync", b["lnpl"][:].rearrange("p (c f) -> p c f", c=8),
               sp["lnl"][:, :, base:base + WIN].transpose([1, 0, 2]))

    def spill_fm(spname, blk, src):
        for hh in range(2):
            cc = blk * 2 + hh
            B_.dma("sync", sp[spname][cc].rearrange("p (c f) -> p c f", c=8),
                   src.rearrange("p (c f) -> p c f", c=8)[:, :, hh * 128:(hh + 1) * 128])

    if cut(2):
        return
    for blk in range(NBLK):
        load_window(blk)
        B_.wait("tensor", "sD")
        project("WAh", "WAl", 0, WIN, b["praw"], WIN)
        conv_silu(0)
        B_.wait("sync", "sS")
        spill_fm("q", blk, b["pfm"])
        project("WBh", "WBl", 0, WIN, b["praw"], WIN)
        conv_silu(1)
        B_.wait("sync", "sS")
        spill_fm("k", blk, b["pfm"])
        B_.barrier("tensor")
        po = pY[0:16, 0:BLK]
        for kk in range(8):
            B_.mm(po, b["c_wbh"][:, kk * H:(kk + 1) * H],
                  b["lnph"][:, kk * WIN + 3:kk * WIN + 3 + BLK],
                  start=(kk == 0), stop=False)
            B_.mm(po, b["c_wbl"][:, kk * H:(kk + 1) * H],
                  b["lnph"][:, kk * WIN + 3:kk * WIN + 3 + BLK],
                  start=False, stop=False)
            B_.mm(po, b["c_wbh"][:, kk * H:(kk + 1) * H],
                  b["lnpl"][:, kk * WIN + 3:kk * WIN + 3 + BLK],
                  start=False, stop=(kk == 7),
                  inc=("sT", 1) if kk == 7 else None)
        B_.wait(SC, "sT")
        B_.act(b["beta"][:, blk * BLK:(blk + 1) * BLK], po, AF.Sigmoid,
               inc=("sS", 1))

    if cut(3):
        return
    load_wpair("wv", "WCh", "WCl")
    load_wpair("wg", "WDh", "WDl")
    for blk in range(NBLK):
        load_window(blk)
        B_.wait("tensor", "sD")
        if _PB < 2:
            continue
        project("WCh", "WCl", 0, WIN, b["praw"], WIN)
        conv_silu(2)
        if _PB < 3:
            continue
        for hh in range(2):
            cc = blk * 2 + hh
            B_.barrier("tensor")
            for i_, h in enumerate([x for x in range(H) if x % 2 == 0] + [x for x in range(H) if x % 2 == 1]):
                B_.tp(pX[:, h * d:(h + 1) * d],
                      b["pfm"][half(h):half(h) + 64,
                               (h >> 1) * BLK + hh * 128:(h >> 1) * BLK + (hh + 1) * 128],
                      b["c_idpair"][half(h):half(h) + 64, 0:64],
                      inc=("sT", 1) if i_ in (7, H - 1) else None,
                      tpos=(half(h), 0))
                if i_ == 7:
                    B_.wait("tensor", "sT")
            B_.wait(V, "sT")
            B_.op(V, "tensor_copy", b["vstage"][:], pX[:, :H * d],
                  inc=("sV", 1))
            B_.wait("sync", "sV")
            B_.dma("sync", sp["vtm"][cc], b["vstage"][:])
        if _PB < 4:
            continue
        project("WDh", "WDl", 3, BLK, b["pfm"], BLK)
        B_.wait("sync", "sV")
        spill_fm("gate", blk, b["pfm"])

    if cut(4):
        return
    B_.barrier(G)
    for name, buf in (("m_trilS", "c_trilS"), ("m_triuS", "c_triuS"),
                      ("m_triuI", "c_triuI")):
        B_.dma(G, b[buf][:], io[name][:])
    sc_bcol = b["scl"][:, 0:16]
    sc_ssk = b["scl"][:, 16:32]
    sc_ssq = b["scl"][:, 32:48]
    sc_tmp = b["scl"][:, 48:64]
    sc_bsc = b["scl"][:, 64:80]
    sc_bb = b["scl"][:, 80:96]
    sc_bb2 = b["scl"][:, 96:112]
    sc_w2 = b["scl"][:, 112:128]
    B_.barrier(V)
    B_.op(V, "tensor_copy", b["mrun"][:], b["c_idpair"][:])
    B_.op(V, "memset", b["brun"][:], 0.0, inc=("sV", 1))

    for cc in range(NCH):
        B_.barrier("sync")
        B_.dma("sync", b["qfm"][:], sp["q"][cc])
        B_.dma("sync", b["kfm"][:], sp["k"][cc])
        B_.barrier("tensor")
        for i_, h in enumerate([x for x in range(H) if x % 2 == 0] + [x for x in range(H) if x % 2 == 1]):
            B_.tp(pX[:, h * d:(h + 1) * d],
                  b["kfm"][half(h):half(h) + 64, (h >> 1) * C:(h >> 1) * C + C],
                  b["c_idpair"][half(h):half(h) + 64, 0:64], tpos=(half(h), 0),
                  inc=("sT", 1) if i_ in (7, H - 1) else None)
            if i_ == 7:
                B_.wait("tensor", "sT")
        for i_, h in enumerate([x for x in range(H) if x % 2 == 0] + [x for x in range(H) if x % 2 == 1]):
            B_.tp(pY[:, h * d:(h + 1) * d],
                  b["qfm"][half(h):half(h) + 64, (h >> 1) * C:(h >> 1) * C + C],
                  b["c_idpair"][half(h):half(h) + 64, 0:64],
                  inc=("sT", 1) if i_ in (7, H - 1) else None, tpos=(half(h), 0))
            if i_ == 7:
                B_.wait("tensor", "sT")
        B_.wait(V, "sT")
        B_.op(V, "tensor_copy", b["ktm"][:], pX[:, :H * d])
        B_.op(V, "tensor_copy", b["qtm"][:], pY[:, :H * d], inc=("sV", 1))
        B_.barrier("tensor", ("sV",))
        B_.tp(pX[:, 0:16], b["beta"][:, cc * C:(cc + 1) * C],
              b["c_idpair"][0:16, 0:16], inc=("sT", 1))
        B_.wait(V, "sT")
        B_.op(V, "tensor_copy", sc_bcol, pX[:, 0:16], inc=("sV", 1))
        B_.wait(SC, "sV")
        B_.act(b["sqs"][:], b["ktm"][:], AF.Square, inc=("sS", 1))
        B_.wait(V, "sS")
        B_.op(V, "reduce_sum", sc_ssk.unsqueeze(2),
              b["sqs"][:].rearrange("p (h j) -> p h j", h=H), axis=AX.X,
              inc=("sV", 1))
        B_.wait(V, "sV")
        B_.wait(SC, "sV")
        B_.act(b["sqs"][:], b["qtm"][:], AF.Square, inc=("sS", 1))
        B_.wait(V, "sS")
        B_.op(V, "reduce_sum", sc_ssq.unsqueeze(2),
              b["sqs"][:].rearrange("p (h j) -> p h j", h=H), axis=AX.X,
              inc=("sF", 1))
        B_.wait(V, "sF")
        B_.op(V, "tensor_scalar_add", sc_tmp, sc_ssk, 1e-6)
        rsqrt(sc_tmp, sc_bsc, sc_bb, sc_w2)
        B_.op(V, "tensor_scalar", sc_tmp, sc_ssq, 64.0, 64e-6, ALU.mult,
              ALU.add)
        rsqrt(sc_tmp, b["asc"][:, cc * H:(cc + 1) * H], sc_bb, sc_w2)
        B_.wait("sync", "sV")
        B_.wait("sync", "sS")
        B_.dma("sync", b["vtm"][:], sp["vtm"][cc])
        B_.op(V, "tensor_mul", sc_bb, sc_bcol, sc_bsc)
        B_.op(V, "tensor_mul", sc_bb2, sc_bb, sc_bsc)
        for h in range(H):
            B_.op(V, "tensor_scalar_mul", b["kdd"][:, h * d:(h + 1) * d],
                  b["ktm"][:, h * d:(h + 1) * d], sc_bb2[:, h:h + 1])
        B_.wait(V, "sD")
        for h in range(H):
            B_.op(V, "tensor_scalar_mul", b["rb"][:, h * C:h * C + 64],
                  b["vtm"][:, h * d:(h + 1) * d], sc_bb[:, h:h + 1])
        B_.op(V, "tensor_copy",
              b["rb"][:].rearrange("p (h t) -> p h t", h=H)[:, :, 64:128],
              b["kdd"][:].rearrange("p (h j) -> p h j", h=H), inc=("sV", 1))
        B_.barrier("tensor", ("sV",))
        for h in range(H):
            B_.mm(pX[half(h):half(h) + 64, (h >> 1) * C:(h >> 1) * C + C],
                  b["kdd"][:, h * d:(h + 1) * d], id128,
                  inc=("sT", 1) if h == H - 1 else None,
                  tpos=(0, half(h)))
        B_.wait(V, "sT")
        B_.op(V, "tensor_copy", b["kddf"][:], pX[:, :8 * C], inc=("sV", 1))
        B_.barrier("tensor", ("sV",))
        for i_, h in enumerate([x for x in range(H) if x % 2 == 0] + [x for x in range(H) if x % 2 == 1]):
            B_.mm(pX[:, h * C:(h + 1) * C],
                  b["kddf"][half(h):half(h) + 64, (h >> 1) * C:(h >> 1) * C + C],
                  b["kfm"][half(h):half(h) + 64, (h >> 1) * C:(h >> 1) * C + C],
                  tpos=(half(h), 0),
                  inc=("sT", 1) if i_ in (7, H - 1) else None)
            if i_ == 7:
                B_.wait("tensor", "sT")
        for i_, h in enumerate([x for x in range(H) if x % 2 == 0] + [x for x in range(H) if x % 2 == 1]):
            B_.mm(pY[:, h * C:(h + 1) * C],
                  b["kfm"][half(h):half(h) + 64, (h >> 1) * C:(h >> 1) * C + C],
                  b["kddf"][half(h):half(h) + 64, (h >> 1) * C:(h >> 1) * C + C],
                  inc=("sT", 1) if i_ in (7, H - 1) else None,
                  tpos=(half(h), 0))
            if i_ == 7:
                B_.wait("tensor", "sT")
        B_.wait(V, "sT")
        B_.op(V, "tensor_mul", b["Xb"][:], pX[:, :], b["c_trilS"][:])
        B_.op(V, "tensor_mul", b["Xtb"][:], pY[:, :], b["c_triuS"][:],
              inc=("sV", 1))
        B_.barrier("tensor", ("sV",))
        for h in range(H):
            B_.mm(pX[:, h * C:(h + 1) * C], b["Xtb"][:, h * C:(h + 1) * C],
                  b["rb"][:, h * C:(h + 1) * C],
                  inc=("sT", 1) if h == H - 1 else None)
        B_.wait(V, "sT")
        B_.op(V, "tensor_sub", b["rb"][:], b["rb"][:], pX[:, :], inc=("sV", 1))
        cur, curT, nxt, nxtT = "Xb", "Xtb", "Pb", "Ptb"
        for lvl in range(6):
            B_.barrier("tensor", ("sV",))
            if lvl < 5:
                for h in range(H):
                    B_.mm(pX[:, h * C:(h + 1) * C],
                          b[curT][:, h * C:(h + 1) * C],
                          b[cur][:, h * C:(h + 1) * C])
            for h in range(H):
                B_.mm(pY[:, h * C:(h + 1) * C], b[cur][:, h * C:(h + 1) * C],
                      b[curT][:, h * C:(h + 1) * C],
                      inc=("sT", 1) if h == H - 1 else None)
            B_.wait(V, "sT")
            if lvl < 5:
                B_.op(V, "tensor_copy", b[nxt][:], pX[:, :])
            B_.op(V, "tensor_copy", b[nxtT][:], pY[:, :], inc=("sV", 1))
            B_.barrier("tensor", ("sV",))
            for h in range(H):
                B_.mm(pX[:, h * C:(h + 1) * C], b[nxtT][:, h * C:(h + 1) * C],
                      b["rb"][:, h * C:(h + 1) * C],
                      inc=("sT", 1) if h == H - 1 else None)
            B_.wait(V, "sT")
            B_.op(V, "tensor_add", b["rb"][:], b["rb"][:], pX[:, :],
                  inc=("sV", 1))
            cur, curT, nxt, nxtT = nxt, nxtT, cur, curT

        def rV(h):
            return b["rb"][:, h * C:h * C + 64]

        def rK(h):
            return b["rb"][:, h * C + 64:h * C + 128]

        B_.barrier("tensor", ("sV",))
        for i_, h in enumerate([x for x in range(H) if x % 2 == 0] + [x for x in range(H) if x % 2 == 1]):
            B_.mm(pX[:, h * C:(h + 1) * C],
                  b["kfm"][half(h):half(h) + 64, (h >> 1) * C:(h >> 1) * C + C],
                  b["qfm"][half(h):half(h) + 64, (h >> 1) * C:(h >> 1) * C + C],
                  inc=("sT", 1) if i_ in (7, H - 1) else None,
                  tpos=(half(h), 0))
            if i_ == 7:
                B_.wait("tensor", "sT")
        B_.wait(V, "sT")
        B_.op(V, "tensor_mul", b["Gtb"][:], pX[:, :], b["c_triuI"][:],
              inc=("sV", 1))
        B_.barrier("tensor", ("sV",))
        for h in range(H):
            B_.mm(pY[:, h * d:(h + 1) * d], b["Gtb"][:, h * C:(h + 1) * C],
                  rV(h))
        for h in range(H):
            B_.mm(pX[half(h):half(h) + 64, (h >> 1) * C:(h >> 1) * C + C],
                  rK(h), b["Gtb"][:, h * C:(h + 1) * C],
                  inc=("sT", 1) if h == H - 1 else None,
                  tpos=(0, half(h)))
        B_.wait(V, "sT")
        B_.op(V, "tensor_copy", b["stg"][:], pY[:, :H * d])
        B_.op(V, "tensor_sub", b["stg2"][:], b["qfm"][:], pX[:, :8 * C],
              inc=("sV", 1))
        B_.wait("sync", "sV")
        B_.dma("sync", sp["gg"][cc], b["stg"][:])
        B_.dma("sync", sp["zt"][cc], b["stg2"][:])
        B_.barrier("tensor")
        for h in range(H):
            B_.mm(pX[half(h):half(h) + 64, (h >> 1) * d:(h >> 1) * d + d],
                  rK(h), b["ktm"][:, h * d:(h + 1) * d], tpos=(0, half(h)))
        for h in range(H):
            B_.mm(pY[half(h):half(h) + 64, (h >> 1) * d:(h >> 1) * d + d],
                  rV(h), b["ktm"][:, h * d:(h + 1) * d],
                  inc=("sT", 1) if h == H - 1 else None, tpos=(0, half(h)))
        B_.wait(V, "sT")
        B_.op(V, "tensor_sub", b["stg"][:, :8 * d], b["c_idpair"][:],
              pX[:, :8 * d])
        B_.op(V, "tensor_copy", b["stg2"][:, :8 * d], pY[:, :8 * d],
              inc=("sV", 1))
        B_.wait("sync", "sV")
        B_.dma("sync", sp["mt"][cc], b["stg"][:, :8 * d])
        B_.dma("sync", sp["bt"][cc], b["stg2"][:, :8 * d])
        B_.barrier("tensor", ("sV",))
        for i_, h in enumerate([x for x in range(H) if x % 2 == 0] + [x for x in range(H) if x % 2 == 1]):
            hb, hp = half(h), (h >> 1) * d
            B_.mm(pX[hb:hb + 64, hp:hp + d], b["stg"][hb:hb + 64, hp:hp + d],
                  b["mrun"][hb:hb + 64, hp:hp + d], tpos=(hb, hb))
            B_.mm(pY[hb:hb + 64, hp:hp + d], b["stg"][hb:hb + 64, hp:hp + d],
                  b["brun"][hb:hb + 64, hp:hp + d], start=True, stop=False,
                  tpos=(hb, hb))
            B_.mm(pY[hb:hb + 64, hp:hp + d], b["stg2"][hb:hb + 64, hp:hp + d],
                  b["c_idpair"][hb:hb + 64, hp:hp + d], start=False, stop=True,
                  inc=("sT", 1) if i_ in (7, H - 1) else None, tpos=(hb, hb))
            if i_ == 7:
                B_.wait("tensor", "sT")
        B_.wait(V, "sT")
        B_.op(V, "tensor_copy", b["mrun"][:], pX[:, :8 * d])
        B_.op(V, "tensor_copy", b["brun"][:], pY[:, :8 * d], inc=("sV", 1))

    if cut(5):
        return
    B_.wait("sync", "sV")
    B_.dma("sync", sp["cc_in"][0:1, 0:65536]
           .rearrange("a (p f) -> (a p) f", p=128), b["mrun"][:])
    B_.dma("sync", sp["cc_in"][0:1, 65536:131072]
           .rearrange("a (p f) -> (a p) f", p=128), b["brun"][:])
    B_.wait(G, "sD")
    B_._emit(G, lambda e: e.collective_compute(
        "AllGather", ALU.bypass, replica_groups=[[0, 1, 2, 3], [4, 5, 6, 7]],
        ins=[sp["cc_in"][:].opt()], outs=[sp["cc_out"][:].opt()]),
        ("sC", 1))

    def wquant_stream(src_ap_quarters, nquarters, sw_col, absacc=True):
        for qq in range(nquarters):
            B_.barrier("sync")
            B_.dma("sync", b["wstage"][:].rearrange("p (c f) -> p c f", c=8),
                   src_ap_quarters(qq))
            B_.wait(V, "sD")
            B_.op(V, "reduce_sum", b["rowm"][:, qq:qq + 1].unsqueeze(1),
                  b["wstage"][:].unsqueeze(1), axis=AX.X,
                  apply_absolute_value=True, inc=("sV", 1))

    if cut(6):
        return
    load_wpair("wo", "WCh", "WCl")

    B_.wait("sync", "sC", 1)
    B_.barrier(V)
    B_.op(V, "memset", b["Sb"][:], 0.0, inc=("sV", 1))
    for j in range(3):
        B_.barrier("sync", ("sV", "sT"))
        B_.dma("sync", b["Mg"][:], sp["cc_out"][j:j + 1, 0:65536]
               .rearrange("a (p f) -> (a p) f", p=128))
        B_.dma("sync", b["Bg"][:], sp["cc_out"][j:j + 1, 65536:131072]
               .rearrange("a (p f) -> (a p) f", p=128))
        B_.wait(V, "sD")
        B_.op(V, "tensor_scalar", b["Mg"][:], b["Mg"][:], b["c_pm"][:, j:j + 1],
              None, ALU.mult, ALU.bypass)
        B_.op(V, "scalar_tensor_tensor", b["Mg"][:], b["c_idpair"][:],
              b["c_pm1"][:, j:j + 1], b["Mg"][:], ALU.mult, ALU.add)
        B_.op(V, "tensor_scalar", b["Bg"][:], b["Bg"][:], b["c_pm"][:, j:j + 1],
              None, ALU.mult, ALU.bypass, inc=("sV", 1))
        B_.barrier("tensor", ("sV",))
        for i_, h in enumerate([x for x in range(H) if x % 2 == 0] + [x for x in range(H) if x % 2 == 1]):
            hb, hp = half(h), (h >> 1) * d
            off = 0 if hb == 0 else 512
            B_.mm(pX[hb:hb + 64, off + hp:off + hp + d],
                  b["Mg"][hb:hb + 64, hp:hp + d],
                  b["c_idpair"][hb:hb + 64, 0:64],
                  inc=("sT", 1) if i_ in (7, H - 1) else None, tpos=(hb, hb))
        B_.wait(V, "sT")
        B_.op(V, "tensor_copy", b["Mg"][0:64, :], pX[0:64, 0:512])
        B_.op(V, "tensor_copy", b["Mg"][64:128, :], pX[64:128, 512:1024],
              inc=("sV", 1))
        B_.barrier("tensor", ("sV",))
        for i_, h in enumerate([x for x in range(H) if x % 2 == 0] + [x for x in range(H) if x % 2 == 1]):
            hb, hp = half(h), (h >> 1) * d
            off = 0 if hb == 0 else 512
            B_.mm(pY[hb:hb + 64, off + hp:off + hp + d],
                  b["Mg"][hb:hb + 64, hp:hp + d],
                  b["Sb"][hb:hb + 64, hp:hp + d], start=True, stop=False,
                  tpos=(hb, hb))
            B_.mm(pY[hb:hb + 64, off + hp:off + hp + d],
                  b["c_idpair"][hb:hb + 64, hp:hp + d],
                  b["Bg"][hb:hb + 64, hp:hp + d], start=False, stop=True,
                  inc=("sT", 1) if i_ in (7, H - 1) else None, tpos=(hb, hb))
        B_.wait(V, "sT")
        B_.op(V, "tensor_copy", b["Sb"][0:64, :], pY[0:64, 0:512])
        B_.op(V, "tensor_copy", b["Sb"][64:128, :], pY[64:128, 512:1024],
              inc=("sV", 1))

    if cut(7):
        return
    B_.barrier("sync", ("sV", "sT"))
    B_.dma("sync", b["c_nwrep"][:], io["nwrep"][:])
    sc_rs = b["scl"][:, 0:16]
    sc_t3 = b["scl"][:, 16:32]
    sc_w3 = b["scl"][:, 32:48]
    sc_w4 = b["scl"][:, 48:64]
    for cc in range(int(os.environ.get("KCC", NCH))):
        ztb = b["Xb"]
        ggb = b["Xtb"]
        mtb = b["Pb"]
        gfb = b["Ptb"]
        B_.barrier("sync")
        B_.dma("sync", ztb[:, :8 * C], sp["zt"][cc])
        B_.dma("sync", ggb[:, :H * d], sp["gg"][cc])
        B_.dma("sync", mtb[:, 0:8 * d], sp["mt"][cc])
        B_.dma("sync", mtb[:, 8 * d:16 * d], sp["bt"][cc])
        B_.dma("sync", gfb[:, :8 * C], sp["gate"][cc])
        B_.wait("tensor", "sD")
        B_.barrier("tensor", ("sV", "sS"))
        for i_, h in enumerate([x for x in range(H) if x % 2 == 0] + [x for x in range(H) if x % 2 == 1]):
            hb = half(h)
            B_.mm(pX[:, h * d:(h + 1) * d],
                  ztb[hb:hb + 64, (h >> 1) * C:(h >> 1) * C + C],
                  b["Sb"][hb:hb + 64, (h >> 1) * d:(h >> 1) * d + d],
                  inc=("sT", 1) if i_ in (7, H - 1) else None, tpos=(hb, 0))
            if i_ == 7:
                B_.wait("tensor", "sT")
        for i_, h in enumerate([x for x in range(H) if x % 2 == 0] + [x for x in range(H) if x % 2 == 1]):
            hb, hp = half(h), (h >> 1) * d
            off = 0 if hb == 0 else 512
            B_.mm(pY[hb:hb + 64, off + hp:off + hp + d],
                  mtb[hb:hb + 64, hp:hp + d],
                  b["Sb"][hb:hb + 64, hp:hp + d], start=True, stop=False,
                  tpos=(hb, hb))
            B_.mm(pY[hb:hb + 64, off + hp:off + hp + d],
                  mtb[hb:hb + 64, 8 * d + hp:8 * d + hp + d],
                  b["c_idpair"][hb:hb + 64, 0:64], start=False, stop=True,
                  inc=("sT", 1) if i_ in (7, H - 1) else None, tpos=(hb, hb))
        B_.wait(V, "sT")
        B_.op(V, "tensor_add", b["rb"][:, :H * d], pX[:, :H * d], ggb[:, :H * d])
        B_.op(V, "tensor_copy", b["Sb"][0:64, :], pY[0:64, 0:512])
        B_.op(V, "tensor_copy", b["Sb"][64:128, :], pY[64:128, 512:1024],
              inc=("sV", 1))
        for h in range(H):
            B_.op(V, "tensor_scalar_mul", b["rb"][:, h * d:(h + 1) * d],
                  b["rb"][:, h * d:(h + 1) * d],
                  b["asc"][:, cc * H + h:cc * H + h + 1],
                  inc=("sV", 1) if h == H - 1 else None)
        B_.wait(SC, "sV")
        B_.act(b["sqs"][:], b["rb"][:, :H * d], AF.Square, inc=("sS", 1))
        B_.wait(V, "sS")
        B_.op(V, "reduce_sum", sc_t3.unsqueeze(2),
              b["sqs"][:].rearrange("p (h j) -> p h j", h=H), axis=AX.X,
              inc=("sF", 1))
        B_.wait(V, "sF")
        B_.op(V, "tensor_scalar", sc_t3, sc_t3, 1.0 / d, 1e-5, ALU.mult,
              ALU.add)
        rsqrt(sc_t3, sc_rs, sc_w3, sc_w4)
        for h in range(H):
            B_.op(V, "scalar_tensor_tensor", b["rb"][:, h * d:(h + 1) * d],
                  b["rb"][:, h * d:(h + 1) * d], sc_rs[:, h:h + 1],
                  b["c_nwrep"][:, h * d:(h + 1) * d], ALU.mult, ALU.mult,
                  inc=("sV", 1) if h == H - 1 else None)
        B_.barrier("tensor", ("sV",))
        for fc in range(8):
            B_.tp(pX[:, fc * 128:(fc + 1) * 128],
                  gfb[:, fc * 128:(fc + 1) * 128], id128,
                  inc=("sT", 1) if fc == 7 else None)
        B_.wait(SC, "sT")
        B_.act(b["qfm"][:, :1024], pX[:, :1024], AF.Silu, inc=("sS", 1))
        B_.wait(V, "sS")
        B_.op(V, "tensor_mul", b["kfm"][:, :1024], b["rb"][:, :H * d],
              b["qfm"][:, :1024])
        B_.op(V, "tensor_copy", b["hif"][:], b["kfm"][:, :1024])
        B_.op(V, "tensor_sub", b["lof"][:], b["kfm"][:, :1024], b["hif"][:],
              inc=("sV", 1))
        B_.barrier("tensor", ("sV",))
        for fc in range(8):
            B_.tp(pXb[:, fc * 128:(fc + 1) * 128],
                  b["hif"][:, fc * 128:(fc + 1) * 128], id128b)
        for fc in range(8):
            B_.tp(pYb[:, fc * 128:(fc + 1) * 128],
                  b["lof"][:, fc * 128:(fc + 1) * 128], id128b,
                  inc=("sT", 1) if fc == 7 else None)
        B_.wait(V, "sT")
        B_.op(V, "tensor_copy",
              b["oghi"][:].rearrange("p (c f) -> p c f", c=16)[:, 0:8, (cc % 4) * 128:(cc % 4 + 1) * 128],
              pXb[:, :1024].rearrange("p (c f) -> p c f", c=8))
        B_.op(V, "tensor_copy",
              b["oglo"][:].rearrange("p (c f) -> p c f", c=16)[:, 0:8, (cc % 4) * 128:(cc % 4 + 1) * 128],
              pYb[:, :1024].rearrange("p (c f) -> p c f", c=8), inc=("sV", 1))
        if cc % 4 == 3:
            sb4 = cc // 4
            for m in range(8):
                po = pX[:, (m % 4) * 512:(m % 4) * 512 + 512]
                B_.barrier("tensor", ("sV",))
                for kk in range(8):
                    B_.mm(po, b["WCh"][:, kk * 1024 + m * 128:kk * 1024 + (m + 1) * 128],
                          b["oghi"][:, kk * 512:(kk + 1) * 512],
                          start=(kk == 0), stop=False)
                    B_.mm(po, b["WCl"][:, kk * 1024 + m * 128:kk * 1024 + (m + 1) * 128],
                          b["oghi"][:, kk * 512:(kk + 1) * 512],
                          start=False, stop=False)
                    B_.mm(po, b["WCh"][:, kk * 1024 + m * 128:kk * 1024 + (m + 1) * 128],
                          b["oglo"][:, kk * 512:(kk + 1) * 512],
                          start=False, stop=(kk == 7),
                          inc=("sT", 1) if kk == 7 else None)
                B_.wait(V, "sT")
                B_.op(V, "tensor_copy", b["yfm"][:, m * 512:(m + 1) * 512], po,
                      inc=("sV", 1))
            for tt4 in range(4):
                gt = sb4 * 4 + tt4
                B_.barrier("tensor", ("sV", "sS"))
                for mc in range(8):
                    B_.tp(pY[:, mc * 128:(mc + 1) * 128],
                          b["yfm"][:, mc * 512 + tt4 * 128:mc * 512 + (tt4 + 1) * 128],
                          id128, inc=("sT", 1) if mc == 7 else None)
                B_.wait(V, "sT")
                B_.wait(V, "sD")
                B_.op(V, "tensor_copy", b["x1t"][:], pY[:, :1024], inc=("sV", 1))
                ln_core(b["x1t"][:], out_f32=b["lnt"][:])
                B_.barrier("sync", ("sV", "sS"))
                B_.dma("sync", b["xt"][:],
                       io["x_seg"][128 + gt * 128:128 + (gt + 1) * 128, :])
                B_.wait(V, "sD")
                B_.wait(V, "sS")
                B_.op(V, "tensor_add", b["x1t"][:], b["lnt"][:], b["xt"][:],
                      inc=("sV", 1))
                B_.wait("sync", "sV")
                B_.dma("sync", sp["x1"][gt], b["x1t"][:])
                if gt == 0:
                    B_.dma("sync", sp["cb_in"][0:1, 0:D], b["x1t"][0:1, :])
                if gt == NT - 1:
                    B_.dma("sync", sp["cb_in"][0:1, D:2 * D],
                           b["x1t"][127:128, :])

    B_.wait(G, "sD")
    B_._emit(G, lambda e: e.collective_compute(
        "AllGather", ALU.bypass, replica_groups=[[0, 1, 2, 3], [4, 5, 6, 7]],
        ins=[sp["cb_in"][:].opt()], outs=[sp["cb_out"][:].opt()]),
        ("sC", 1))

    if cut(8):
        return
    def w1q(qq):
        return io["fw1"][qq * 1024:(qq + 1) * 1024, :] \
            .rearrange("(c p) f -> p c f", p=128)

    wquant_stream(w1q, 4, 0)
    B_.barrier("tensor", ("sV",))
    B_.mm(pX[0:1, 0:4], b["c_ones"][:, 0:1], b["rowm"][:, 0:4], inc=("sT", 1))
    B_.wait(V, "sT")
    B_.op(V, "reduce_sum", b["ffs"][0:1, 0:1].unsqueeze(1),
          pX[0:1, 0:4].unsqueeze(1), axis=AX.X, inc=("sF", 1))
    B_.wait(V, "sF")
    B_.op(V, "tensor_scalar_mul", b["ffs"][0:1, 1:2], b["ffs"][0:1, 0:1],
          1.0 / (DH * D))
    B_.op(V, "tensor_scalar_max", b["ffs"][0:1, 1:2], b["ffs"][0:1, 1:2],
          1e-5)
    B_.op(V, "reciprocal", b["ffs"][0:1, 2:3], b["ffs"][0:1, 1:2],
          inc=("sV", 1))
    B_.barrier("tensor", ("sV",))
    B_.mm(pX[:, 0:1], b["onesrow"][:], b["ffs"][0:1, 2:3], inc=("sT", 1))
    B_.mm(pX[:, 1:2], b["onesrow"][:], b["ffs"][0:1, 1:2], inc=("sT", 1))
    B_.wait(V, "sT")
    B_.op(V, "tensor_copy", b["ffsb"][:, 0:2], pX[:, 0:2], inc=("sV", 1))

    W1T = [("WAh", 0), ("WAh", 1), ("WAl", 0), ("WAl", 1),
           ("WBh", 0), ("WBh", 1), ("WBl", 0), ("WBl", 1)]

    for qq in range(4):
        B_.barrier("sync")
        B_.dma("sync", b["wstage"][:].rearrange("p (c f) -> p c f", c=8),
               w1q(qq))
        B_.wait(V, "sD")
        for hh2 in range(4):
            sl = slice(hh2 * 2048, (hh2 + 1) * 2048)
            B_.op(V, "tensor_scalar", b["wstage"][:, sl], b["wstage"][:, sl],
                  b["ffsb"][:, 0:1], None, ALU.mult, ALU.bypass)
            B_.op(V, "tensor_scalar", b["wstage"][:, sl], b["wstage"][:, sl],
                  -1.0, 1.0, ALU.max, ALU.min)
            B_.op(V, "tensor_copy", b["wi32"][:], b["wstage"][:, sl])
            B_.op(V, "tensor_copy", b["wq16"][:, sl], b["wi32"][:],
                  inc=("sV", 1))
        for dc in range(8):
            buf, plane = W1T[dc]
            B_.barrier("tensor", ("sV",))
            for jc in range(8):
                B_.tp(pXb[:, jc * 128:(jc + 1) * 128],
                      b["wq16"][:, jc * 1024 + dc * 128:jc * 1024 + (dc + 1) * 128],
                      id128b, inc=("sT", 1) if jc == 7 else None)
            B_.wait(V, "sT")
            B_.op(V, "tensor_copy",
                  b[buf][:, plane * 4096 + qq * 1024:plane * 4096 + (qq + 1) * 1024],
                  pXb[:, 0:1024], inc=("sV", 1))

    def w1qT_sl(dc, jc):
        buf, plane = W1T[dc]
        return b[buf][:, plane * 4096 + jc * 128:plane * 4096 + (jc + 1) * 128]


    FB = 128
    B_.barrier(V)
    B_.op(V, "memset", b["hscr"][:], 1.0)
    B_.op(V, "memset", b["zscr"][:], 1.0, inc=("sV", 1))

    if cut(8):
        return
    B_.dma("sync", b["c_fcw"][:].rearrange("p (j t) -> p j t", j=32),
           io["fcw"][:].rearrange("(j p) t -> p j t", p=128))
    B_.dma("sync", b["c_fcb"][:].unsqueeze(2),
           io["fcb"][:].rearrange("(j p) o -> p j o", p=128))
    B_.wait(SC, "sD")
    for tt in range(NT):
        B_.barrier("sync")
        B_.dma("sync", b["x1t"][:], sp["x1"][tt])
        B_.wait(V, "sD")
        B_.wait(SC, "sD")
        ln_core(b["x1t"][:], out_f32=b["lnt"][:])
        B_.wait(V, "sS")
        B_.op(V, "reduce_max", b["rowm"][:, 0:1].unsqueeze(1),
              b["lnt"][:].unsqueeze(1), axis=AX.X, apply_absolute_value=True,
              inc=("sF", 1))
        B_.wait(V, "sF")
        B_.op(V, "tensor_scalar_max", b["rowm"][:, 1:2], b["rowm"][:, 0:1],
              1e-5)
        B_.op(V, "reciprocal", b["rowm"][:, 2:3], b["rowm"][:, 1:2],
              inc=("sF", 1))
        B_.wait(V, "sF")
        B_.op(V, "tensor_scalar", b["sqs"][:], b["lnt"][:],
              b["rowm"][:, 2:3], 127.0, ALU.mult, ALU.mult)
        B_.op(V, "tensor_copy", b["xi32"][:], b["sqs"][:])
        B_.op(V, "tensor_copy", b["xit"][:], b["xi32"][:])
        B_.op(V, "tensor_scalar", b["rowm"][:, 3:4], b["rowm"][:, 1:2],
              b["ffsb"][:, 1:2], 1.0 / 127.0, ALU.mult, ALU.mult,
              inc=("sV", 1))
        B_.barrier("tensor", ("sV",))
        for fc in range(8):
            B_.tp(pXb[:, fc * 128:(fc + 1) * 128],
                  b["xit"][:, fc * 128:(fc + 1) * 128], id128b)
        B_.mm(pY[0:1, 0:128], b["rowm"][:, 3:4], id128, inc=("sT", 1))
        B_.wait(V, "sT")
        B_.op(V, "tensor_copy", b["hqt"][:, 0:1024], pXb[:, 0:1024])
        B_.op(V, "tensor_copy", b["hscr"][0:1, 1 + tt * 128:1 + (tt + 1) * 128],
              pY[0:1, 0:128], inc=("sV", 1))
        B_.wait("sync", "sV")
        B_.dma("sync", sp["xi"][tt], b["hqt"][:, 0:1024])

    B_.wait("sync", "sC", 2)
    B_.barrier("sync", ("sV", "sS", "sT"))
    B_.dma("sync", b["cball"][:], sp["cb_out"][:])
    B_.wait("tensor", "sD")
    B_.mm(pX[0:1, 0:512], b["c_sel"][:, 0:1], b["cball"][:, D:D + 512])
    B_.mm(pX[0:1, 512:1024], b["c_sel"][:, 0:1], b["cball"][:, D + 512:2 * D])
    B_.mm(pX[0:1, 1024:1536], b["c_sel"][:, 1:2], b["cball"][:, 0:512])
    B_.mm(pX[0:1, 1536:2048], b["c_sel"][:, 1:2], b["cball"][:, 512:D],
          inc=("sT", 1))
    B_.wait(V, "sT")
    B_.op(V, "tensor_copy", b["brow"][:], pX[0:1, 0:2048], inc=("sV", 1))
    for s in range(2):
        xrow = b["brow"][0:1, s * D:(s + 1) * D]
        ln_core(xrow, out_f32=b["lnt"][0:1, :])
        B_.wait(V, "sS")
        B_.op(V, "reduce_max", b["rowm"][0:1, 4:5].unsqueeze(1),
              b["lnt"][0:1, :].unsqueeze(1), axis=AX.X,
              apply_absolute_value=True, inc=("sF", 1))
        B_.wait(V, "sF")
        B_.op(V, "tensor_scalar_max", b["rowm"][0:1, 4:5], b["rowm"][0:1, 4:5],
              1e-5)
        B_.op(V, "reciprocal", b["rowm"][0:1, 5:6], b["rowm"][0:1, 4:5],
              inc=("sF", 1))
        B_.wait(V, "sF")
        B_.op(V, "tensor_scalar", b["sqs"][0:1, :], b["lnt"][0:1, :],
              b["rowm"][0:1, 5:6], 127.0, ALU.mult, ALU.mult)
        B_.op(V, "tensor_copy", b["xi32"][0:1, :], b["sqs"][0:1, :])
        B_.op(V, "tensor_copy", b["xit"][0:1, :], b["xi32"][0:1, :])
        B_.op(V, "tensor_scalar", b["rowm"][0:1, 6:7], b["rowm"][0:1, 4:5],
              b["ffs"][0:1, 1:2], 1.0 / 127.0, ALU.mult, ALU.mult,
              inc=("sV", 1))
        B_.barrier("tensor", ("sV",))
        for fc in range(8):
            B_.mm(pY[:, fc:fc + 1], b["xit"][0:1, fc * 128:(fc + 1) * 128],
                  id128b[0:1, 0:1],
                  inc=("sT", 1) if fc == 7 else None)
        B_.mm(pY[:, 8:9], b["onesrow"][:], b["rowm"][0:1, 6:7], inc=("sT", 1))
        B_.wait(V, "sT")
        B_.op(V, "tensor_copy", b["xif_h"][:], pY[:, 0:8])
        B_.op(V, "tensor_copy", b["rowm"][:, 6:7], pY[:, 8:9], inc=("sV", 1))
        B_.barrier("tensor", ("sV",))
        for jc in range(32):
            for dc in range(8):
                B_.mm(pX[:, jc:jc + 1], w1qT_sl(dc, jc),
                      b["xif_h"][:, dc:dc + 1], start=(dc == 0),
                      stop=(dc == 7),
                      inc=("sT", 1) if (jc == 31 and dc == 7) else None)
        B_.wait(V, "sT")
        B_.op(V, "tensor_scalar", b["hhalo"][:, s * 32:(s + 1) * 32],
              pX[:, 0:32], b["rowm"][:, 6:7], b["c_hm"][:, s:s + 1],
              ALU.mult, ALU.mult, inc=("sV", 1))

    if cut(9):
        return
    for fb in range(int(os.environ.get("KFB", NT))):
        B_.barrier("tensor", ("sV", "sS"))
        B_.mm(pX[:, 0:FB + 2], b["onesrow"][:],
              b["hscr"][0:1, fb * FB:fb * FB + FB + 2], inc=("sT", 1))
        B_.wait(V, "sT")
        B_.op(V, "tensor_copy", b["hscp"][:], pX[:, 0:FB + 2], inc=("sV", 1))
        B_.barrier("sync", ("sV",))
        if fb == 0:
            B_.dma("sync", b["xib"][:, 0:1024], sp["xi"][0])
        if fb + 1 < NT:
            B_.dma("sync", b["xib"][:, ((fb + 1) % 3) * 1024:
                                     ((fb + 1) % 3) * 1024 + 1024],
                   sp["xi"][fb + 1])
        B_.wait(V, "sD")
        xiwr = b["xiw"][:].rearrange("p (c f) -> p c f", c=8)
        cur = b["xib"][:, (fb % 3) * 1024:(fb % 3) * 1024 + 1024] \
            .rearrange("p (c f) -> p c f", c=8)
        B_.op(V, "tensor_copy", xiwr[:, :, 1:FB + 1], cur)
        if fb == 0:
            B_.op(V, "memset", xiwr[:, :, 0:1], 0.0)
        else:
            prev = b["xib"][:, ((fb - 1) % 3) * 1024:
                            ((fb - 1) % 3) * 1024 + 1024] \
                .rearrange("p (c f) -> p c f", c=8)
            B_.op(V, "tensor_copy", xiwr[:, :, 0:1], prev[:, :, 127:128])
        if fb == NT - 1:
            B_.op(V, "memset", xiwr[:, :, FB + 1:FB + 2], 0.0, inc=("sV", 1))
        else:
            nxt = b["xib"][:, ((fb + 1) % 3) * 1024:
                           ((fb + 1) % 3) * 1024 + 1024] \
                .rearrange("p (c f) -> p c f", c=8)
            B_.op(V, "tensor_copy", xiwr[:, :, FB + 1:FB + 2],
                  nxt[:, :, 0:1], inc=("sV", 1))
        B_.barrier("tensor", ("sV",))
        for grp in range(4):
            B_.barrier("tensor", ("sV",))
            for j2 in range(8):
                jc = grp * 8 + j2
                po = (pX if j2 < 4 else pY)[:, (j2 % 4) * 512:(j2 % 4) * 512 + FB + 2]
                for dc in range(8):
                    B_.mm(po, w1qT_sl(dc, jc),
                          b["xiw"][:, dc * (FB + 2):(dc + 1) * (FB + 2)],
                          start=(dc == 0), stop=(dc == 7),
                          inc=("sT", 1) if (j2 == 7 and dc == 7) else None)
            B_.wait(V, "sT")
            for j2 in range(8):
                jc = grp * 8 + j2
                po = (pX if j2 < 4 else pY)[:, (j2 % 4) * 512:(j2 % 4) * 512 + FB + 2]
                B_.op(V, "tensor_mul",
                      b["hraw"][:, jc * (FB + 2):(jc + 1) * (FB + 2)],
                      po, b["hscp"][:],
                      inc=("sV", 1) if j2 == 7 else None)
        if fb == 0:
            B_.op(V, "tensor_copy",
                  b["hraw"][:].rearrange("p (j f) -> p j f", j=32)[:, :, 0:1],
                  b["hhalo"][:, 0:32].unsqueeze(2), inc=("sV", 1))
        if fb == NT - 1:
            B_.op(V, "tensor_copy",
                  b["hraw"][:].rearrange("p (j f) -> p j f", j=32)
                  [:, :, FB + 1:FB + 2],
                  b["hhalo"][:, 32:64].unsqueeze(2), inc=("sV", 1))
        B_.barrier(SC, ("sV",))
        for jc in range(32):
            B_.act(b["hcs"][:, jc * FB:(jc + 1) * FB],
                   b["hraw"][:, jc * (FB + 2) + 1:jc * (FB + 2) + 1 + FB],
                   AF.Identity, scale=b["c_fcw"][:, jc * 3 + 1:jc * 3 + 2],
                   bias=b["c_fcb"][:, jc:jc + 1],
                   inc=("sS", 1) if jc == 31 else None)
        B_.wait(V, "sS")
        for j in (0, 2):
            for jc in range(32):
                B_.op(V, "scalar_tensor_tensor",
                      b["hcs"][:, jc * FB:(jc + 1) * FB],
                      b["hraw"][:, jc * (FB + 2) + j:jc * (FB + 2) + j + FB],
                      b["c_fcw"][:, jc * 3 + j:jc * 3 + j + 1],
                      b["hcs"][:, jc * FB:(jc + 1) * FB],
                      ALU.mult, ALU.add,
                      inc=("sV", 1) if (j == 2 and jc == 31) else None)
        B_.wait(SC, "sV")
        B_.act(b["hcs"][:], b["hcs"][:], AF.Silu, inc=("sS", 1))
        for rnd in range(2):
            B_.barrier("tensor", ("sV", "sS"))
            for jj in range(16):
                jc = rnd * 16 + jj
                po = pX if jj < 8 else pY
                B_.tp(po[:, (jj % 8) * 128:(jj % 8 + 1) * 128],
                      b["hcs"][:, jc * FB:(jc + 1) * FB], id128,
                      inc=("sT", 1) if jj == 15 else None)
            B_.wait(V, "sT")
            B_.op(V, "tensor_copy",
                  b["htt"][:, rnd * 2048:rnd * 2048 + 1024], pX[:, 0:1024])
            B_.op(V, "tensor_copy",
                  b["htt"][:, rnd * 2048 + 1024:rnd * 2048 + 2048],
                  pY[:, 0:1024], inc=("sV", 1))
        B_.op(V, "reduce_max", b["rowm"][:, 4:5].unsqueeze(1),
              b["htt"][:].unsqueeze(1), axis=AX.X, apply_absolute_value=True,
              inc=("sF", 1))
        B_.wait(V, "sF")
        B_.op(V, "tensor_scalar_max", b["rowm"][:, 5:6], b["rowm"][:, 4:5],
              1e-5)
        B_.op(V, "reciprocal", b["rowm"][:, 6:7], b["rowm"][:, 5:6],
              inc=("sF", 1))
        B_.wait(V, "sF")
        B_.op(V, "tensor_scalar", b["htt"][:], b["htt"][:],
              b["rowm"][:, 6:7], 127.0, ALU.mult, ALU.mult)
        for g in range(2):
            B_.op(V, "tensor_copy", b["hi32"][:],
                  b["htt"][:, g * 2048:(g + 1) * 2048])
            B_.op(V, "tensor_copy", b["hqt"][:, g * 2048:(g + 1) * 2048],
                  b["hi32"][:])
        B_.op(V, "tensor_scalar_mul", b["rowm"][:, 7:8], b["rowm"][:, 5:6],
              1.0 / 127.0, inc=("sV", 1))
        B_.barrier("tensor", ("sV",))
        B_.mm(pX[0:1, 1536:1664], b["rowm"][:, 7:8], id128, inc=("sT", 1))
        B_.wait(V, "sT")
        B_.op(V, "tensor_copy",
              b["zscr"][0:1, fb * FB:(fb + 1) * FB], pX[0:1, 1536:1664],
              inc=("sV", 1))
        for rnd in range(2):
            B_.barrier("tensor", ("sV",))
            for jj in range(16):
                jc = rnd * 16 + jj
                pob = pXb if jj < 8 else pYb
                B_.tp(pob[:, (jj % 8) * 128:(jj % 8 + 1) * 128],
                      b["hqt"][:, jc * 128:(jc + 1) * 128], id128b,
                      inc=("sT", 1) if jj == 15 else None)
            B_.wait(V, "sT")
            for g in range(2):
                pob = pXb if g == 0 else pYb
                B_.op(V, "tensor_copy",
                      b["hqf"][:].rearrange("p (j f) -> p j f", j=32)
                      [:, rnd * 16 + g * 8:rnd * 16 + (g + 1) * 8, :],
                      pob[:, 0:1024].rearrange("p (j f) -> p j f", j=8),
                      inc=("sV", 1) if g == 1 else None)
        B_.wait("sync", "sV")
        B_.dma("sync", sp["hqf"][fb], b["hqf"][:])

    def w2chunk(mc):
        return io["fw2"][mc * 128:(mc + 1) * 128, :]

    for mc in range(8):
        B_.barrier("sync")
        B_.dma("sync", b["wstage"][:, :DH], w2chunk(mc))
        B_.wait(V, "sD")
        B_.op(V, "reduce_sum", b["rowm"][:, mc:mc + 1].unsqueeze(1),
              b["wstage"][:, :DH].unsqueeze(1), axis=AX.X,
              apply_absolute_value=True, inc=("sV", 1))
    B_.barrier("tensor", ("sV",))
    B_.mm(pX[0:1, 0:8], b["c_ones"][:, 0:1], b["rowm"][:, 0:8], inc=("sT", 1))
    B_.wait(V, "sT")
    B_.op(V, "reduce_sum", b["ffs"][0:1, 3:4].unsqueeze(1),
          pX[0:1, 0:8].unsqueeze(1), axis=AX.X, inc=("sF", 1))
    B_.wait(V, "sF")
    B_.op(V, "tensor_scalar_mul", b["ffs"][0:1, 4:5], b["ffs"][0:1, 3:4],
          1.0 / (DH * D))
    B_.op(V, "tensor_scalar_max", b["ffs"][0:1, 4:5], b["ffs"][0:1, 4:5], 1e-5)
    B_.op(V, "reciprocal", b["ffs"][0:1, 5:6], b["ffs"][0:1, 4:5],
          inc=("sV", 1))
    B_.barrier("tensor", ("sV",))
    B_.mm(pX[:, 0:1], b["onesrow"][:], b["ffs"][0:1, 5:6])
    B_.mm(pX[:, 1:2], b["onesrow"][:], b["ffs"][0:1, 4:5], inc=("sT", 1))
    B_.wait(V, "sT")
    B_.op(V, "tensor_copy", b["ffsb"][:, 2:4], pX[:, 0:2], inc=("sV", 1))

    W2T = [("WCh", 0), ("WCh", 1), ("WCl", 0), ("WCl", 1),
           ("WDh", 0), ("WDh", 1), ("WDl", 0), ("WDl", 1)]

    def w2qT_sl(jc, mc):
        bufn = ("WCh", "WCl", "WDh", "WDl")[jc // 8]
        off = (jc % 8) * 1024 + mc * 128
        return b[bufn][:, off:off + 128]

    for mc in range(8):
        B_.barrier("sync")
        B_.dma("sync", b["wstage"][:, :DH], w2chunk(mc))
        B_.wait(V, "sD")
        for hh2 in range(2):
            sl = slice(hh2 * 2048, (hh2 + 1) * 2048)
            B_.op(V, "tensor_scalar", b["wstage"][:, sl], b["wstage"][:, sl],
                  b["ffsb"][:, 2:3], None, ALU.mult, ALU.bypass)
            B_.op(V, "tensor_scalar", b["wstage"][:, sl], b["wstage"][:, sl],
                  -1.0, 1.0, ALU.max, ALU.min)
            B_.op(V, "tensor_copy", b["wi32"][:], b["wstage"][:, sl])
            B_.op(V, "tensor_copy", b["hqt"][:, sl], b["wi32"][:],
                  inc=("sV", 1))
        for rnd in range(2):
            B_.barrier("tensor", ("sV",))
            for jj in range(16):
                jc = rnd * 16 + jj
                pob = pXb if jj < 8 else pYb
                B_.tp(pob[:, (jj % 8) * 128:(jj % 8 + 1) * 128],
                      b["hqt"][:, jc * 128:(jc + 1) * 128], id128b,
                      inc=("sT", 1) if jj == 15 else None)
            B_.wait(V, "sT")
            for g4 in range(2):
                pob = pXb if g4 == 0 else pYb
                base_jc = rnd * 16 + g4 * 8
                bufn = ("WCh", "WCl", "WDh", "WDl")[base_jc // 8]
                B_.op(V, "tensor_copy",
                      b[bufn][:].rearrange("p (j f) -> p j f", j=8)
                      [:, :, mc * 128:(mc + 1) * 128],
                      pob[:, 0:1024].rearrange("p (j f) -> p j f", j=8),
                      inc=("sV", 1) if g4 == 1 else None)

    if cut(10):
        return
    for fb in range(NT):
        B_.barrier("sync")
        B_.dma("sync", b["hqf"][:], sp["hqf"][fb])
        B_.barrier("tensor", ("sV", "sS"))
        B_.mm(pX[:, 0:FB], b["onesrow"][:],
              b["zscr"][0:1, fb * FB:(fb + 1) * FB], inc=("sT", 1))
        B_.wait(V, "sT")
        B_.op(V, "tensor_scalar", b["zscp"][:], pX[:, 0:FB],
              b["ffsb"][:, 3:4], None, ALU.mult, ALU.bypass, inc=("sV", 1))
        if _PS < 2:
            continue
        B_.barrier("tensor", ("sV", "sD"))
        for mc in range(8):
            po = (pX if mc < 4 else pY)[:, (mc % 4) * 512:(mc % 4) * 512 + FB]
            for jc in range(32):
                B_.mm(po, w2qT_sl(jc, mc), b["hqf"][:, jc * FB:(jc + 1) * FB],
                      start=(jc == 0), stop=(jc == 31),
                      inc=("sT", 1) if (mc == 7 and jc == 31) else None)
        B_.wait(V, "sT")
        for mc in range(8):
            po = (pX if mc < 4 else pY)[:, (mc % 4) * 512:(mc % 4) * 512 + FB]
            B_.op(V, "tensor_mul", b["zf"][:, mc * FB:(mc + 1) * FB], po,
                  b["zscp"][:], inc=("sV", 1) if mc == 7 else None)
        if _PS < 3:
            continue
        B_.barrier("tensor", ("sV", "sS"))
        for mc in range(8):
            B_.tp(pX[:, mc * 128:(mc + 1) * 128],
                  b["zf"][:, mc * FB:(mc + 1) * FB], id128,
                  inc=("sT", 1) if mc == 7 else None)
        B_.wait(V, "sT")
        B_.op(V, "tensor_copy", b["xt"][:], pX[:, 0:1024], inc=("sV", 1))
        ln_core(b["xt"][:], out_f32=b["lnt"][:])
        B_.barrier("sync", ("sV", "sS"))
        B_.dma("sync", b["x1t"][:], sp["x1"][fb])
        B_.wait(V, "sD")
        B_.wait(V, "sS")
        B_.op(V, "tensor_add", b["praw"][:, 0:1024], b["lnt"][:],
              b["x1t"][:], inc=("sV", 1))
        B_.wait("sync", "sV")
        B_.dma("sync", out[fb * 128:(fb + 1) * 128, :], b["praw"][:, 0:1024])

    _dbg_tap()

    B_.wait("sync", "sD")
    B_.wait("sync", "sG")


_NC_CACHE = {}


def _get_nc():
    if "nc" not in _NC_CACHE:
        _NC_CACHE["nc"] = build_nc()
    return _NC_CACHE["nc"]


def kernel(**inputs):
    x = np.asarray(inputs["L0_attn_x"], dtype=np.float32)
    f32 = np.float32

    def c_(a):
        return np.ascontiguousarray(np.asarray(a, dtype=f32))

    base = {
        "wq": c_(inputs["L0_attn_Wq"]), "wk": c_(inputs["L0_attn_Wk"]),
        "wv": c_(inputs["L0_attn_Wv"]), "wg": c_(inputs["L0_attn_Wg"]),
        "wo": c_(inputs["L0_attn_Wo"]), "wb": c_(inputs["L0_attn_Wb"]),
        "cqw": c_(inputs["L0_attn_cq_w"]),
        "cqb": c_(inputs["L0_attn_cq_b"]).reshape(D, 1),
        "ckw": c_(inputs["L0_attn_ck_w"]),
        "ckb": c_(inputs["L0_attn_ck_b"]).reshape(D, 1),
        "cvw": c_(inputs["L0_attn_cv_w"]),
        "cvb": c_(inputs["L0_attn_cv_b"]).reshape(D, 1),
        "nwrep": np.ascontiguousarray(
            np.tile(np.asarray(inputs["L0_attn_nw"], f32).reshape(1, d),
                    (128, H))),
        "fw1": c_(inputs["L0_ffn_w1"]),
        "fw2": c_(inputs["L0_ffn_w2"]),
        "fcw": c_(inputs["L0_ffn_conv_w"]).reshape(DH, 3),
        "fcb": c_(inputs["L0_ffn_conv_b"]).reshape(DH, 1),
        "ident128": np.eye(128, dtype=f32),
        "identpair": np.ascontiguousarray(
            np.tile(np.concatenate([np.eye(d, dtype=f32)] * 2, axis=0),
                    (1, 8))),
        "m_trilS": np.ascontiguousarray(
            np.tile(np.tril(np.ones((128, 128), f32), -1), (1, H))),
        "m_triuS": np.ascontiguousarray(
            np.tile(np.triu(np.ones((128, 128), f32), 1), (1, H))),
        "m_triuI": np.ascontiguousarray(
            np.tile(np.triu(np.ones((128, 128), f32), 0), (1, H))),
        "onescol": np.ones((128, 2), f32),
    }
    in_maps = []
    for core in range(8):
        bidx, pos = core // 4, core % 4
        seg0 = pos * R
        halo3 = (x[bidx, seg0 - 3:seg0] if pos > 0
                 else np.zeros((3, D), f32))
        x_seg = np.concatenate(
            [np.zeros((125, D), f32), halo3, x[bidx, seg0:seg0 + R]], axis=0)
        pm = np.zeros((128, 4), f32)
        pm[:, :3] = [1.0 if j < pos else 0.0 for j in range(3)]
        pm1 = np.zeros((128, 4), f32)
        pm1[:, :3] = 1.0 - pm[:, :3]
        hm = np.zeros((128, 2), f32)
        hm[:, 0] = 1.0 if pos > 0 else 0.0
        hm[:, 1] = 1.0 if pos < 3 else 0.0
        cs = np.zeros((4, 2), f32)
        if pos > 0:
            cs[pos - 1, 0] = 1.0
        if pos < 3:
            cs[pos + 1, 1] = 1.0
        m = dict(base)
        m["x_seg"] = np.ascontiguousarray(x_seg)
        m["pmask"] = pm
        m["pmask1"] = pm1
        m["hmask"] = hm
        m["csel"] = cs
        in_maps.append(m)

    nc = _get_nc()
    res = run_bass_kernel_spmd(nc, in_maps, core_ids=list(range(8)))
    outp = np.empty((B, T, D), dtype=np.float32)
    for core in range(8):
        bidx, pos = core // 4, core % 4
        outp[bidx, pos * R:(pos + 1) * R, :] = res.results[core]["out"]
    return outp


# version 08
# speedup vs baseline: 1.0074x; 1.0047x over previous
import os
import numpy as np
import ml_dtypes
import concourse.bass as bass
import concourse.mybir as mybir
from concourse.bass_utils import run_bass_kernel_spmd
from contextlib import ExitStack

F32 = mybir.dt.float32
BF16 = mybir.dt.bfloat16
I32 = mybir.dt.int32
AF = mybir.ActivationFunctionType
ALU = mybir.AluOpType
AX = mybir.AxisListType

B, T, D = 2, 8192, 1024
H, d = 16, 64
R = 2048
C = 128
NCH = R // C
NT = R // 128
BLK = 256
NBLK = R // BLK
DH = 4096
XROWS = 128 + R
WIN = 259


class Bld:
    def __init__(self, nc):
        self.nc = nc
        self.ops = {k: [] for k in ("sync", "gpsimd", "vector", "scalar", "tensor")}
        self.cnt = {k: 0 for k in ("sD", "sG", "sT", "sV", "sS", "sC", "sF")}
        self.sems = {}

    def _emit(self, eng, fn, inc=None):
        if inc is not None:
            semname, n = inc
            s = self.sems[semname]

            def g(e, fn=fn, s=s, n=n):
                fn(e).then_inc(s, n)
            self.ops[eng].append(g)
            self.cnt[semname] += n
        else:
            self.ops[eng].append(fn)

    def wait(self, eng, semname, val=None):
        s = self.sems[semname]
        v = self.cnt[semname] if val is None else val
        if v <= 0:
            return
        self.ops[eng].append(lambda e, s=s, v=v: e.wait_ge(s, v))

    def barrier(self, eng, sems=("sD", "sG", "sT", "sV", "sS")):
        for s in sems:
            self.wait(eng, s)

    def dma(self, q, out, in_):
        sn = "sD" if q == "sync" else "sG"
        self._emit(q, lambda e, o=out, i=in_: e.dma_start(out=o, in_=i, allow_slow_non_contiguous=True), (sn, 16))

    def mm(self, out, lhsT, rhs, start=True, stop=True, inc=None, tpos=None):
        self._emit("tensor",
                   lambda e, o=out, l=lhsT, r=rhs, s1=start, s2=stop, t=tpos:
                   e.matmul(o, l, r, start=s1, stop=s2, tile_position=t), inc)

    def tp(self, out, in_, ident, inc=None, tpos=None):
        self._emit("tensor",
                   lambda e, o=out, i=in_, dd=ident, t=tpos:
                   e.transpose(o, i, dd, tile_position=t), inc)

    def op(self, eng, meth, *args, inc=None, **kw):
        def f(e, m=meth, a=args, k=kw):
            return getattr(e, m)(*a, **k)
        fence = False
        if eng in ("vector", "gpsimd") and args:
            try:
                fence = args[0].free_size() < 64
            except Exception:
                fence = False
        if fence and inc is None:
            inc = ("sF", 1)
        self._emit(eng, f, inc)
        if fence:
            self.wait(eng, inc[0])

    def act(self, out, in_, func, scale=1.0, bias=0.0, accum_out=None, inc=None):
        kw = {}
        if accum_out is not None:
            kw["accum_out"] = accum_out
        self._emit("scalar",
                   lambda e, o=out, i=in_, f=func, s=scale, bb=bias, k=kw:
                   e.activation(out=o, in_=i, func=f, scale=s, bias=bb, **k), inc)


def bcl(ap, n):
    return ap.unsqueeze(ap.ndim).broadcast_to(list(ap.shape) + [n])


def build_nc():
    nc = bass.Bass("TRN2", target_bir_lowering=False, debug=False)

    def din(name, shape, dt=F32):
        return nc.dram_tensor(name, shape, dt, kind="ExternalInput").ap()

    io = {}
    io["x_seg"] = din("x_seg", [XROWS, D])
    for n in ("wq", "wk", "wv", "wg", "wo"):
        io[n] = din(n, [D, D])
    io["wb"] = din("wb", [D, H])
    for n in ("cqw", "ckw", "cvw"):
        io[n] = din(n, [D, 4])
    for n in ("cqb", "ckb", "cvb"):
        io[n] = din(n, [D, 1])
    io["nwrep"] = din("nwrep", [128, H * d])
    io["fw1"] = din("fw1", [DH, D])
    io["fw2"] = din("fw2", [D, DH])
    io["fcw"] = din("fcw", [DH, 3])
    io["fcb"] = din("fcb", [DH, 1])
    io["ident128"] = din("ident128", [128, 128])
    io["identpair"] = din("identpair", [128, 8 * d])
    io["m_trilS"] = din("m_trilS", [128, H * C])
    io["m_triuS"] = din("m_triuS", [128, H * C])
    io["m_triuI"] = din("m_triuI", [128, H * C])
    io["pmask"] = din("pmask", [128, 4])
    io["pmask1"] = din("pmask1", [128, 4])
    io["hmask"] = din("hmask", [128, 2])
    io["onescol"] = din("onescol", [128, 2])
    io["csel"] = din("csel", [4, 2])

    out = nc.dram_tensor("out", [R, D], F32, kind="ExternalOutput").ap()

    def scr(name, shape, dt=F32):
        return nc.dram_tensor(name, shape, dt).ap()

    sp = {}
    sp["lnh"] = scr("sp_lnh", [8, 128, XROWS], BF16)
    sp["lnl"] = scr("sp_lnl", [8, 128, XROWS], BF16)
    sp["q"] = scr("sp_q", [NCH, 128, 8 * C])
    sp["k"] = scr("sp_k", [NCH, 128, 8 * C])
    sp["vtm"] = scr("sp_vtm", [NCH, 128, H * d])
    sp["gate"] = scr("sp_gate", [NCH, 128, 8 * C])
    sp["zt"] = scr("sp_zt", [NCH, 128, 8 * C])
    sp["gg"] = scr("sp_gg", [NCH, 128, H * d])
    sp["mt"] = scr("sp_mt", [NCH, 128, 8 * d])
    sp["bt"] = scr("sp_bt", [NCH, 128, 8 * d])
    sp["x1"] = scr("sp_x1", [NT, 128, D])
    sp["xi"] = scr("sp_xi", [NT, 128, 1024], BF16)
    sp["hqf"] = scr("sp_hqf", [NT, 128, 32 * 128], BF16)
    sp["cc_in"] = scr("cc_in", [1, 2 * 128 * 8 * d])
    sp["cc_out"] = scr("cc_out", [4, 2 * 128 * 8 * d])
    sp["cb_in"] = scr("cb_in", [1, 2 * D])
    sp["cb_out"] = scr("cb_out", [4, 2 * D])

    bld = Bld(nc)

    with ExitStack() as ctx:
        def sb(name, shape, dt=F32):
            return ctx.enter_context(nc.sbuf_tensor(name, shape, dt))

        def psum(name, shape):
            return ctx.enter_context(nc.psum_tensor(name, shape, F32))

        for s in ("sD", "sG", "sT", "sV", "sS", "sC", "sF"):
            bld.sems[s] = ctx.enter_context(nc.semaphore(s))

        b = {}
        PF = sb("PF", [128, 28224])
        PB = sb("PB", [128, 43040], BF16)
        PI = sb("PI", [128, 2048], I32)

        def fs(a, n, p=128, p0=0):
            return PF[p0:p0 + p, a:a + n]

        def bs(a, n, p=128, p0=0):
            return PB[p0:p0 + p, a:a + n]

        b["lnt"] = fs(0, 1024); b["sqs"] = fs(1024, 1024)
        b["x1t"] = fs(2048, 1024); b["xt"] = fs(3072, 1024)
        b["lnst"] = fs(4096, 16); b["rowm"] = fs(4112, 8)
        b["scl"] = fs(4120, 128); b["asc"] = fs(4248, 256)
        b["ffsb"] = fs(4504, 4)
        b["onesrow"] = fs(8608, 128, p=1)
        b["ffs"] = fs(8736, 8, p=4)
        b["praw"] = fs(8744, 8 * WIN)
        b["pfm"] = fs(10816, 2048)
        b["beta"] = fs(4510, 2048, p=16)
        b["wstage"] = fs(8744, 8192)
        b["qfm"] = fs(8744, 1024); b["kfm"] = fs(9768, 1024)
        b["ktm"] = fs(10792, 1024); b["qtm"] = fs(11816, 1024)
        b["vtm"] = b["qtm"]
        b["kdd"] = fs(14912, 1024); b["kddf"] = fs(15936, 1024)
        b["vstage"] = fs(14912, 1024)
        b["Xb"] = fs(16960, 2048); b["Xtb"] = fs(19008, 2048)
        b["Pb"] = fs(21056, 2048); b["Ptb"] = fs(23104, 2048)
        b["rb"] = fs(25152, 2048)
        b["mrun"] = fs(27200, 512); b["brun"] = fs(27712, 512)
        b["Gtb"] = b["Pb"]
        b["stg"] = b["kdd"]; b["stg2"] = b["kddf"]
        b["Sb"] = fs(27200, 512); b["Mg"] = fs(27712, 512)
        b["c_nwrep"] = fs(14912, 1024)
        b["Bg"] = fs(11816, 512)
        b["yfm"] = fs(10792, 4096)
        b["hraw"] = fs(8744, 32 * 130)
        b["htt"] = fs(8744, 4096)
        b["hcs"] = fs(12904, 4096)
        b["hscp"] = fs(17000, 130); b["zscp"] = fs(17130, 128)
        b["hscr"] = fs(17258, 2050, p=1)
        b["zscr"] = fs(19308, 2048, p=1)
        b["cball"] = fs(21356, 2048, p=4)
        b["brow"] = fs(23404, 2048, p=1)
        b["zf"] = fs(25452, 1024)
        b["xib"] = PF[:].bitcast(BF16)[:, 52952:56024]
        b["WAh"] = bs(0, 8192); b["WAl"] = bs(8192, 8192)
        b["WBh"] = bs(16384, 8192); b["WBl"] = bs(24576, 8192)
        b["WCh"] = b["WAh"]; b["WCl"] = b["WAl"]
        b["WDh"] = b["WBh"]; b["WDl"] = b["WBl"]
        b["lnph"] = bs(32768, 8 * WIN); b["lnpl"] = bs(34840, 8 * WIN)
        b["hif"] = bs(36912, 1024); b["lof"] = bs(37936, 1024)
        b["c_trilS"] = bs(0, 2048); b["c_triuS"] = bs(2048, 2048)
        b["c_triuI"] = bs(4096, 2048)
        b["oghi"] = bs(16384, 8192); b["oglo"] = bs(24576, 8192)
        b["xiw"] = bs(32768, 8 * 130); b["xit"] = bs(33808, 1024)
        b["wq16"] = bs(32768, 8192)
        b["hqt"] = bs(34832, 4096); b["hqf"] = bs(38928, 4096)
        b["wi32"] = PI[:, 0:2048]
        b["xi32"] = PI[:, 0:1024]
        b["hi32"] = PI[:, 0:2048]
        b["c_id128"] = sb("c_id128", [128, 128])
        b["c_idpair"] = sb("c_idpair", [128, 8 * d])
        b["c_convw"] = sb("c_convw", [128, 8 * 12])
        b["c_convb"] = sb("c_convb", [128, 8 * 3])
        b["c_wbh"] = sb("c_wbh", [128, 8 * H], BF16)
        b["c_wbl"] = sb("c_wbl", [128, 8 * H], BF16)
        b["c_pm"] = sb("c_pm", [128, 4])
        b["c_pm1"] = sb("c_pm1", [128, 4])
        b["c_hm"] = sb("c_hm", [128, 2])
        b["c_ones"] = sb("c_ones", [128, 2])
        b["c_sel"] = sb("c_sel", [4, 2])
        b["hhalo"] = sb("hhalo", [128, 2 * 32])
        b["c_fcw"] = sb("c_fcw", [128, 32 * 3])
        b["c_id128b"] = sb("c_id128b", [128, 128], BF16)
        b["c_fcb"] = sb("c_fcb", [128, 32])
        b["xif_h"] = sb("xif_h", [128, 8], BF16)
        b["pX"] = psum("pX", [128, H * C])
        b["pY"] = psum("pY", [128, H * C])

        emit(nc, bld, b, io, sp, out)

        with nc.Block() as block:
            @block.sync
            def _(e):
                for f in bld.ops["sync"]:
                    f(e)

            @block.gpsimd
            def _(e):
                for f in bld.ops["gpsimd"]:
                    f(e)

            @block.vector
            def _(e):
                for f in bld.ops["vector"]:
                    f(e)

            @block.scalar
            def _(e):
                for f in bld.ops["scalar"]:
                    f(e)

            @block.tensor
            def _(e):
                for f in bld.ops["tensor"]:
                    f(e)

    return nc


_PH = int(os.environ.get("KPH", "99"))
_PB = int(os.environ.get("KPB", "99"))
_PS = int(os.environ.get("KPS", "99"))


def emit(nc, B_, b, io, sp, out):
    V, SC, G = "vector", "scalar", "gpsimd"

    def cut(k):
        if _PH < k:
            _dbg_tap()
            B_.wait("sync", "sD")
            B_.wait("sync", "sG")
            return True
        return False

    def _dbg_tap():
        dbg = os.environ.get("KDBG", "")
        if not dbg:
            return
        B_.barrier("sync")
        B_.barrier(G)
        if dbg in ("q", "k", "vtm", "gate", "zt", "gg", "x1"):
            for cc in range(16):
                B_.dma("sync", out[cc * 128:(cc + 1) * 128, :], sp[dbg][cc])
        elif dbg == "mtbt":
            for cc in range(16):
                B_.dma("sync", out[cc * 128:(cc + 1) * 128, 0:512], sp["mt"][cc])
                B_.dma("sync", out[cc * 128:(cc + 1) * 128, 512:1024], sp["bt"][cc])
        elif dbg == "xi":
            for cc in range(NT):
                B_.dma(G, out[cc * 128:(cc + 1) * 128, :], sp["xi"][cc])
        elif dbg == "lnh":
            for half_ in range(2):
                for fc in range(8):
                    B_.dma(G, b["praw"][:, 0:1024],
                           sp["lnh"][fc, :, half_ * 1024:(half_ + 1) * 1024])
                    B_.wait("sync", "sG")
                    B_.dma("sync", out[half_ * 1024 + fc * 128:
                                       half_ * 1024 + (fc + 1) * 128, :]
                           .transpose([1, 0]) if False else
                           out[half_ * 1024 + fc * 128:half_ * 1024 + (fc + 1) * 128, :],
                           b["praw"][0:128, 0:1024])
                    B_.barrier(G, ("sD",))
        elif dbg == "hif":
            B_.dma(G, out[0:128, :], b["hif"][:])
            B_.dma(G, out[128:256, :], b["lof"][:])
            B_.dma("sync", out[256:384, :], b["lnt"][:])
            B_.dma("sync", out[384:512, 0:16], b["lnst"][:])
        elif dbg == "pfm":
            B_.dma("sync", out[0:128, :], b["pfm"][:, 0:1024])
            B_.dma("sync", out[128:256, :], b["pfm"][:, 1024:2048])
            B_.dma("sync", out[256:384, :], b["praw"][:, 0:1024])
            B_.dma("sync", out[384:512, :], b["praw"][:, 1024:2048])
            B_.dma("sync", out[512:640, 0:72], b["praw"][:, 2000:2072])
        elif dbg == "p3o":
            B_.dma("sync", out[0:128, :], b["kfm"][:, 0:1024])
            B_.dma("sync", out[128:256, :], b["qfm"][:, 0:1024])
            B_.dma("sync", out[256:384, :], b["rb"][:, 0:1024])
            B_.dma("sync", out[384:512, 0:512], b["Sb"][:])
            B_.dma("sync", out[512:640, 0:256], b["asc"][:])
        elif dbg == "wo":
            B_.dma(G, out[0:128, :], b["WCh"][:, 0:1024])
            B_.dma(G, out[128:256, :], b["WCl"][:, 0:1024])
            B_.dma(G, out[256:384, :], b["WCh"][:, 1024:2048])
        elif dbg == "yfm":
            B_.dma("sync", out[0:128, :], b["yfm"][:, 0:1024])
            B_.dma("sync", out[128:256, :], b["yfm"][:, 1024:2048])
            B_.dma("sync", out[256:384, :], b["yfm"][:, 2048:3072])
            B_.dma("sync", out[384:512, :], b["yfm"][:, 3072:4096])
            B_.dma(G, out[512:640, :], b["oghi"][:, 0:1024])
            B_.dma(G, out[640:768, :], b["oghi"][:, 4096:5120])
            B_.dma(G, out[768:896, :], b["oglo"][:, 0:1024])
        elif dbg == "xseg":
            B_.dma("sync", out[:, :], io["x_seg"][128:128 + R, :])
        elif dbg == "beta":
            B_.dma("sync", out[0:16, :], b["beta"][:, 0:1024])
            B_.dma("sync", out[16:32, :], b["beta"][:, 1024:2048])
        elif dbg == "asc":
            B_.dma("sync", out[0:128, 0:256], b["asc"][:])
        elif dbg == "sb":
            B_.dma("sync", out[0:128, 0:512], b["Sb"][:])
        elif dbg == "hst":
            for g4 in range(4):
                B_.dma("sync", out[g4 * 128:(g4 + 1) * 128, :],
                       b["hcs"][:, g4 * 1024:(g4 + 1) * 1024])
            for g4 in range(4):
                B_.dma("sync", out[512 + g4 * 128:512 + (g4 + 1) * 128, :],
                       b["htt"][:, g4 * 1024:(g4 + 1) * 1024])
            B_.dma(G, out[1024:1152, :], b["xiw"][:, 0:1024])
            B_.dma(G, out[1152:1280, 0:16], b["xiw"][:, 1024:1040])
            B_.dma("sync", out[1280:1408, 0:130], b["hscp"][:])
            B_.dma(G, out[1408:1536, :], b["hqt"][:, 0:1024])
            B_.dma("sync", out[1536:1664, 0:96], b["c_fcw"][:])
            B_.dma("sync", out[1664:1792, 0:32], b["c_fcb"][:])
            B_.dma("sync", out[1792:1920, 0:64], b["hhalo"][:])
        elif dbg == "hqf":
            qq = int(os.environ.get("KDBGI", "0"))
            for cc in range(4):
                B_.dma(G, out[cc * 512:(cc + 1) * 512, :]
                       .rearrange("(a p) f -> a p f", p=128),
                       sp["hqf"][qq * 4 + cc].rearrange("p (a f) -> a p f", a=4))
    pX, pY = b["pX"], b["pY"]
    id128 = b["c_id128"][:]
    id128b = b["c_id128b"][:]
    pXb = pX[:].bitcast(BF16)
    pYb = pY[:].bitcast(BF16)
    lnst = b["lnst"]

    def half(h):
        return 64 * (h & 1)

    def rsqrt(t_ap, s_ap, w1, w2):
        B_.op(V, "reciprocal", w1, t_ap, inc=("sV", 1))
        B_.wait(SC, "sV")
        B_.act(s_ap, w1, AF.Sqrt, inc=("sS", 1))
        B_.wait(V, "sS")
        B_.op(V, "tensor_mul", w1, t_ap, s_ap)
        B_.op(V, "tensor_mul", w2, w1, s_ap)
        B_.op(V, "tensor_scalar", w1, w2, -0.5, 1.5, ALU.mult, ALU.add)
        B_.op(V, "tensor_mul", s_ap, s_ap, w1)

    def ln_core(x_ap, out_f32=None, hi_ap=None, lo_ap=None):
        P_ = x_ap.shape[0]
        st = lnst[0:P_, :]
        B_.barrier(SC)
        B_.act(b["sqs"][0:P_, :], x_ap, AF.Square,
               accum_out=st[:, 1:2], inc=("sS", 1))
        B_.wait(V, "sS")
        B_.op(V, "reduce_sum", st[:, 0:1], x_ap.unsqueeze(1), axis=AX.X,
              inc=("sF", 1))
        B_.wait(V, "sF")
        B_.op(V, "tensor_scalar_mul", st[:, 0:1], st[:, 0:1], 1.0 / D)
        B_.op(V, "tensor_mul", st[:, 3:4], st[:, 0:1], st[:, 0:1])
        B_.op(V, "tensor_scalar_mul", st[:, 2:3], st[:, 1:2], 1.0 / D)
        B_.op(V, "tensor_sub", st[:, 2:3], st[:, 2:3], st[:, 3:4])
        B_.op(V, "tensor_scalar_add", st[:, 2:3], st[:, 2:3], 1e-5)
        rsqrt(st[:, 2:3], st[:, 5:6], st[:, 6:7], st[:, 7:8])
        B_.op(V, "tensor_mul", st[:, 8:9], st[:, 0:1], st[:, 5:6])
        B_.op(V, "tensor_scalar_mul", st[:, 8:9], st[:, 8:9], -1.0,
              inc=("sV", 1))
        B_.wait(SC, "sV")
        if out_f32 is not None:
            B_.act(out_f32, x_ap, AF.Identity, scale=st[:, 5:6], bias=st[:, 8:9],
                   inc=("sS", 1))
        if hi_ap is not None:
            B_.act(hi_ap, x_ap, AF.Identity, scale=st[:, 5:6], bias=st[:, 8:9],
                   inc=("sS", 1))
            B_.wait(V, "sS")
            B_.op(V, "scalar_tensor_tensor", b["sqs"][0:P_, :], x_ap,
                  st[:, 5:6], hi_ap, ALU.mult, ALU.subtract)
            B_.op(V, "tensor_scalar", lo_ap, b["sqs"][0:P_, :],
                  st[:, 8:9], None, ALU.add, ALU.bypass, inc=("sV", 1))

    for name, buf in (("ident128", "c_id128"), ("identpair", "c_idpair"),
                      ("pmask", "c_pm"), ("pmask1", "c_pm1"),
                      ("hmask", "c_hm"), ("onescol", "c_ones"),
                      ("csel", "c_sel")):
        B_.dma("sync", b[buf][:], io[name][:])
    for i, (wn, bn) in enumerate((("cqw", "cqb"), ("ckw", "ckb"),
                                  ("cvw", "cvb"))):
        B_.dma("sync", b["c_convw"][:, i * 32:(i + 1) * 32]
               .rearrange("p (c f) -> p c f", c=8),
               io[wn].rearrange("(c p) f -> p c f", p=128))
        B_.dma("sync", b["c_convb"][:, i * 8:(i + 1) * 8]
               .rearrange("p (c f) -> p c f", c=8),
               io[bn].rearrange("(c p) f -> p c f", p=128))
    B_.dma(G, b["c_id128b"][:], io["ident128"][:])
    B_.dma(G, b["c_wbh"][:].rearrange("p (c f) -> p c f", c=8),
           io["wb"].rearrange("(c p) f -> p c f", p=128))
    B_.dma("sync", b["wstage"][:, :8 * H].rearrange("p (c f) -> p c f", c=8),
           io["wb"].rearrange("(c p) f -> p c f", p=128))
    B_.wait(V, "sD")
    B_.wait(V, "sG")
    B_.op(V, "tensor_sub", b["c_wbl"][:], b["wstage"][:, :8 * H],
          b["c_wbh"][:], inc=("sV", 1))
    B_.barrier("tensor")
    B_.mm(pX[0:1, 0:128], b["c_ones"][:, 0:1], id128, inc=("sT", 1))
    B_.wait(V, "sT")
    B_.op(V, "tensor_copy", b["onesrow"][:], pX[0:1, 0:128], inc=("sV", 1))

    def load_wpair(wname, hi, lo):
        B_.barrier(G, ("sV",))
        B_.barrier("sync", ("sV",))
        B_.dma(G, b[hi][:].rearrange("p (c f) -> p c f", c=8),
               io[wname].rearrange("(c p) f -> p c f", p=128))
        B_.dma("sync", b["wstage"][:].rearrange("p (c f) -> p c f", c=8),
               io[wname].rearrange("(c p) f -> p c f", p=128))
        B_.wait(V, "sD")
        B_.wait(V, "sG")
        B_.op(V, "tensor_sub", b[lo][:], b["wstage"][:], b[hi][:],
              inc=("sV", 1))

    load_wpair("wq", "WAh", "WAl")
    load_wpair("wk", "WBh", "WBl")

    for tt in range(17):
        B_.barrier("sync")
        B_.dma("sync", b["lnt"][:], io["x_seg"][tt * 128:(tt + 1) * 128, :])
        B_.wait(SC, "sD")
        B_.wait(V, "sD")
        ln_core(b["lnt"][:], hi_ap=b["hif"][:], lo_ap=b["lof"][:])
        B_.barrier("tensor")
        for fc in range(8):
            B_.tp(pXb[:, fc * 128:(fc + 1) * 128],
                  b["hif"][:, fc * 128:(fc + 1) * 128], id128b)
        for fc in range(8):
            B_.tp(pYb[:, fc * 128:(fc + 1) * 128],
                  b["lof"][:, fc * 128:(fc + 1) * 128], id128b,
                  inc=("sT", 1) if fc == 7 else None)
        B_.wait(V, "sT")
        B_.op(V, "tensor_copy", b["hif"][:], pXb[:, :1024])
        B_.op(V, "tensor_copy", b["lof"][:], pYb[:, :1024], inc=("sV", 1))
        B_.wait("sync", "sV")
        B_.dma("sync", sp["lnh"][:, :, tt * 128:(tt + 1) * 128].transpose([1, 0, 2]),
               b["hif"][:].rearrange("p (c f) -> p c f", c=8))
        B_.dma("sync", sp["lnl"][:, :, tt * 128:(tt + 1) * 128].transpose([1, 0, 2]),
               b["lof"][:].rearrange("p (c f) -> p c f", c=8))

    def project(hi, lo, psrc, ncols, dst, dstw, m_list=range(8), pstart=0):
        for mi, m in enumerate(m_list):
            bank = mi % 4
            po = pX[:, bank * 512:bank * 512 + ncols]
            B_.barrier("tensor", ("sV", "sS", "sD", "sG"))
            for kk in range(8):
                B_.mm(po, b[hi][:, kk * D + m * 128:kk * D + (m + 1) * 128],
                      b["lnph"][:, kk * WIN + psrc:kk * WIN + psrc + ncols],
                      start=(kk == 0), stop=False)
                B_.mm(po, b[lo][:, kk * D + m * 128:kk * D + (m + 1) * 128],
                      b["lnph"][:, kk * WIN + psrc:kk * WIN + psrc + ncols],
                      start=False, stop=False)
                B_.mm(po, b[hi][:, kk * D + m * 128:kk * D + (m + 1) * 128],
                      b["lnpl"][:, kk * WIN + psrc:kk * WIN + psrc + ncols],
                      start=False, stop=(kk == 7),
                      inc=("sT", 1) if kk == 7 else None)
            B_.wait(V, "sT")
            B_.op(V, "tensor_copy",
                  dst[:, m * dstw + pstart:m * dstw + pstart + ncols], po,
                  inc=("sV", 1))

    def conv_silu(tensor_idx, do_silu=True):
        cw = b["c_convw"]
        cb_ = b["c_convb"]
        for m in range(8):
            wofs = tensor_idx * 32 + m * 4
            B_.barrier(SC, ("sV", "sD", "sG"))
            B_.act(b["pfm"][:, m * BLK:(m + 1) * BLK],
                   b["praw"][:, m * WIN + 3:m * WIN + 3 + BLK], AF.Identity,
                   scale=cw[:, wofs + 3:wofs + 4],
                   bias=cb_[:, tensor_idx * 8 + m:tensor_idx * 8 + m + 1],
                   inc=("sS", 1))
        B_.wait(V, "sS")
        for j in range(3):
            for m in range(8):
                wofs = tensor_idx * 32 + m * 4
                B_.op(V, "scalar_tensor_tensor",
                      b["pfm"][:, m * BLK:(m + 1) * BLK],
                      b["praw"][:, m * WIN + j:m * WIN + j + BLK],
                      cw[:, wofs + j:wofs + j + 1],
                      b["pfm"][:, m * BLK:(m + 1) * BLK],
                      ALU.mult, ALU.add,
                      inc=("sV", 1) if (j == 2 and m == 7) else None)
        if do_silu:
            B_.wait(SC, "sV")
            B_.act(b["pfm"][:], b["pfm"][:], AF.Silu, inc=("sS", 1))

    def load_window(blk):
        base = 125 + BLK * blk
        B_.barrier("sync")
        B_.dma("sync", b["lnph"][:].rearrange("p (c f) -> p c f", c=8),
               sp["lnh"][:, :, base:base + WIN].transpose([1, 0, 2]))
        B_.dma("sync", b["lnpl"][:].rearrange("p (c f) -> p c f", c=8),
               sp["lnl"][:, :, base:base + WIN].transpose([1, 0, 2]))

    def spill_fm(spname, blk, src):
        for hh in range(2):
            cc = blk * 2 + hh
            B_.dma("sync", sp[spname][cc].rearrange("p (c f) -> p c f", c=8),
                   src.rearrange("p (c f) -> p c f", c=8)[:, :, hh * 128:(hh + 1) * 128])

    if cut(2):
        return
    for blk in range(NBLK):
        load_window(blk)
        B_.wait("tensor", "sD")
        project("WAh", "WAl", 0, WIN, b["praw"], WIN)
        conv_silu(0)
        B_.wait("sync", "sS")
        spill_fm("q", blk, b["pfm"])
        project("WBh", "WBl", 0, WIN, b["praw"], WIN)
        conv_silu(1)
        B_.wait("sync", "sS")
        spill_fm("k", blk, b["pfm"])
        B_.barrier("tensor")
        po = pY[0:16, 0:BLK]
        for kk in range(8):
            B_.mm(po, b["c_wbh"][:, kk * H:(kk + 1) * H],
                  b["lnph"][:, kk * WIN + 3:kk * WIN + 3 + BLK],
                  start=(kk == 0), stop=False)
            B_.mm(po, b["c_wbl"][:, kk * H:(kk + 1) * H],
                  b["lnph"][:, kk * WIN + 3:kk * WIN + 3 + BLK],
                  start=False, stop=False)
            B_.mm(po, b["c_wbh"][:, kk * H:(kk + 1) * H],
                  b["lnpl"][:, kk * WIN + 3:kk * WIN + 3 + BLK],
                  start=False, stop=(kk == 7),
                  inc=("sT", 1) if kk == 7 else None)
        B_.wait(SC, "sT")
        B_.act(b["beta"][:, blk * BLK:(blk + 1) * BLK], po, AF.Sigmoid,
               inc=("sS", 1))

    if cut(3):
        return
    load_wpair("wv", "WCh", "WCl")
    load_wpair("wg", "WDh", "WDl")
    for blk in range(NBLK):
        load_window(blk)
        B_.wait("tensor", "sD")
        if _PB < 2:
            continue
        project("WCh", "WCl", 0, WIN, b["praw"], WIN)
        conv_silu(2)
        if _PB < 3:
            continue
        for hh in range(2):
            cc = blk * 2 + hh
            B_.barrier("tensor")
            for i_, h in enumerate([x for x in range(H) if x % 2 == 0] + [x for x in range(H) if x % 2 == 1]):
                B_.tp(pX[:, h * d:(h + 1) * d],
                      b["pfm"][half(h):half(h) + 64,
                               (h >> 1) * BLK + hh * 128:(h >> 1) * BLK + (hh + 1) * 128],
                      b["c_idpair"][half(h):half(h) + 64, 0:64],
                      inc=("sT", 1) if i_ in (7, H - 1) else None,
                      tpos=(half(h), 0))
                if i_ == 7:
                    B_.wait("tensor", "sT")
            B_.wait(V, "sT")
            B_.op(V, "tensor_copy", b["vstage"][:], pX[:, :H * d],
                  inc=("sV", 1))
            B_.wait("sync", "sV")
            B_.dma("sync", sp["vtm"][cc], b["vstage"][:])
        if _PB < 4:
            continue
        project("WDh", "WDl", 3, BLK, b["pfm"], BLK)
        B_.wait("sync", "sV")
        spill_fm("gate", blk, b["pfm"])

    if cut(4):
        return
    B_.barrier(G)
    for name, buf in (("m_trilS", "c_trilS"), ("m_triuS", "c_triuS"),
                      ("m_triuI", "c_triuI")):
        B_.dma(G, b[buf][:], io[name][:])
    sc_bcol = b["scl"][:, 0:16]
    sc_ssk = b["scl"][:, 16:32]
    sc_ssq = b["scl"][:, 32:48]
    sc_tmp = b["scl"][:, 48:64]
    sc_bsc = b["scl"][:, 64:80]
    sc_bb = b["scl"][:, 80:96]
    sc_bb2 = b["scl"][:, 96:112]
    sc_w2 = b["scl"][:, 112:128]
    B_.barrier(V)
    B_.op(V, "tensor_copy", b["mrun"][:], b["c_idpair"][:])
    B_.op(V, "memset", b["brun"][:], 0.0, inc=("sV", 1))

    for cc in range(NCH):
        B_.barrier("sync")
        if cc == 0:
            B_.dma("sync", b["qfm"][:], sp["q"][cc])
            B_.dma("sync", b["kfm"][:], sp["k"][cc])
        B_.barrier("tensor")
        for i_, h in enumerate([x for x in range(H) if x % 2 == 0] + [x for x in range(H) if x % 2 == 1]):
            B_.tp(pX[:, h * d:(h + 1) * d],
                  b["kfm"][half(h):half(h) + 64, (h >> 1) * C:(h >> 1) * C + C],
                  b["c_idpair"][half(h):half(h) + 64, 0:64], tpos=(half(h), 0),
                  inc=("sT", 1) if i_ in (7, H - 1) else None)
            if i_ == 7:
                B_.wait("tensor", "sT")
        for i_, h in enumerate([x for x in range(H) if x % 2 == 0] + [x for x in range(H) if x % 2 == 1]):
            B_.tp(pY[:, h * d:(h + 1) * d],
                  b["qfm"][half(h):half(h) + 64, (h >> 1) * C:(h >> 1) * C + C],
                  b["c_idpair"][half(h):half(h) + 64, 0:64],
                  inc=("sT", 1) if i_ in (7, H - 1) else None, tpos=(half(h), 0))
            if i_ == 7:
                B_.wait("tensor", "sT")
        B_.wait(V, "sT")
        B_.op(V, "tensor_copy", b["ktm"][:], pX[:, :H * d])
        B_.op(V, "tensor_copy", b["qtm"][:], pY[:, :H * d], inc=("sV", 1))
        B_.barrier("tensor", ("sV",))
        B_.tp(pX[:, 0:16], b["beta"][:, cc * C:(cc + 1) * C],
              b["c_idpair"][0:16, 0:16], inc=("sT", 1))
        B_.wait(V, "sT")
        B_.op(V, "tensor_copy", sc_bcol, pX[:, 0:16], inc=("sV", 1))
        B_.wait(SC, "sV")
        B_.act(b["sqs"][:], b["ktm"][:], AF.Square, inc=("sS", 1))
        B_.wait(V, "sS")
        B_.op(V, "reduce_sum", sc_ssk.unsqueeze(2),
              b["sqs"][:].rearrange("p (h j) -> p h j", h=H), axis=AX.X,
              inc=("sV", 1))
        B_.wait(V, "sV")
        B_.wait(SC, "sV")
        B_.act(b["sqs"][:], b["qtm"][:], AF.Square, inc=("sS", 1))
        B_.wait(V, "sS")
        B_.op(V, "reduce_sum", sc_ssq.unsqueeze(2),
              b["sqs"][:].rearrange("p (h j) -> p h j", h=H), axis=AX.X,
              inc=("sF", 1))
        B_.wait(V, "sF")
        B_.op(V, "tensor_scalar_add", sc_tmp, sc_ssk, 1e-6)
        rsqrt(sc_tmp, sc_bsc, sc_bb, sc_w2)
        B_.op(V, "tensor_scalar", sc_tmp, sc_ssq, 64.0, 64e-6, ALU.mult,
              ALU.add)
        rsqrt(sc_tmp, b["asc"][:, cc * H:(cc + 1) * H], sc_bb, sc_w2)
        B_.wait("sync", "sV")
        B_.wait("sync", "sS")
        B_.dma("sync", b["vtm"][:], sp["vtm"][cc])
        B_.op(V, "tensor_mul", sc_bb, sc_bcol, sc_bsc)
        B_.op(V, "tensor_mul", sc_bb2, sc_bb, sc_bsc)
        for h in range(H):
            B_.op(V, "tensor_scalar_mul", b["kdd"][:, h * d:(h + 1) * d],
                  b["ktm"][:, h * d:(h + 1) * d], sc_bb2[:, h:h + 1])
        B_.wait(V, "sD")
        for h in range(H):
            B_.op(V, "tensor_scalar_mul", b["rb"][:, h * C:h * C + 64],
                  b["vtm"][:, h * d:(h + 1) * d], sc_bb[:, h:h + 1])
        B_.op(V, "tensor_copy",
              b["rb"][:].rearrange("p (h t) -> p h t", h=H)[:, :, 64:128],
              b["kdd"][:].rearrange("p (h j) -> p h j", h=H), inc=("sV", 1))
        B_.barrier("tensor", ("sV",))
        for h in range(H):
            B_.mm(pX[half(h):half(h) + 64, (h >> 1) * C:(h >> 1) * C + C],
                  b["kdd"][:, h * d:(h + 1) * d], id128,
                  inc=("sT", 1) if h == H - 1 else None,
                  tpos=(0, half(h)))
        B_.wait(V, "sT")
        B_.op(V, "tensor_copy", b["kddf"][:], pX[:, :8 * C], inc=("sV", 1))
        B_.barrier("tensor", ("sV",))
        for i_, h in enumerate([x for x in range(H) if x % 2 == 0] + [x for x in range(H) if x % 2 == 1]):
            B_.mm(pX[:, h * C:(h + 1) * C],
                  b["kddf"][half(h):half(h) + 64, (h >> 1) * C:(h >> 1) * C + C],
                  b["kfm"][half(h):half(h) + 64, (h >> 1) * C:(h >> 1) * C + C],
                  tpos=(half(h), 0),
                  inc=("sT", 1) if i_ in (7, H - 1) else None)
            if i_ == 7:
                B_.wait("tensor", "sT")
        for i_, h in enumerate([x for x in range(H) if x % 2 == 0] + [x for x in range(H) if x % 2 == 1]):
            B_.mm(pY[:, h * C:(h + 1) * C],
                  b["kfm"][half(h):half(h) + 64, (h >> 1) * C:(h >> 1) * C + C],
                  b["kddf"][half(h):half(h) + 64, (h >> 1) * C:(h >> 1) * C + C],
                  inc=("sT", 1) if i_ in (7, H - 1) else None,
                  tpos=(half(h), 0))
            if i_ == 7:
                B_.wait("tensor", "sT")
        B_.wait(V, "sT")
        B_.op(V, "tensor_mul", b["Xb"][:], pX[:, :], b["c_trilS"][:])
        B_.op(V, "tensor_mul", b["Xtb"][:], pY[:, :], b["c_triuS"][:],
              inc=("sV", 1))
        B_.barrier("tensor", ("sV",))
        for h in range(H):
            B_.mm(pX[:, h * C:(h + 1) * C], b["Xtb"][:, h * C:(h + 1) * C],
                  b["rb"][:, h * C:(h + 1) * C],
                  inc=("sT", 1) if h == H - 1 else None)
        B_.wait(V, "sT")
        B_.op(V, "tensor_sub", b["rb"][:], b["rb"][:], pX[:, :], inc=("sV", 1))
        cur, curT, nxt, nxtT = "Xb", "Xtb", "Pb", "Ptb"
        for lvl in range(6):
            B_.barrier("tensor", ("sV",))
            if lvl < 5:
                for h in range(H):
                    B_.mm(pX[:, h * C:(h + 1) * C],
                          b[curT][:, h * C:(h + 1) * C],
                          b[cur][:, h * C:(h + 1) * C])
            for h in range(H):
                B_.mm(pY[:, h * C:(h + 1) * C], b[cur][:, h * C:(h + 1) * C],
                      b[curT][:, h * C:(h + 1) * C],
                      inc=("sT", 1) if h == H - 1 else None)
            B_.wait(V, "sT")
            if lvl < 5:
                B_.op(V, "tensor_copy", b[nxt][:], pX[:, :])
            B_.op(V, "tensor_copy", b[nxtT][:], pY[:, :], inc=("sV", 1))
            B_.barrier("tensor", ("sV",))
            for h in range(H):
                B_.mm(pX[:, h * C:(h + 1) * C], b[nxtT][:, h * C:(h + 1) * C],
                      b["rb"][:, h * C:(h + 1) * C],
                      inc=("sT", 1) if h == H - 1 else None)
            B_.wait(V, "sT")
            B_.op(V, "tensor_add", b["rb"][:], b["rb"][:], pX[:, :],
                  inc=("sV", 1))
            cur, curT, nxt, nxtT = nxt, nxtT, cur, curT

        def rV(h):
            return b["rb"][:, h * C:h * C + 64]

        def rK(h):
            return b["rb"][:, h * C + 64:h * C + 128]

        B_.barrier("tensor", ("sV",))
        for i_, h in enumerate([x for x in range(H) if x % 2 == 0] + [x for x in range(H) if x % 2 == 1]):
            B_.mm(pX[:, h * C:(h + 1) * C],
                  b["kfm"][half(h):half(h) + 64, (h >> 1) * C:(h >> 1) * C + C],
                  b["qfm"][half(h):half(h) + 64, (h >> 1) * C:(h >> 1) * C + C],
                  inc=("sT", 1) if i_ in (7, H - 1) else None,
                  tpos=(half(h), 0))
            if i_ == 7:
                B_.wait("tensor", "sT")
        B_.wait(V, "sT")
        B_.op(V, "tensor_mul", b["Gtb"][:], pX[:, :], b["c_triuI"][:],
              inc=("sV", 1))
        B_.barrier("tensor", ("sV",))
        for h in range(H):
            B_.mm(pY[:, h * d:(h + 1) * d], b["Gtb"][:, h * C:(h + 1) * C],
                  rV(h))
        for h in range(H):
            B_.mm(pX[half(h):half(h) + 64, (h >> 1) * C:(h >> 1) * C + C],
                  rK(h), b["Gtb"][:, h * C:(h + 1) * C],
                  inc=("sT", 1) if h == H - 1 else None,
                  tpos=(0, half(h)))
        B_.wait(V, "sT")
        B_.op(V, "tensor_copy", b["stg"][:], pY[:, :H * d])
        B_.op(V, "tensor_sub", b["stg2"][:], b["qfm"][:], pX[:, :8 * C],
              inc=("sV", 1))
        B_.wait("sync", "sV")
        B_.dma("sync", sp["gg"][cc], b["stg"][:])
        B_.dma("sync", sp["zt"][cc], b["stg2"][:])
        if cc + 1 < NCH:
            B_.dma("sync", b["qfm"][:], sp["q"][cc + 1])
            B_.dma("sync", b["kfm"][:], sp["k"][cc + 1])
        B_.barrier("tensor")
        for h in range(H):
            B_.mm(pX[half(h):half(h) + 64, (h >> 1) * d:(h >> 1) * d + d],
                  rK(h), b["ktm"][:, h * d:(h + 1) * d], tpos=(0, half(h)))
        for h in range(H):
            B_.mm(pY[half(h):half(h) + 64, (h >> 1) * d:(h >> 1) * d + d],
                  rV(h), b["ktm"][:, h * d:(h + 1) * d],
                  inc=("sT", 1) if h == H - 1 else None, tpos=(0, half(h)))
        B_.wait(V, "sT")
        B_.op(V, "tensor_sub", b["stg"][:, :8 * d], b["c_idpair"][:],
              pX[:, :8 * d])
        B_.op(V, "tensor_copy", b["stg2"][:, :8 * d], pY[:, :8 * d],
              inc=("sV", 1))
        B_.wait("sync", "sV")
        B_.dma("sync", sp["mt"][cc], b["stg"][:, :8 * d])
        B_.dma("sync", sp["bt"][cc], b["stg2"][:, :8 * d])
        B_.barrier("tensor", ("sV",))
        for i_, h in enumerate([x for x in range(H) if x % 2 == 0] + [x for x in range(H) if x % 2 == 1]):
            hb, hp = half(h), (h >> 1) * d
            B_.mm(pX[hb:hb + 64, hp:hp + d], b["stg"][hb:hb + 64, hp:hp + d],
                  b["mrun"][hb:hb + 64, hp:hp + d], tpos=(hb, hb))
            B_.mm(pY[hb:hb + 64, hp:hp + d], b["stg"][hb:hb + 64, hp:hp + d],
                  b["brun"][hb:hb + 64, hp:hp + d], start=True, stop=False,
                  tpos=(hb, hb))
            B_.mm(pY[hb:hb + 64, hp:hp + d], b["stg2"][hb:hb + 64, hp:hp + d],
                  b["c_idpair"][hb:hb + 64, hp:hp + d], start=False, stop=True,
                  inc=("sT", 1) if i_ in (7, H - 1) else None, tpos=(hb, hb))
            if i_ == 7:
                B_.wait("tensor", "sT")
        B_.wait(V, "sT")
        B_.op(V, "tensor_copy", b["mrun"][:], pX[:, :8 * d])
        B_.op(V, "tensor_copy", b["brun"][:], pY[:, :8 * d], inc=("sV", 1))

    if cut(5):
        return
    B_.wait("sync", "sV")
    B_.dma("sync", sp["cc_in"][0:1, 0:65536]
           .rearrange("a (p f) -> (a p) f", p=128), b["mrun"][:])
    B_.dma("sync", sp["cc_in"][0:1, 65536:131072]
           .rearrange("a (p f) -> (a p) f", p=128), b["brun"][:])
    B_.wait(G, "sD")
    B_._emit(G, lambda e: e.collective_compute(
        "AllGather", ALU.bypass, replica_groups=[[0, 1, 2, 3], [4, 5, 6, 7]],
        ins=[sp["cc_in"][:].opt()], outs=[sp["cc_out"][:].opt()]),
        ("sC", 1))

    def wquant_stream(src_ap_quarters, nquarters, sw_col, absacc=True):
        for qq in range(nquarters):
            B_.barrier("sync")
            B_.dma("sync", b["wstage"][:].rearrange("p (c f) -> p c f", c=8),
                   src_ap_quarters(qq))
            B_.wait(V, "sD")
            B_.op(V, "reduce_sum", b["rowm"][:, qq:qq + 1].unsqueeze(1),
                  b["wstage"][:].unsqueeze(1), axis=AX.X,
                  apply_absolute_value=True, inc=("sV", 1))

    if cut(6):
        return
    load_wpair("wo", "WCh", "WCl")

    B_.wait("sync", "sC", 1)
    B_.barrier(V)
    B_.op(V, "memset", b["Sb"][:], 0.0, inc=("sV", 1))
    for j in range(3):
        B_.barrier("sync", ("sV", "sT"))
        B_.dma("sync", b["Mg"][:], sp["cc_out"][j:j + 1, 0:65536]
               .rearrange("a (p f) -> (a p) f", p=128))
        B_.dma("sync", b["Bg"][:], sp["cc_out"][j:j + 1, 65536:131072]
               .rearrange("a (p f) -> (a p) f", p=128))
        B_.wait(V, "sD")
        B_.op(V, "tensor_scalar", b["Mg"][:], b["Mg"][:], b["c_pm"][:, j:j + 1],
              None, ALU.mult, ALU.bypass)
        B_.op(V, "scalar_tensor_tensor", b["Mg"][:], b["c_idpair"][:],
              b["c_pm1"][:, j:j + 1], b["Mg"][:], ALU.mult, ALU.add)
        B_.op(V, "tensor_scalar", b["Bg"][:], b["Bg"][:], b["c_pm"][:, j:j + 1],
              None, ALU.mult, ALU.bypass, inc=("sV", 1))
        B_.barrier("tensor", ("sV",))
        for i_, h in enumerate([x for x in range(H) if x % 2 == 0] + [x for x in range(H) if x % 2 == 1]):
            hb, hp = half(h), (h >> 1) * d
            off = 0 if hb == 0 else 512
            B_.mm(pX[hb:hb + 64, off + hp:off + hp + d],
                  b["Mg"][hb:hb + 64, hp:hp + d],
                  b["c_idpair"][hb:hb + 64, 0:64],
                  inc=("sT", 1) if i_ in (7, H - 1) else None, tpos=(hb, hb))
        B_.wait(V, "sT")
        B_.op(V, "tensor_copy", b["Mg"][0:64, :], pX[0:64, 0:512])
        B_.op(V, "tensor_copy", b["Mg"][64:128, :], pX[64:128, 512:1024],
              inc=("sV", 1))
        B_.barrier("tensor", ("sV",))
        for i_, h in enumerate([x for x in range(H) if x % 2 == 0] + [x for x in range(H) if x % 2 == 1]):
            hb, hp = half(h), (h >> 1) * d
            off = 0 if hb == 0 else 512
            B_.mm(pY[hb:hb + 64, off + hp:off + hp + d],
                  b["Mg"][hb:hb + 64, hp:hp + d],
                  b["Sb"][hb:hb + 64, hp:hp + d], start=True, stop=False,
                  tpos=(hb, hb))
            B_.mm(pY[hb:hb + 64, off + hp:off + hp + d],
                  b["c_idpair"][hb:hb + 64, hp:hp + d],
                  b["Bg"][hb:hb + 64, hp:hp + d], start=False, stop=True,
                  inc=("sT", 1) if i_ in (7, H - 1) else None, tpos=(hb, hb))
        B_.wait(V, "sT")
        B_.op(V, "tensor_copy", b["Sb"][0:64, :], pY[0:64, 0:512])
        B_.op(V, "tensor_copy", b["Sb"][64:128, :], pY[64:128, 512:1024],
              inc=("sV", 1))

    if cut(7):
        return
    B_.barrier("sync", ("sV", "sT"))
    B_.dma("sync", b["c_nwrep"][:], io["nwrep"][:])
    sc_rs = b["scl"][:, 0:16]
    sc_t3 = b["scl"][:, 16:32]
    sc_w3 = b["scl"][:, 32:48]
    sc_w4 = b["scl"][:, 48:64]
    for cc in range(int(os.environ.get("KCC", NCH))):
        ztb = b["Xb"]
        ggb = b["Xtb"]
        mtb = b["Pb"]
        gfb = b["Ptb"]
        B_.barrier("sync")
        B_.dma("sync", ztb[:, :8 * C], sp["zt"][cc])
        B_.dma("sync", ggb[:, :H * d], sp["gg"][cc])
        B_.dma("sync", mtb[:, 0:8 * d], sp["mt"][cc])
        B_.dma("sync", mtb[:, 8 * d:16 * d], sp["bt"][cc])
        B_.dma("sync", gfb[:, :8 * C], sp["gate"][cc])
        B_.wait("tensor", "sD")
        B_.barrier("tensor", ("sV", "sS"))
        for i_, h in enumerate([x for x in range(H) if x % 2 == 0] + [x for x in range(H) if x % 2 == 1]):
            hb = half(h)
            B_.mm(pX[:, h * d:(h + 1) * d],
                  ztb[hb:hb + 64, (h >> 1) * C:(h >> 1) * C + C],
                  b["Sb"][hb:hb + 64, (h >> 1) * d:(h >> 1) * d + d],
                  inc=("sT", 1) if i_ in (7, H - 1) else None, tpos=(hb, 0))
            if i_ == 7:
                B_.wait("tensor", "sT")
        for i_, h in enumerate([x for x in range(H) if x % 2 == 0] + [x for x in range(H) if x % 2 == 1]):
            hb, hp = half(h), (h >> 1) * d
            off = 0 if hb == 0 else 512
            B_.mm(pY[hb:hb + 64, off + hp:off + hp + d],
                  mtb[hb:hb + 64, hp:hp + d],
                  b["Sb"][hb:hb + 64, hp:hp + d], start=True, stop=False,
                  tpos=(hb, hb))
            B_.mm(pY[hb:hb + 64, off + hp:off + hp + d],
                  mtb[hb:hb + 64, 8 * d + hp:8 * d + hp + d],
                  b["c_idpair"][hb:hb + 64, 0:64], start=False, stop=True,
                  inc=("sT", 1) if i_ in (7, H - 1) else None, tpos=(hb, hb))
        B_.wait(V, "sT")
        B_.op(V, "tensor_add", b["rb"][:, :H * d], pX[:, :H * d], ggb[:, :H * d])
        B_.op(V, "tensor_copy", b["Sb"][0:64, :], pY[0:64, 0:512])
        B_.op(V, "tensor_copy", b["Sb"][64:128, :], pY[64:128, 512:1024],
              inc=("sV", 1))
        for h in range(H):
            B_.op(V, "tensor_scalar_mul", b["rb"][:, h * d:(h + 1) * d],
                  b["rb"][:, h * d:(h + 1) * d],
                  b["asc"][:, cc * H + h:cc * H + h + 1],
                  inc=("sV", 1) if h == H - 1 else None)
        B_.wait(SC, "sV")
        B_.act(b["sqs"][:], b["rb"][:, :H * d], AF.Square, inc=("sS", 1))
        B_.wait(V, "sS")
        B_.op(V, "reduce_sum", sc_t3.unsqueeze(2),
              b["sqs"][:].rearrange("p (h j) -> p h j", h=H), axis=AX.X,
              inc=("sF", 1))
        B_.wait(V, "sF")
        B_.op(V, "tensor_scalar", sc_t3, sc_t3, 1.0 / d, 1e-5, ALU.mult,
              ALU.add)
        rsqrt(sc_t3, sc_rs, sc_w3, sc_w4)
        for h in range(H):
            B_.op(V, "scalar_tensor_tensor", b["rb"][:, h * d:(h + 1) * d],
                  b["rb"][:, h * d:(h + 1) * d], sc_rs[:, h:h + 1],
                  b["c_nwrep"][:, h * d:(h + 1) * d], ALU.mult, ALU.mult,
                  inc=("sV", 1) if h == H - 1 else None)
        B_.barrier("tensor", ("sV",))
        for fc in range(8):
            B_.tp(pX[:, fc * 128:(fc + 1) * 128],
                  gfb[:, fc * 128:(fc + 1) * 128], id128,
                  inc=("sT", 1) if fc == 7 else None)
        B_.wait(SC, "sT")
        B_.act(b["qfm"][:, :1024], pX[:, :1024], AF.Silu, inc=("sS", 1))
        B_.wait(V, "sS")
        B_.op(V, "tensor_mul", b["kfm"][:, :1024], b["rb"][:, :H * d],
              b["qfm"][:, :1024])
        B_.op(V, "tensor_copy", b["hif"][:], b["kfm"][:, :1024])
        B_.op(V, "tensor_sub", b["lof"][:], b["kfm"][:, :1024], b["hif"][:],
              inc=("sV", 1))
        B_.barrier("tensor", ("sV",))
        for fc in range(8):
            B_.tp(pXb[:, fc * 128:(fc + 1) * 128],
                  b["hif"][:, fc * 128:(fc + 1) * 128], id128b)
        for fc in range(8):
            B_.tp(pYb[:, fc * 128:(fc + 1) * 128],
                  b["lof"][:, fc * 128:(fc + 1) * 128], id128b,
                  inc=("sT", 1) if fc == 7 else None)
        B_.wait(V, "sT")
        B_.op(V, "tensor_copy",
              b["oghi"][:].rearrange("p (c f) -> p c f", c=16)[:, 0:8, (cc % 4) * 128:(cc % 4 + 1) * 128],
              pXb[:, :1024].rearrange("p (c f) -> p c f", c=8))
        B_.op(V, "tensor_copy",
              b["oglo"][:].rearrange("p (c f) -> p c f", c=16)[:, 0:8, (cc % 4) * 128:(cc % 4 + 1) * 128],
              pYb[:, :1024].rearrange("p (c f) -> p c f", c=8), inc=("sV", 1))
        if cc % 4 == 3:
            sb4 = cc // 4
            for m in range(8):
                po = pX[:, (m % 4) * 512:(m % 4) * 512 + 512]
                B_.barrier("tensor", ("sV",))
                for kk in range(8):
                    B_.mm(po, b["WCh"][:, kk * 1024 + m * 128:kk * 1024 + (m + 1) * 128],
                          b["oghi"][:, kk * 512:(kk + 1) * 512],
                          start=(kk == 0), stop=False)
                    B_.mm(po, b["WCl"][:, kk * 1024 + m * 128:kk * 1024 + (m + 1) * 128],
                          b["oghi"][:, kk * 512:(kk + 1) * 512],
                          start=False, stop=False)
                    B_.mm(po, b["WCh"][:, kk * 1024 + m * 128:kk * 1024 + (m + 1) * 128],
                          b["oglo"][:, kk * 512:(kk + 1) * 512],
                          start=False, stop=(kk == 7),
                          inc=("sT", 1) if kk == 7 else None)
                B_.wait(V, "sT")
                B_.op(V, "tensor_copy", b["yfm"][:, m * 512:(m + 1) * 512], po,
                      inc=("sV", 1))
            for tt4 in range(4):
                gt = sb4 * 4 + tt4
                B_.barrier("tensor", ("sV", "sS"))
                for mc in range(8):
                    B_.tp(pY[:, mc * 128:(mc + 1) * 128],
                          b["yfm"][:, mc * 512 + tt4 * 128:mc * 512 + (tt4 + 1) * 128],
                          id128, inc=("sT", 1) if mc == 7 else None)
                B_.wait(V, "sT")
                B_.wait(V, "sD")
                B_.op(V, "tensor_copy", b["x1t"][:], pY[:, :1024], inc=("sV", 1))
                ln_core(b["x1t"][:], out_f32=b["lnt"][:])
                B_.barrier("sync", ("sV", "sS"))
                B_.dma("sync", b["xt"][:],
                       io["x_seg"][128 + gt * 128:128 + (gt + 1) * 128, :])
                B_.wait(V, "sD")
                B_.wait(V, "sS")
                B_.op(V, "tensor_add", b["x1t"][:], b["lnt"][:], b["xt"][:],
                      inc=("sV", 1))
                B_.wait("sync", "sV")
                B_.dma("sync", sp["x1"][gt], b["x1t"][:])
                if gt == 0:
                    B_.dma("sync", sp["cb_in"][0:1, 0:D], b["x1t"][0:1, :])
                if gt == NT - 1:
                    B_.dma("sync", sp["cb_in"][0:1, D:2 * D],
                           b["x1t"][127:128, :])

    B_.wait(G, "sD")
    B_._emit(G, lambda e: e.collective_compute(
        "AllGather", ALU.bypass, replica_groups=[[0, 1, 2, 3], [4, 5, 6, 7]],
        ins=[sp["cb_in"][:].opt()], outs=[sp["cb_out"][:].opt()]),
        ("sC", 1))

    if cut(8):
        return
    def w1q(qq):
        return io["fw1"][qq * 1024:(qq + 1) * 1024, :] \
            .rearrange("(c p) f -> p c f", p=128)

    wquant_stream(w1q, 4, 0)
    B_.barrier("tensor", ("sV",))
    B_.mm(pX[0:1, 0:4], b["c_ones"][:, 0:1], b["rowm"][:, 0:4], inc=("sT", 1))
    B_.wait(V, "sT")
    B_.op(V, "reduce_sum", b["ffs"][0:1, 0:1].unsqueeze(1),
          pX[0:1, 0:4].unsqueeze(1), axis=AX.X, inc=("sF", 1))
    B_.wait(V, "sF")
    B_.op(V, "tensor_scalar_mul", b["ffs"][0:1, 1:2], b["ffs"][0:1, 0:1],
          1.0 / (DH * D))
    B_.op(V, "tensor_scalar_max", b["ffs"][0:1, 1:2], b["ffs"][0:1, 1:2],
          1e-5)
    B_.op(V, "reciprocal", b["ffs"][0:1, 2:3], b["ffs"][0:1, 1:2],
          inc=("sV", 1))
    B_.barrier("tensor", ("sV",))
    B_.mm(pX[:, 0:1], b["onesrow"][:], b["ffs"][0:1, 2:3], inc=("sT", 1))
    B_.mm(pX[:, 1:2], b["onesrow"][:], b["ffs"][0:1, 1:2], inc=("sT", 1))
    B_.wait(V, "sT")
    B_.op(V, "tensor_copy", b["ffsb"][:, 0:2], pX[:, 0:2], inc=("sV", 1))

    W1T = [("WAh", 0), ("WAh", 1), ("WAl", 0), ("WAl", 1),
           ("WBh", 0), ("WBh", 1), ("WBl", 0), ("WBl", 1)]

    for qq in range(4):
        B_.barrier("sync")
        B_.dma("sync", b["wstage"][:].rearrange("p (c f) -> p c f", c=8),
               w1q(qq))
        B_.wait(V, "sD")
        for hh2 in range(4):
            sl = slice(hh2 * 2048, (hh2 + 1) * 2048)
            B_.op(V, "tensor_scalar", b["wstage"][:, sl], b["wstage"][:, sl],
                  b["ffsb"][:, 0:1], None, ALU.mult, ALU.bypass)
            B_.op(V, "tensor_scalar", b["wstage"][:, sl], b["wstage"][:, sl],
                  -1.0, 1.0, ALU.max, ALU.min)
            B_.op(V, "tensor_copy", b["wi32"][:], b["wstage"][:, sl])
            B_.op(V, "tensor_copy", b["wq16"][:, sl], b["wi32"][:],
                  inc=("sV", 1))
        for dc in range(8):
            buf, plane = W1T[dc]
            B_.barrier("tensor", ("sV",))
            for jc in range(8):
                B_.tp(pXb[:, jc * 128:(jc + 1) * 128],
                      b["wq16"][:, jc * 1024 + dc * 128:jc * 1024 + (dc + 1) * 128],
                      id128b, inc=("sT", 1) if jc == 7 else None)
            B_.wait(V, "sT")
            B_.op(V, "tensor_copy",
                  b[buf][:, plane * 4096 + qq * 1024:plane * 4096 + (qq + 1) * 1024],
                  pXb[:, 0:1024], inc=("sV", 1))

    def w1qT_sl(dc, jc):
        buf, plane = W1T[dc]
        return b[buf][:, plane * 4096 + jc * 128:plane * 4096 + (jc + 1) * 128]


    FB = 128
    B_.barrier(V)
    B_.op(V, "memset", b["hscr"][:], 1.0)
    B_.op(V, "memset", b["zscr"][:], 1.0, inc=("sV", 1))

    if cut(8):
        return
    B_.dma("sync", b["c_fcw"][:].rearrange("p (j t) -> p j t", j=32),
           io["fcw"][:].rearrange("(j p) t -> p j t", p=128))
    B_.dma("sync", b["c_fcb"][:].unsqueeze(2),
           io["fcb"][:].rearrange("(j p) o -> p j o", p=128))
    B_.wait(SC, "sD")
    for tt in range(NT):
        B_.barrier("sync")
        B_.dma("sync", b["x1t"][:], sp["x1"][tt])
        B_.wait(V, "sD")
        B_.wait(SC, "sD")
        ln_core(b["x1t"][:], out_f32=b["lnt"][:])
        B_.wait(V, "sS")
        B_.op(V, "reduce_max", b["rowm"][:, 0:1].unsqueeze(1),
              b["lnt"][:].unsqueeze(1), axis=AX.X, apply_absolute_value=True,
              inc=("sF", 1))
        B_.wait(V, "sF")
        B_.op(V, "tensor_scalar_max", b["rowm"][:, 1:2], b["rowm"][:, 0:1],
              1e-5)
        B_.op(V, "reciprocal", b["rowm"][:, 2:3], b["rowm"][:, 1:2],
              inc=("sF", 1))
        B_.wait(V, "sF")
        B_.op(V, "tensor_scalar", b["sqs"][:], b["lnt"][:],
              b["rowm"][:, 2:3], 127.0, ALU.mult, ALU.mult)
        B_.op(V, "tensor_copy", b["xi32"][:], b["sqs"][:])
        B_.op(V, "tensor_copy", b["xit"][:], b["xi32"][:])
        B_.op(V, "tensor_scalar", b["rowm"][:, 3:4], b["rowm"][:, 1:2],
              b["ffsb"][:, 1:2], 1.0 / 127.0, ALU.mult, ALU.mult,
              inc=("sV", 1))
        B_.barrier("tensor", ("sV",))
        for fc in range(8):
            B_.tp(pXb[:, fc * 128:(fc + 1) * 128],
                  b["xit"][:, fc * 128:(fc + 1) * 128], id128b)
        B_.mm(pY[0:1, 0:128], b["rowm"][:, 3:4], id128, inc=("sT", 1))
        B_.wait(V, "sT")
        B_.op(V, "tensor_copy", b["hqt"][:, 0:1024], pXb[:, 0:1024])
        B_.op(V, "tensor_copy", b["hscr"][0:1, 1 + tt * 128:1 + (tt + 1) * 128],
              pY[0:1, 0:128], inc=("sV", 1))
        B_.wait("sync", "sV")
        B_.dma("sync", sp["xi"][tt], b["hqt"][:, 0:1024])

    B_.wait("sync", "sC", 2)
    B_.barrier("sync", ("sV", "sS", "sT"))
    B_.dma("sync", b["cball"][:], sp["cb_out"][:])
    B_.wait("tensor", "sD")
    B_.mm(pX[0:1, 0:512], b["c_sel"][:, 0:1], b["cball"][:, D:D + 512])
    B_.mm(pX[0:1, 512:1024], b["c_sel"][:, 0:1], b["cball"][:, D + 512:2 * D])
    B_.mm(pX[0:1, 1024:1536], b["c_sel"][:, 1:2], b["cball"][:, 0:512])
    B_.mm(pX[0:1, 1536:2048], b["c_sel"][:, 1:2], b["cball"][:, 512:D],
          inc=("sT", 1))
    B_.wait(V, "sT")
    B_.op(V, "tensor_copy", b["brow"][:], pX[0:1, 0:2048], inc=("sV", 1))
    for s in range(2):
        xrow = b["brow"][0:1, s * D:(s + 1) * D]
        ln_core(xrow, out_f32=b["lnt"][0:1, :])
        B_.wait(V, "sS")
        B_.op(V, "reduce_max", b["rowm"][0:1, 4:5].unsqueeze(1),
              b["lnt"][0:1, :].unsqueeze(1), axis=AX.X,
              apply_absolute_value=True, inc=("sF", 1))
        B_.wait(V, "sF")
        B_.op(V, "tensor_scalar_max", b["rowm"][0:1, 4:5], b["rowm"][0:1, 4:5],
              1e-5)
        B_.op(V, "reciprocal", b["rowm"][0:1, 5:6], b["rowm"][0:1, 4:5],
              inc=("sF", 1))
        B_.wait(V, "sF")
        B_.op(V, "tensor_scalar", b["sqs"][0:1, :], b["lnt"][0:1, :],
              b["rowm"][0:1, 5:6], 127.0, ALU.mult, ALU.mult)
        B_.op(V, "tensor_copy", b["xi32"][0:1, :], b["sqs"][0:1, :])
        B_.op(V, "tensor_copy", b["xit"][0:1, :], b["xi32"][0:1, :])
        B_.op(V, "tensor_scalar", b["rowm"][0:1, 6:7], b["rowm"][0:1, 4:5],
              b["ffs"][0:1, 1:2], 1.0 / 127.0, ALU.mult, ALU.mult,
              inc=("sV", 1))
        B_.barrier("tensor", ("sV",))
        for fc in range(8):
            B_.mm(pY[:, fc:fc + 1], b["xit"][0:1, fc * 128:(fc + 1) * 128],
                  id128b[0:1, 0:1],
                  inc=("sT", 1) if fc == 7 else None)
        B_.mm(pY[:, 8:9], b["onesrow"][:], b["rowm"][0:1, 6:7], inc=("sT", 1))
        B_.wait(V, "sT")
        B_.op(V, "tensor_copy", b["xif_h"][:], pY[:, 0:8])
        B_.op(V, "tensor_copy", b["rowm"][:, 6:7], pY[:, 8:9], inc=("sV", 1))
        B_.barrier("tensor", ("sV",))
        for jc in range(32):
            for dc in range(8):
                B_.mm(pX[:, jc:jc + 1], w1qT_sl(dc, jc),
                      b["xif_h"][:, dc:dc + 1], start=(dc == 0),
                      stop=(dc == 7),
                      inc=("sT", 1) if (jc == 31 and dc == 7) else None)
        B_.wait(V, "sT")
        B_.op(V, "tensor_scalar", b["hhalo"][:, s * 32:(s + 1) * 32],
              pX[:, 0:32], b["rowm"][:, 6:7], b["c_hm"][:, s:s + 1],
              ALU.mult, ALU.mult, inc=("sV", 1))

    if cut(9):
        return
    for fb in range(int(os.environ.get("KFB", NT))):
        B_.barrier("tensor", ("sV", "sS"))
        B_.mm(pX[:, 0:FB + 2], b["onesrow"][:],
              b["hscr"][0:1, fb * FB:fb * FB + FB + 2], inc=("sT", 1))
        B_.wait(V, "sT")
        B_.op(V, "tensor_copy", b["hscp"][:], pX[:, 0:FB + 2], inc=("sV", 1))
        B_.barrier("sync", ("sV",))
        if fb == 0:
            B_.dma("sync", b["xib"][:, 0:1024], sp["xi"][0])
        if fb + 1 < NT:
            B_.dma("sync", b["xib"][:, ((fb + 1) % 3) * 1024:
                                     ((fb + 1) % 3) * 1024 + 1024],
                   sp["xi"][fb + 1])
        B_.wait(V, "sD")
        xiwr = b["xiw"][:].rearrange("p (c f) -> p c f", c=8)
        cur = b["xib"][:, (fb % 3) * 1024:(fb % 3) * 1024 + 1024] \
            .rearrange("p (c f) -> p c f", c=8)
        B_.op(V, "tensor_copy", xiwr[:, :, 1:FB + 1], cur)
        if fb == 0:
            B_.op(V, "memset", xiwr[:, :, 0:1], 0.0)
        else:
            prev = b["xib"][:, ((fb - 1) % 3) * 1024:
                            ((fb - 1) % 3) * 1024 + 1024] \
                .rearrange("p (c f) -> p c f", c=8)
            B_.op(V, "tensor_copy", xiwr[:, :, 0:1], prev[:, :, 127:128])
        if fb == NT - 1:
            B_.op(V, "memset", xiwr[:, :, FB + 1:FB + 2], 0.0, inc=("sV", 1))
        else:
            nxt = b["xib"][:, ((fb + 1) % 3) * 1024:
                           ((fb + 1) % 3) * 1024 + 1024] \
                .rearrange("p (c f) -> p c f", c=8)
            B_.op(V, "tensor_copy", xiwr[:, :, FB + 1:FB + 2],
                  nxt[:, :, 0:1], inc=("sV", 1))
        B_.barrier("tensor", ("sV",))
        for grp in range(4):
            B_.barrier("tensor", ("sV",))
            for j2 in range(8):
                jc = grp * 8 + j2
                po = (pX if j2 < 4 else pY)[:, (j2 % 4) * 512:(j2 % 4) * 512 + FB + 2]
                for dc in range(8):
                    B_.mm(po, w1qT_sl(dc, jc),
                          b["xiw"][:, dc * (FB + 2):(dc + 1) * (FB + 2)],
                          start=(dc == 0), stop=(dc == 7),
                          inc=("sT", 1) if (j2 == 7 and dc == 7) else None)
            B_.wait(V, "sT")
            for j2 in range(8):
                jc = grp * 8 + j2
                po = (pX if j2 < 4 else pY)[:, (j2 % 4) * 512:(j2 % 4) * 512 + FB + 2]
                B_.op(V, "tensor_mul",
                      b["hraw"][:, jc * (FB + 2):(jc + 1) * (FB + 2)],
                      po, b["hscp"][:],
                      inc=("sV", 1) if j2 == 7 else None)
        if fb == 0:
            B_.op(V, "tensor_copy",
                  b["hraw"][:].rearrange("p (j f) -> p j f", j=32)[:, :, 0:1],
                  b["hhalo"][:, 0:32].unsqueeze(2), inc=("sV", 1))
        if fb == NT - 1:
            B_.op(V, "tensor_copy",
                  b["hraw"][:].rearrange("p (j f) -> p j f", j=32)
                  [:, :, FB + 1:FB + 2],
                  b["hhalo"][:, 32:64].unsqueeze(2), inc=("sV", 1))
        B_.barrier(SC, ("sV",))
        for jc in range(32):
            B_.act(b["hcs"][:, jc * FB:(jc + 1) * FB],
                   b["hraw"][:, jc * (FB + 2) + 1:jc * (FB + 2) + 1 + FB],
                   AF.Identity, scale=b["c_fcw"][:, jc * 3 + 1:jc * 3 + 2],
                   bias=b["c_fcb"][:, jc:jc + 1],
                   inc=("sS", 1) if jc == 31 else None)
        B_.wait(V, "sS")
        for j in (0, 2):
            for jc in range(32):
                B_.op(V, "scalar_tensor_tensor",
                      b["hcs"][:, jc * FB:(jc + 1) * FB],
                      b["hraw"][:, jc * (FB + 2) + j:jc * (FB + 2) + j + FB],
                      b["c_fcw"][:, jc * 3 + j:jc * 3 + j + 1],
                      b["hcs"][:, jc * FB:(jc + 1) * FB],
                      ALU.mult, ALU.add,
                      inc=("sV", 1) if (j == 2 and jc == 31) else None)
        B_.wait(SC, "sV")
        B_.act(b["hcs"][:], b["hcs"][:], AF.Silu, inc=("sS", 1))
        for rnd in range(2):
            B_.barrier("tensor", ("sV", "sS"))
            for jj in range(16):
                jc = rnd * 16 + jj
                po = pX if jj < 8 else pY
                B_.tp(po[:, (jj % 8) * 128:(jj % 8 + 1) * 128],
                      b["hcs"][:, jc * FB:(jc + 1) * FB], id128,
                      inc=("sT", 1) if jj == 15 else None)
            B_.wait(V, "sT")
            B_.op(V, "tensor_copy",
                  b["htt"][:, rnd * 2048:rnd * 2048 + 1024], pX[:, 0:1024])
            B_.op(V, "tensor_copy",
                  b["htt"][:, rnd * 2048 + 1024:rnd * 2048 + 2048],
                  pY[:, 0:1024], inc=("sV", 1))
        B_.op(V, "reduce_max", b["rowm"][:, 4:5].unsqueeze(1),
              b["htt"][:].unsqueeze(1), axis=AX.X, apply_absolute_value=True,
              inc=("sF", 1))
        B_.wait(V, "sF")
        B_.op(V, "tensor_scalar_max", b["rowm"][:, 5:6], b["rowm"][:, 4:5],
              1e-5)
        B_.op(V, "reciprocal", b["rowm"][:, 6:7], b["rowm"][:, 5:6],
              inc=("sF", 1))
        B_.wait(V, "sF")
        B_.op(V, "tensor_scalar", b["htt"][:], b["htt"][:],
              b["rowm"][:, 6:7], 127.0, ALU.mult, ALU.mult)
        for g in range(2):
            B_.op(V, "tensor_copy", b["hi32"][:],
                  b["htt"][:, g * 2048:(g + 1) * 2048])
            B_.op(V, "tensor_copy", b["hqt"][:, g * 2048:(g + 1) * 2048],
                  b["hi32"][:])
        B_.op(V, "tensor_scalar_mul", b["rowm"][:, 7:8], b["rowm"][:, 5:6],
              1.0 / 127.0, inc=("sV", 1))
        B_.barrier("tensor", ("sV",))
        B_.mm(pX[0:1, 1536:1664], b["rowm"][:, 7:8], id128, inc=("sT", 1))
        B_.wait(V, "sT")
        B_.op(V, "tensor_copy",
              b["zscr"][0:1, fb * FB:(fb + 1) * FB], pX[0:1, 1536:1664],
              inc=("sV", 1))
        for rnd in range(2):
            B_.barrier("tensor", ("sV",))
            for jj in range(16):
                jc = rnd * 16 + jj
                pob = pXb if jj < 8 else pYb
                B_.tp(pob[:, (jj % 8) * 128:(jj % 8 + 1) * 128],
                      b["hqt"][:, jc * 128:(jc + 1) * 128], id128b,
                      inc=("sT", 1) if jj == 15 else None)
            B_.wait(V, "sT")
            for g in range(2):
                pob = pXb if g == 0 else pYb
                B_.op(V, "tensor_copy",
                      b["hqf"][:].rearrange("p (j f) -> p j f", j=32)
                      [:, rnd * 16 + g * 8:rnd * 16 + (g + 1) * 8, :],
                      pob[:, 0:1024].rearrange("p (j f) -> p j f", j=8),
                      inc=("sV", 1) if g == 1 else None)
        B_.wait("sync", "sV")
        B_.dma("sync", sp["hqf"][fb], b["hqf"][:])

    def w2chunk(mc):
        return io["fw2"][mc * 128:(mc + 1) * 128, :]

    for mc in range(8):
        B_.barrier("sync")
        B_.dma("sync", b["wstage"][:, :DH], w2chunk(mc))
        B_.wait(V, "sD")
        B_.op(V, "reduce_sum", b["rowm"][:, mc:mc + 1].unsqueeze(1),
              b["wstage"][:, :DH].unsqueeze(1), axis=AX.X,
              apply_absolute_value=True, inc=("sV", 1))
    B_.barrier("tensor", ("sV",))
    B_.mm(pX[0:1, 0:8], b["c_ones"][:, 0:1], b["rowm"][:, 0:8], inc=("sT", 1))
    B_.wait(V, "sT")
    B_.op(V, "reduce_sum", b["ffs"][0:1, 3:4].unsqueeze(1),
          pX[0:1, 0:8].unsqueeze(1), axis=AX.X, inc=("sF", 1))
    B_.wait(V, "sF")
    B_.op(V, "tensor_scalar_mul", b["ffs"][0:1, 4:5], b["ffs"][0:1, 3:4],
          1.0 / (DH * D))
    B_.op(V, "tensor_scalar_max", b["ffs"][0:1, 4:5], b["ffs"][0:1, 4:5], 1e-5)
    B_.op(V, "reciprocal", b["ffs"][0:1, 5:6], b["ffs"][0:1, 4:5],
          inc=("sV", 1))
    B_.barrier("tensor", ("sV",))
    B_.mm(pX[:, 0:1], b["onesrow"][:], b["ffs"][0:1, 5:6])
    B_.mm(pX[:, 1:2], b["onesrow"][:], b["ffs"][0:1, 4:5], inc=("sT", 1))
    B_.wait(V, "sT")
    B_.op(V, "tensor_copy", b["ffsb"][:, 2:4], pX[:, 0:2], inc=("sV", 1))

    W2T = [("WCh", 0), ("WCh", 1), ("WCl", 0), ("WCl", 1),
           ("WDh", 0), ("WDh", 1), ("WDl", 0), ("WDl", 1)]

    def w2qT_sl(jc, mc):
        bufn = ("WCh", "WCl", "WDh", "WDl")[jc // 8]
        off = (jc % 8) * 1024 + mc * 128
        return b[bufn][:, off:off + 128]

    for mc in range(8):
        B_.barrier("sync")
        B_.dma("sync", b["wstage"][:, :DH], w2chunk(mc))
        B_.wait(V, "sD")
        for hh2 in range(2):
            sl = slice(hh2 * 2048, (hh2 + 1) * 2048)
            B_.op(V, "tensor_scalar", b["wstage"][:, sl], b["wstage"][:, sl],
                  b["ffsb"][:, 2:3], None, ALU.mult, ALU.bypass)
            B_.op(V, "tensor_scalar", b["wstage"][:, sl], b["wstage"][:, sl],
                  -1.0, 1.0, ALU.max, ALU.min)
            B_.op(V, "tensor_copy", b["wi32"][:], b["wstage"][:, sl])
            B_.op(V, "tensor_copy", b["hqt"][:, sl], b["wi32"][:],
                  inc=("sV", 1))
        for rnd in range(2):
            B_.barrier("tensor", ("sV",))
            for jj in range(16):
                jc = rnd * 16 + jj
                pob = pXb if jj < 8 else pYb
                B_.tp(pob[:, (jj % 8) * 128:(jj % 8 + 1) * 128],
                      b["hqt"][:, jc * 128:(jc + 1) * 128], id128b,
                      inc=("sT", 1) if jj == 15 else None)
            B_.wait(V, "sT")
            for g4 in range(2):
                pob = pXb if g4 == 0 else pYb
                base_jc = rnd * 16 + g4 * 8
                bufn = ("WCh", "WCl", "WDh", "WDl")[base_jc // 8]
                B_.op(V, "tensor_copy",
                      b[bufn][:].rearrange("p (j f) -> p j f", j=8)
                      [:, :, mc * 128:(mc + 1) * 128],
                      pob[:, 0:1024].rearrange("p (j f) -> p j f", j=8),
                      inc=("sV", 1) if g4 == 1 else None)

    if cut(10):
        return
    for fb in range(NT):
        B_.barrier("sync")
        B_.dma("sync", b["hqf"][:], sp["hqf"][fb])
        B_.barrier("tensor", ("sV", "sS"))
        B_.mm(pX[:, 0:FB], b["onesrow"][:],
              b["zscr"][0:1, fb * FB:(fb + 1) * FB], inc=("sT", 1))
        B_.wait(V, "sT")
        B_.op(V, "tensor_scalar", b["zscp"][:], pX[:, 0:FB],
              b["ffsb"][:, 3:4], None, ALU.mult, ALU.bypass, inc=("sV", 1))
        if _PS < 2:
            continue
        B_.barrier("tensor", ("sV", "sD"))
        for mc in range(8):
            po = (pX if mc < 4 else pY)[:, (mc % 4) * 512:(mc % 4) * 512 + FB]
            for jc in range(32):
                B_.mm(po, w2qT_sl(jc, mc), b["hqf"][:, jc * FB:(jc + 1) * FB],
                      start=(jc == 0), stop=(jc == 31),
                      inc=("sT", 1) if (mc == 7 and jc == 31) else None)
        B_.wait(V, "sT")
        for mc in range(8):
            po = (pX if mc < 4 else pY)[:, (mc % 4) * 512:(mc % 4) * 512 + FB]
            B_.op(V, "tensor_mul", b["zf"][:, mc * FB:(mc + 1) * FB], po,
                  b["zscp"][:], inc=("sV", 1) if mc == 7 else None)
        if _PS < 3:
            continue
        B_.barrier("tensor", ("sV", "sS"))
        for mc in range(8):
            B_.tp(pX[:, mc * 128:(mc + 1) * 128],
                  b["zf"][:, mc * FB:(mc + 1) * FB], id128,
                  inc=("sT", 1) if mc == 7 else None)
        B_.wait(V, "sT")
        B_.op(V, "tensor_copy", b["xt"][:], pX[:, 0:1024], inc=("sV", 1))
        ln_core(b["xt"][:], out_f32=b["lnt"][:])
        B_.barrier("sync", ("sV", "sS"))
        B_.dma("sync", b["x1t"][:], sp["x1"][fb])
        B_.wait(V, "sD")
        B_.wait(V, "sS")
        B_.op(V, "tensor_add", b["praw"][:, 0:1024], b["lnt"][:],
              b["x1t"][:], inc=("sV", 1))
        B_.wait("sync", "sV")
        B_.dma("sync", out[fb * 128:(fb + 1) * 128, :], b["praw"][:, 0:1024])

    _dbg_tap()

    B_.wait("sync", "sD")
    B_.wait("sync", "sG")


_NC_CACHE = {}


def _get_nc():
    if "nc" not in _NC_CACHE:
        _NC_CACHE["nc"] = build_nc()
    return _NC_CACHE["nc"]


def kernel(**inputs):
    x = np.asarray(inputs["L0_attn_x"], dtype=np.float32)
    f32 = np.float32

    def c_(a):
        return np.ascontiguousarray(np.asarray(a, dtype=f32))

    base = {
        "wq": c_(inputs["L0_attn_Wq"]), "wk": c_(inputs["L0_attn_Wk"]),
        "wv": c_(inputs["L0_attn_Wv"]), "wg": c_(inputs["L0_attn_Wg"]),
        "wo": c_(inputs["L0_attn_Wo"]), "wb": c_(inputs["L0_attn_Wb"]),
        "cqw": c_(inputs["L0_attn_cq_w"]),
        "cqb": c_(inputs["L0_attn_cq_b"]).reshape(D, 1),
        "ckw": c_(inputs["L0_attn_ck_w"]),
        "ckb": c_(inputs["L0_attn_ck_b"]).reshape(D, 1),
        "cvw": c_(inputs["L0_attn_cv_w"]),
        "cvb": c_(inputs["L0_attn_cv_b"]).reshape(D, 1),
        "nwrep": np.ascontiguousarray(
            np.tile(np.asarray(inputs["L0_attn_nw"], f32).reshape(1, d),
                    (128, H))),
        "fw1": c_(inputs["L0_ffn_w1"]),
        "fw2": c_(inputs["L0_ffn_w2"]),
        "fcw": c_(inputs["L0_ffn_conv_w"]).reshape(DH, 3),
        "fcb": c_(inputs["L0_ffn_conv_b"]).reshape(DH, 1),
        "ident128": np.eye(128, dtype=f32),
        "identpair": np.ascontiguousarray(
            np.tile(np.concatenate([np.eye(d, dtype=f32)] * 2, axis=0),
                    (1, 8))),
        "m_trilS": np.ascontiguousarray(
            np.tile(np.tril(np.ones((128, 128), f32), -1), (1, H))),
        "m_triuS": np.ascontiguousarray(
            np.tile(np.triu(np.ones((128, 128), f32), 1), (1, H))),
        "m_triuI": np.ascontiguousarray(
            np.tile(np.triu(np.ones((128, 128), f32), 0), (1, H))),
        "onescol": np.ones((128, 2), f32),
    }
    in_maps = []
    for core in range(8):
        bidx, pos = core // 4, core % 4
        seg0 = pos * R
        halo3 = (x[bidx, seg0 - 3:seg0] if pos > 0
                 else np.zeros((3, D), f32))
        x_seg = np.concatenate(
            [np.zeros((125, D), f32), halo3, x[bidx, seg0:seg0 + R]], axis=0)
        pm = np.zeros((128, 4), f32)
        pm[:, :3] = [1.0 if j < pos else 0.0 for j in range(3)]
        pm1 = np.zeros((128, 4), f32)
        pm1[:, :3] = 1.0 - pm[:, :3]
        hm = np.zeros((128, 2), f32)
        hm[:, 0] = 1.0 if pos > 0 else 0.0
        hm[:, 1] = 1.0 if pos < 3 else 0.0
        cs = np.zeros((4, 2), f32)
        if pos > 0:
            cs[pos - 1, 0] = 1.0
        if pos < 3:
            cs[pos + 1, 1] = 1.0
        m = dict(base)
        m["x_seg"] = np.ascontiguousarray(x_seg)
        m["pmask"] = pm
        m["pmask1"] = pm1
        m["hmask"] = hm
        m["csel"] = cs
        in_maps.append(m)

    nc = _get_nc()
    res = run_bass_kernel_spmd(nc, in_maps, core_ids=list(range(8)))
    outp = np.empty((B, T, D), dtype=np.float32)
    for core in range(8):
        bidx, pos = core // 4, core % 4
        outp[bidx, pos * R:(pos + 1) * R, :] = res.results[core]["out"]
    return outp


# version 09
# speedup vs baseline: 1.0084x; 1.0011x over previous
import os
import numpy as np
import ml_dtypes
import concourse.bass as bass
import concourse.mybir as mybir
from concourse.bass_utils import run_bass_kernel_spmd
from contextlib import ExitStack

F32 = mybir.dt.float32
BF16 = mybir.dt.bfloat16
I32 = mybir.dt.int32
AF = mybir.ActivationFunctionType
ALU = mybir.AluOpType
AX = mybir.AxisListType

B, T, D = 2, 8192, 1024
H, d = 16, 64
R = 2048
C = 128
NCH = R // C
NT = R // 128
BLK = 256
NBLK = R // BLK
DH = 4096
XROWS = 128 + R
WIN = 259


class Bld:
    def __init__(self, nc):
        self.nc = nc
        self.ops = {k: [] for k in ("sync", "gpsimd", "vector", "scalar", "tensor")}
        self.cnt = {k: 0 for k in ("sD", "sG", "sT", "sV", "sS", "sC", "sF")}
        self.sems = {}

    def _emit(self, eng, fn, inc=None):
        if inc is not None:
            semname, n = inc
            s = self.sems[semname]

            def g(e, fn=fn, s=s, n=n):
                fn(e).then_inc(s, n)
            self.ops[eng].append(g)
            self.cnt[semname] += n
        else:
            self.ops[eng].append(fn)

    def wait(self, eng, semname, val=None):
        s = self.sems[semname]
        v = self.cnt[semname] if val is None else val
        if v <= 0:
            return
        self.ops[eng].append(lambda e, s=s, v=v: e.wait_ge(s, v))

    def barrier(self, eng, sems=("sD", "sG", "sT", "sV", "sS")):
        for s in sems:
            self.wait(eng, s)

    def dma(self, q, out, in_):
        sn = "sD" if q == "sync" else "sG"
        self._emit(q, lambda e, o=out, i=in_: e.dma_start(out=o, in_=i, allow_slow_non_contiguous=True), (sn, 16))

    def mm(self, out, lhsT, rhs, start=True, stop=True, inc=None, tpos=None):
        self._emit("tensor",
                   lambda e, o=out, l=lhsT, r=rhs, s1=start, s2=stop, t=tpos:
                   e.matmul(o, l, r, start=s1, stop=s2, tile_position=t), inc)

    def tp(self, out, in_, ident, inc=None, tpos=None):
        self._emit("tensor",
                   lambda e, o=out, i=in_, dd=ident, t=tpos:
                   e.transpose(o, i, dd, tile_position=t), inc)

    def op(self, eng, meth, *args, inc=None, **kw):
        def f(e, m=meth, a=args, k=kw):
            return getattr(e, m)(*a, **k)
        fence = False
        if eng in ("vector", "gpsimd") and args:
            try:
                fence = args[0].free_size() < 64
            except Exception:
                fence = False
        if fence and inc is None:
            inc = ("sF", 1)
        self._emit(eng, f, inc)
        if fence:
            self.wait(eng, inc[0])

    def act(self, out, in_, func, scale=1.0, bias=0.0, accum_out=None, inc=None):
        kw = {}
        if accum_out is not None:
            kw["accum_out"] = accum_out
        self._emit("scalar",
                   lambda e, o=out, i=in_, f=func, s=scale, bb=bias, k=kw:
                   e.activation(out=o, in_=i, func=f, scale=s, bias=bb, **k), inc)


def bcl(ap, n):
    return ap.unsqueeze(ap.ndim).broadcast_to(list(ap.shape) + [n])


def build_nc():
    nc = bass.Bass("TRN2", target_bir_lowering=False, debug=False)

    def din(name, shape, dt=F32):
        return nc.dram_tensor(name, shape, dt, kind="ExternalInput").ap()

    io = {}
    io["x_seg"] = din("x_seg", [XROWS, D])
    for n in ("wq", "wk", "wv", "wg", "wo"):
        io[n] = din(n, [D, D])
    io["wb"] = din("wb", [D, H])
    for n in ("cqw", "ckw", "cvw"):
        io[n] = din(n, [D, 4])
    for n in ("cqb", "ckb", "cvb"):
        io[n] = din(n, [D, 1])
    io["nwrep"] = din("nwrep", [128, H * d])
    io["fw1"] = din("fw1", [DH, D])
    io["fw2"] = din("fw2", [D, DH])
    io["fcw"] = din("fcw", [DH, 3])
    io["fcb"] = din("fcb", [DH, 1])
    io["ident128"] = din("ident128", [128, 128])
    io["identpair"] = din("identpair", [128, 8 * d])
    io["m_trilS"] = din("m_trilS", [128, H * C])
    io["m_triuS"] = din("m_triuS", [128, H * C])
    io["m_triuI"] = din("m_triuI", [128, H * C])
    io["pmask"] = din("pmask", [128, 4])
    io["pmask1"] = din("pmask1", [128, 4])
    io["hmask"] = din("hmask", [128, 2])
    io["onescol"] = din("onescol", [128, 2])
    io["csel"] = din("csel", [4, 2])

    out = nc.dram_tensor("out", [R, D], F32, kind="ExternalOutput").ap()

    def scr(name, shape, dt=F32):
        return nc.dram_tensor(name, shape, dt).ap()

    sp = {}
    sp["lnh"] = scr("sp_lnh", [8, 128, XROWS], BF16)
    sp["lnl"] = scr("sp_lnl", [8, 128, XROWS], BF16)
    sp["q"] = scr("sp_q", [NCH, 128, 8 * C])
    sp["k"] = scr("sp_k", [NCH, 128, 8 * C])
    sp["vtm"] = scr("sp_vtm", [NCH, 128, H * d])
    sp["gate"] = scr("sp_gate", [NCH, 128, 8 * C])
    sp["zt"] = scr("sp_zt", [NCH, 128, 8 * C])
    sp["gg"] = scr("sp_gg", [NCH, 128, H * d])
    sp["mt"] = scr("sp_mt", [NCH, 128, 8 * d])
    sp["bt"] = scr("sp_bt", [NCH, 128, 8 * d])
    sp["x1"] = scr("sp_x1", [NT, 128, D])
    sp["xi"] = scr("sp_xi", [NT, 128, 1024], BF16)
    sp["hqf"] = scr("sp_hqf", [NT, 128, 32 * 128], BF16)
    sp["cc_in"] = scr("cc_in", [1, 2 * 128 * 8 * d])
    sp["cc_out"] = scr("cc_out", [4, 2 * 128 * 8 * d])
    sp["cb_in"] = scr("cb_in", [1, 2 * D])
    sp["cb_out"] = scr("cb_out", [4, 2 * D])

    bld = Bld(nc)

    with ExitStack() as ctx:
        def sb(name, shape, dt=F32):
            return ctx.enter_context(nc.sbuf_tensor(name, shape, dt))

        def psum(name, shape):
            return ctx.enter_context(nc.psum_tensor(name, shape, F32))

        for s in ("sD", "sG", "sT", "sV", "sS", "sC", "sF"):
            bld.sems[s] = ctx.enter_context(nc.semaphore(s))

        b = {}
        PF = sb("PF", [128, 28224])
        PB = sb("PB", [128, 43040], BF16)
        PI = sb("PI", [128, 2048], I32)

        def fs(a, n, p=128, p0=0):
            return PF[p0:p0 + p, a:a + n]

        def bs(a, n, p=128, p0=0):
            return PB[p0:p0 + p, a:a + n]

        b["lnt"] = fs(0, 1024); b["sqs"] = fs(1024, 1024)
        b["x1t"] = fs(2048, 1024); b["xt"] = fs(3072, 1024)
        b["lnst"] = fs(4096, 16); b["rowm"] = fs(4112, 8)
        b["scl"] = fs(4120, 128); b["asc"] = fs(4248, 256)
        b["ffsb"] = fs(4504, 4)
        b["onesrow"] = fs(8608, 128, p=1)
        b["ffs"] = fs(8736, 8, p=4)
        b["praw"] = fs(8744, 8 * WIN)
        b["pfm"] = fs(10816, 2048)
        b["beta"] = fs(4510, 2048, p=16)
        b["wstage"] = fs(8744, 8192)
        b["qfm"] = fs(8744, 1024); b["kfm"] = fs(9768, 1024)
        b["ktm"] = fs(10792, 1024); b["qtm"] = fs(11816, 1024)
        b["vtm"] = b["qtm"]
        b["kdd"] = fs(14912, 1024); b["kddf"] = fs(15936, 1024)
        b["vstage"] = fs(14912, 1024)
        b["Xb"] = fs(16960, 2048); b["Xtb"] = fs(19008, 2048)
        b["Pb"] = fs(21056, 2048); b["Ptb"] = fs(23104, 2048)
        b["rb"] = fs(25152, 2048)
        b["mrun"] = fs(27200, 512); b["brun"] = fs(27712, 512)
        b["Gtb"] = b["Pb"]
        b["stg"] = b["kdd"]; b["stg2"] = b["kddf"]
        b["Sb"] = fs(27200, 512); b["Mg"] = fs(27712, 512)
        b["c_nwrep"] = fs(14912, 1024)
        b["Bg"] = fs(11816, 512)
        b["yfm"] = fs(10792, 4096)
        b["hraw"] = fs(8744, 32 * 130)
        b["htt"] = fs(8744, 4096)
        b["hcs"] = fs(12904, 4096)
        b["hscp"] = fs(17000, 130); b["zscp"] = fs(17130, 128)
        b["hscr"] = fs(17258, 2050, p=1)
        b["zscr"] = fs(19308, 2048, p=1)
        b["cball"] = fs(21356, 2048, p=4)
        b["brow"] = fs(23404, 2048, p=1)
        b["zf"] = fs(25452, 1024)
        b["xib"] = PF[:].bitcast(BF16)[:, 52952:56024]
        b["WAh"] = bs(0, 8192); b["WAl"] = bs(8192, 8192)
        b["WBh"] = bs(16384, 8192); b["WBl"] = bs(24576, 8192)
        b["WCh"] = b["WAh"]; b["WCl"] = b["WAl"]
        b["WDh"] = b["WBh"]; b["WDl"] = b["WBl"]
        b["lnph"] = bs(32768, 8 * WIN); b["lnpl"] = bs(34840, 8 * WIN)
        b["hif"] = bs(36912, 1024); b["lof"] = bs(37936, 1024)
        b["c_trilS"] = bs(0, 2048); b["c_triuS"] = bs(2048, 2048)
        b["c_triuI"] = bs(4096, 2048)
        b["oghi"] = bs(16384, 8192); b["oglo"] = bs(24576, 8192)
        b["xiw"] = bs(32768, 8 * 130); b["xit"] = bs(33808, 1024)
        b["wq16"] = bs(32768, 8192)
        b["hqt"] = bs(34832, 4096); b["hqf"] = bs(38928, 4096)
        b["wi32"] = PI[:, 0:2048]
        b["xi32"] = PI[:, 0:1024]
        b["hi32"] = PI[:, 0:2048]
        b["c_id128"] = sb("c_id128", [128, 128])
        b["c_idpair"] = sb("c_idpair", [128, 8 * d])
        b["c_convw"] = sb("c_convw", [128, 8 * 12])
        b["c_convb"] = sb("c_convb", [128, 8 * 3])
        b["c_wbh"] = sb("c_wbh", [128, 8 * H], BF16)
        b["c_wbl"] = sb("c_wbl", [128, 8 * H], BF16)
        b["c_pm"] = sb("c_pm", [128, 4])
        b["c_pm1"] = sb("c_pm1", [128, 4])
        b["c_hm"] = sb("c_hm", [128, 2])
        b["c_ones"] = sb("c_ones", [128, 2])
        b["c_sel"] = sb("c_sel", [4, 2])
        b["hhalo"] = sb("hhalo", [128, 2 * 32])
        b["c_fcw"] = sb("c_fcw", [128, 32 * 3])
        b["c_id128b"] = sb("c_id128b", [128, 128], BF16)
        b["c_fcb"] = sb("c_fcb", [128, 32])
        b["xif_h"] = sb("xif_h", [128, 8], BF16)
        b["pX"] = psum("pX", [128, H * C])
        b["pY"] = psum("pY", [128, H * C])

        emit(nc, bld, b, io, sp, out)

        with nc.Block() as block:
            @block.sync
            def _(e):
                for f in bld.ops["sync"]:
                    f(e)

            @block.gpsimd
            def _(e):
                for f in bld.ops["gpsimd"]:
                    f(e)

            @block.vector
            def _(e):
                for f in bld.ops["vector"]:
                    f(e)

            @block.scalar
            def _(e):
                for f in bld.ops["scalar"]:
                    f(e)

            @block.tensor
            def _(e):
                for f in bld.ops["tensor"]:
                    f(e)

    return nc


_PH = int(os.environ.get("KPH", "99"))
_PB = int(os.environ.get("KPB", "99"))
_PS = int(os.environ.get("KPS", "99"))


def emit(nc, B_, b, io, sp, out):
    V, SC, G = "vector", "scalar", "gpsimd"

    def cut(k):
        if _PH < k:
            _dbg_tap()
            B_.wait("sync", "sD")
            B_.wait("sync", "sG")
            return True
        return False

    def _dbg_tap():
        dbg = os.environ.get("KDBG", "")
        if not dbg:
            return
        B_.barrier("sync")
        B_.barrier(G)
        if dbg in ("q", "k", "vtm", "gate", "zt", "gg", "x1"):
            for cc in range(16):
                B_.dma("sync", out[cc * 128:(cc + 1) * 128, :], sp[dbg][cc])
        elif dbg == "mtbt":
            for cc in range(16):
                B_.dma("sync", out[cc * 128:(cc + 1) * 128, 0:512], sp["mt"][cc])
                B_.dma("sync", out[cc * 128:(cc + 1) * 128, 512:1024], sp["bt"][cc])
        elif dbg == "xi":
            for cc in range(NT):
                B_.dma(G, out[cc * 128:(cc + 1) * 128, :], sp["xi"][cc])
        elif dbg == "lnh":
            for half_ in range(2):
                for fc in range(8):
                    B_.dma(G, b["praw"][:, 0:1024],
                           sp["lnh"][fc, :, half_ * 1024:(half_ + 1) * 1024])
                    B_.wait("sync", "sG")
                    B_.dma("sync", out[half_ * 1024 + fc * 128:
                                       half_ * 1024 + (fc + 1) * 128, :]
                           .transpose([1, 0]) if False else
                           out[half_ * 1024 + fc * 128:half_ * 1024 + (fc + 1) * 128, :],
                           b["praw"][0:128, 0:1024])
                    B_.barrier(G, ("sD",))
        elif dbg == "hif":
            B_.dma(G, out[0:128, :], b["hif"][:])
            B_.dma(G, out[128:256, :], b["lof"][:])
            B_.dma("sync", out[256:384, :], b["lnt"][:])
            B_.dma("sync", out[384:512, 0:16], b["lnst"][:])
        elif dbg == "pfm":
            B_.dma("sync", out[0:128, :], b["pfm"][:, 0:1024])
            B_.dma("sync", out[128:256, :], b["pfm"][:, 1024:2048])
            B_.dma("sync", out[256:384, :], b["praw"][:, 0:1024])
            B_.dma("sync", out[384:512, :], b["praw"][:, 1024:2048])
            B_.dma("sync", out[512:640, 0:72], b["praw"][:, 2000:2072])
        elif dbg == "p3o":
            B_.dma("sync", out[0:128, :], b["kfm"][:, 0:1024])
            B_.dma("sync", out[128:256, :], b["qfm"][:, 0:1024])
            B_.dma("sync", out[256:384, :], b["rb"][:, 0:1024])
            B_.dma("sync", out[384:512, 0:512], b["Sb"][:])
            B_.dma("sync", out[512:640, 0:256], b["asc"][:])
        elif dbg == "wo":
            B_.dma(G, out[0:128, :], b["WCh"][:, 0:1024])
            B_.dma(G, out[128:256, :], b["WCl"][:, 0:1024])
            B_.dma(G, out[256:384, :], b["WCh"][:, 1024:2048])
        elif dbg == "yfm":
            B_.dma("sync", out[0:128, :], b["yfm"][:, 0:1024])
            B_.dma("sync", out[128:256, :], b["yfm"][:, 1024:2048])
            B_.dma("sync", out[256:384, :], b["yfm"][:, 2048:3072])
            B_.dma("sync", out[384:512, :], b["yfm"][:, 3072:4096])
            B_.dma(G, out[512:640, :], b["oghi"][:, 0:1024])
            B_.dma(G, out[640:768, :], b["oghi"][:, 4096:5120])
            B_.dma(G, out[768:896, :], b["oglo"][:, 0:1024])
        elif dbg == "xseg":
            B_.dma("sync", out[:, :], io["x_seg"][128:128 + R, :])
        elif dbg == "beta":
            B_.dma("sync", out[0:16, :], b["beta"][:, 0:1024])
            B_.dma("sync", out[16:32, :], b["beta"][:, 1024:2048])
        elif dbg == "asc":
            B_.dma("sync", out[0:128, 0:256], b["asc"][:])
        elif dbg == "sb":
            B_.dma("sync", out[0:128, 0:512], b["Sb"][:])
        elif dbg == "hst":
            for g4 in range(4):
                B_.dma("sync", out[g4 * 128:(g4 + 1) * 128, :],
                       b["hcs"][:, g4 * 1024:(g4 + 1) * 1024])
            for g4 in range(4):
                B_.dma("sync", out[512 + g4 * 128:512 + (g4 + 1) * 128, :],
                       b["htt"][:, g4 * 1024:(g4 + 1) * 1024])
            B_.dma(G, out[1024:1152, :], b["xiw"][:, 0:1024])
            B_.dma(G, out[1152:1280, 0:16], b["xiw"][:, 1024:1040])
            B_.dma("sync", out[1280:1408, 0:130], b["hscp"][:])
            B_.dma(G, out[1408:1536, :], b["hqt"][:, 0:1024])
            B_.dma("sync", out[1536:1664, 0:96], b["c_fcw"][:])
            B_.dma("sync", out[1664:1792, 0:32], b["c_fcb"][:])
            B_.dma("sync", out[1792:1920, 0:64], b["hhalo"][:])
        elif dbg == "hqf":
            qq = int(os.environ.get("KDBGI", "0"))
            for cc in range(4):
                B_.dma(G, out[cc * 512:(cc + 1) * 512, :]
                       .rearrange("(a p) f -> a p f", p=128),
                       sp["hqf"][qq * 4 + cc].rearrange("p (a f) -> a p f", a=4))
    pX, pY = b["pX"], b["pY"]
    id128 = b["c_id128"][:]
    id128b = b["c_id128b"][:]
    pXb = pX[:].bitcast(BF16)
    pYb = pY[:].bitcast(BF16)
    lnst = b["lnst"]

    def half(h):
        return 64 * (h & 1)

    def rsqrt(t_ap, s_ap, w1, w2):
        B_.op(V, "reciprocal", w1, t_ap, inc=("sV", 1))
        B_.wait(SC, "sV")
        B_.act(s_ap, w1, AF.Sqrt, inc=("sS", 1))
        B_.wait(V, "sS")
        B_.op(V, "tensor_mul", w1, t_ap, s_ap)
        B_.op(V, "tensor_mul", w2, w1, s_ap)
        B_.op(V, "tensor_scalar", w1, w2, -0.5, 1.5, ALU.mult, ALU.add)
        B_.op(V, "tensor_mul", s_ap, s_ap, w1)

    def ln_core(x_ap, out_f32=None, hi_ap=None, lo_ap=None):
        P_ = x_ap.shape[0]
        st = lnst[0:P_, :]
        B_.barrier(SC)
        B_.act(b["sqs"][0:P_, :], x_ap, AF.Square,
               accum_out=st[:, 1:2], inc=("sS", 1))
        B_.wait(V, "sS")
        B_.op(V, "reduce_sum", st[:, 0:1], x_ap.unsqueeze(1), axis=AX.X,
              inc=("sF", 1))
        B_.wait(V, "sF")
        B_.op(V, "tensor_scalar_mul", st[:, 0:1], st[:, 0:1], 1.0 / D)
        B_.op(V, "tensor_mul", st[:, 3:4], st[:, 0:1], st[:, 0:1])
        B_.op(V, "tensor_scalar_mul", st[:, 2:3], st[:, 1:2], 1.0 / D)
        B_.op(V, "tensor_sub", st[:, 2:3], st[:, 2:3], st[:, 3:4])
        B_.op(V, "tensor_scalar_add", st[:, 2:3], st[:, 2:3], 1e-5)
        rsqrt(st[:, 2:3], st[:, 5:6], st[:, 6:7], st[:, 7:8])
        B_.op(V, "tensor_mul", st[:, 8:9], st[:, 0:1], st[:, 5:6])
        B_.op(V, "tensor_scalar_mul", st[:, 8:9], st[:, 8:9], -1.0,
              inc=("sV", 1))
        B_.wait(SC, "sV")
        if out_f32 is not None:
            B_.act(out_f32, x_ap, AF.Identity, scale=st[:, 5:6], bias=st[:, 8:9],
                   inc=("sS", 1))
        if hi_ap is not None:
            B_.act(hi_ap, x_ap, AF.Identity, scale=st[:, 5:6], bias=st[:, 8:9],
                   inc=("sS", 1))
            B_.wait(V, "sS")
            B_.op(V, "scalar_tensor_tensor", b["sqs"][0:P_, :], x_ap,
                  st[:, 5:6], hi_ap, ALU.mult, ALU.subtract)
            B_.op(V, "tensor_scalar", lo_ap, b["sqs"][0:P_, :],
                  st[:, 8:9], None, ALU.add, ALU.bypass, inc=("sV", 1))

    for name, buf in (("ident128", "c_id128"), ("identpair", "c_idpair"),
                      ("pmask", "c_pm"), ("pmask1", "c_pm1"),
                      ("hmask", "c_hm"), ("onescol", "c_ones"),
                      ("csel", "c_sel")):
        B_.dma("sync", b[buf][:], io[name][:])
    for i, (wn, bn) in enumerate((("cqw", "cqb"), ("ckw", "ckb"),
                                  ("cvw", "cvb"))):
        B_.dma("sync", b["c_convw"][:, i * 32:(i + 1) * 32]
               .rearrange("p (c f) -> p c f", c=8),
               io[wn].rearrange("(c p) f -> p c f", p=128))
        B_.dma("sync", b["c_convb"][:, i * 8:(i + 1) * 8]
               .rearrange("p (c f) -> p c f", c=8),
               io[bn].rearrange("(c p) f -> p c f", p=128))
    B_.dma(G, b["c_id128b"][:], io["ident128"][:])
    B_.dma(G, b["c_wbh"][:].rearrange("p (c f) -> p c f", c=8),
           io["wb"].rearrange("(c p) f -> p c f", p=128))
    B_.dma("sync", b["wstage"][:, :8 * H].rearrange("p (c f) -> p c f", c=8),
           io["wb"].rearrange("(c p) f -> p c f", p=128))
    B_.wait(V, "sD")
    B_.wait(V, "sG")
    B_.op(V, "tensor_sub", b["c_wbl"][:], b["wstage"][:, :8 * H],
          b["c_wbh"][:], inc=("sV", 1))
    B_.barrier("tensor")
    B_.mm(pX[0:1, 0:128], b["c_ones"][:, 0:1], id128, inc=("sT", 1))
    B_.wait(V, "sT")
    B_.op(V, "tensor_copy", b["onesrow"][:], pX[0:1, 0:128], inc=("sV", 1))

    def load_wpair(wname, hi, lo):
        B_.barrier(G, ("sV",))
        B_.barrier("sync", ("sV",))
        B_.dma(G, b[hi][:].rearrange("p (c f) -> p c f", c=8),
               io[wname].rearrange("(c p) f -> p c f", p=128))
        B_.dma("sync", b["wstage"][:].rearrange("p (c f) -> p c f", c=8),
               io[wname].rearrange("(c p) f -> p c f", p=128))
        B_.wait(V, "sD")
        B_.wait(V, "sG")
        B_.op(V, "tensor_sub", b[lo][:], b["wstage"][:], b[hi][:],
              inc=("sV", 1))

    load_wpair("wq", "WAh", "WAl")
    load_wpair("wk", "WBh", "WBl")

    for tt in range(17):
        B_.barrier("sync")
        B_.dma("sync", b["lnt"][:], io["x_seg"][tt * 128:(tt + 1) * 128, :])
        B_.wait(SC, "sD")
        B_.wait(V, "sD")
        ln_core(b["lnt"][:], hi_ap=b["hif"][:], lo_ap=b["lof"][:])
        B_.barrier("tensor")
        for fc in range(8):
            B_.tp(pXb[:, fc * 128:(fc + 1) * 128],
                  b["hif"][:, fc * 128:(fc + 1) * 128], id128b)
        for fc in range(8):
            B_.tp(pYb[:, fc * 128:(fc + 1) * 128],
                  b["lof"][:, fc * 128:(fc + 1) * 128], id128b,
                  inc=("sT", 1) if fc == 7 else None)
        B_.wait(V, "sT")
        B_.op(V, "tensor_copy", b["hif"][:], pXb[:, :1024])
        B_.op(V, "tensor_copy", b["lof"][:], pYb[:, :1024], inc=("sV", 1))
        B_.wait("sync", "sV")
        B_.dma("sync", sp["lnh"][:, :, tt * 128:(tt + 1) * 128].transpose([1, 0, 2]),
               b["hif"][:].rearrange("p (c f) -> p c f", c=8))
        B_.dma("sync", sp["lnl"][:, :, tt * 128:(tt + 1) * 128].transpose([1, 0, 2]),
               b["lof"][:].rearrange("p (c f) -> p c f", c=8))

    def project(hi, lo, psrc, ncols, dst, dstw, m_list=range(8), pstart=0):
        for mi, m in enumerate(m_list):
            bank = mi % 4
            po = pX[:, bank * 512:bank * 512 + ncols]
            B_.barrier("tensor", ("sV", "sS", "sD", "sG"))
            for kk in range(8):
                B_.mm(po, b[hi][:, kk * D + m * 128:kk * D + (m + 1) * 128],
                      b["lnph"][:, kk * WIN + psrc:kk * WIN + psrc + ncols],
                      start=(kk == 0), stop=False)
                B_.mm(po, b[lo][:, kk * D + m * 128:kk * D + (m + 1) * 128],
                      b["lnph"][:, kk * WIN + psrc:kk * WIN + psrc + ncols],
                      start=False, stop=False)
                B_.mm(po, b[hi][:, kk * D + m * 128:kk * D + (m + 1) * 128],
                      b["lnpl"][:, kk * WIN + psrc:kk * WIN + psrc + ncols],
                      start=False, stop=(kk == 7),
                      inc=("sT", 1) if kk == 7 else None)
            B_.wait(V, "sT")
            B_.op(V, "tensor_copy",
                  dst[:, m * dstw + pstart:m * dstw + pstart + ncols], po,
                  inc=("sV", 1))

    def conv_silu(tensor_idx, do_silu=True):
        cw = b["c_convw"]
        cb_ = b["c_convb"]
        for m in range(8):
            wofs = tensor_idx * 32 + m * 4
            B_.barrier(SC, ("sV", "sD", "sG"))
            B_.act(b["pfm"][:, m * BLK:(m + 1) * BLK],
                   b["praw"][:, m * WIN + 3:m * WIN + 3 + BLK], AF.Identity,
                   scale=cw[:, wofs + 3:wofs + 4],
                   bias=cb_[:, tensor_idx * 8 + m:tensor_idx * 8 + m + 1],
                   inc=("sS", 1))
        B_.wait(V, "sS")
        for j in range(3):
            for m in range(8):
                wofs = tensor_idx * 32 + m * 4
                B_.op(V, "scalar_tensor_tensor",
                      b["pfm"][:, m * BLK:(m + 1) * BLK],
                      b["praw"][:, m * WIN + j:m * WIN + j + BLK],
                      cw[:, wofs + j:wofs + j + 1],
                      b["pfm"][:, m * BLK:(m + 1) * BLK],
                      ALU.mult, ALU.add,
                      inc=("sV", 1) if (j == 2 and m == 7) else None)
        if do_silu:
            B_.wait(SC, "sV")
            B_.act(b["pfm"][:], b["pfm"][:], AF.Silu, inc=("sS", 1))

    def load_window(blk):
        base = 125 + BLK * blk
        B_.barrier("sync")
        B_.dma("sync", b["lnph"][:].rearrange("p (c f) -> p c f", c=8),
               sp["lnh"][:, :, base:base + WIN].transpose([1, 0, 2]))
        B_.dma("sync", b["lnpl"][:].rearrange("p (c f) -> p c f", c=8),
               sp["lnl"][:, :, base:base + WIN].transpose([1, 0, 2]))

    def spill_fm(spname, blk, src):
        for hh in range(2):
            cc = blk * 2 + hh
            B_.dma("sync", sp[spname][cc].rearrange("p (c f) -> p c f", c=8),
                   src.rearrange("p (c f) -> p c f", c=8)[:, :, hh * 128:(hh + 1) * 128])

    if cut(2):
        return
    for blk in range(NBLK):
        load_window(blk)
        B_.wait("tensor", "sD")
        project("WAh", "WAl", 0, WIN, b["praw"], WIN)
        conv_silu(0)
        B_.wait("sync", "sS")
        spill_fm("q", blk, b["pfm"])
        project("WBh", "WBl", 0, WIN, b["praw"], WIN)
        conv_silu(1)
        B_.wait("sync", "sS")
        spill_fm("k", blk, b["pfm"])
        B_.barrier("tensor")
        po = pY[0:16, 0:BLK]
        for kk in range(8):
            B_.mm(po, b["c_wbh"][:, kk * H:(kk + 1) * H],
                  b["lnph"][:, kk * WIN + 3:kk * WIN + 3 + BLK],
                  start=(kk == 0), stop=False)
            B_.mm(po, b["c_wbl"][:, kk * H:(kk + 1) * H],
                  b["lnph"][:, kk * WIN + 3:kk * WIN + 3 + BLK],
                  start=False, stop=False)
            B_.mm(po, b["c_wbh"][:, kk * H:(kk + 1) * H],
                  b["lnpl"][:, kk * WIN + 3:kk * WIN + 3 + BLK],
                  start=False, stop=(kk == 7),
                  inc=("sT", 1) if kk == 7 else None)
        B_.wait(SC, "sT")
        B_.act(b["beta"][:, blk * BLK:(blk + 1) * BLK], po, AF.Sigmoid,
               inc=("sS", 1))

    if cut(3):
        return
    load_wpair("wv", "WCh", "WCl")
    load_wpair("wg", "WDh", "WDl")
    for blk in range(NBLK):
        load_window(blk)
        B_.wait("tensor", "sD")
        if _PB < 2:
            continue
        project("WCh", "WCl", 0, WIN, b["praw"], WIN)
        conv_silu(2)
        if _PB < 3:
            continue
        for hh in range(2):
            cc = blk * 2 + hh
            B_.barrier("tensor")
            for i_, h in enumerate([x for x in range(H) if x % 2 == 0] + [x for x in range(H) if x % 2 == 1]):
                B_.tp(pX[:, h * d:(h + 1) * d],
                      b["pfm"][half(h):half(h) + 64,
                               (h >> 1) * BLK + hh * 128:(h >> 1) * BLK + (hh + 1) * 128],
                      b["c_idpair"][half(h):half(h) + 64, 0:64],
                      inc=("sT", 1) if i_ in (7, H - 1) else None,
                      tpos=(half(h), 0))
                if i_ == 7:
                    B_.wait("tensor", "sT")
            B_.wait(V, "sT")
            B_.op(V, "tensor_copy", b["vstage"][:], pX[:, :H * d],
                  inc=("sV", 1))
            B_.wait("sync", "sV")
            B_.dma("sync", sp["vtm"][cc], b["vstage"][:])
        if _PB < 4:
            continue
        project("WDh", "WDl", 3, BLK, b["pfm"], BLK)
        B_.wait("sync", "sV")
        spill_fm("gate", blk, b["pfm"])

    if cut(4):
        return
    B_.barrier(G)
    for name, buf in (("m_trilS", "c_trilS"), ("m_triuS", "c_triuS"),
                      ("m_triuI", "c_triuI")):
        B_.dma(G, b[buf][:], io[name][:])
    sc_bcol = b["scl"][:, 0:16]
    sc_ssk = b["scl"][:, 16:32]
    sc_ssq = b["scl"][:, 32:48]
    sc_tmp = b["scl"][:, 48:64]
    sc_bsc = b["scl"][:, 64:80]
    sc_bb = b["scl"][:, 80:96]
    sc_bb2 = b["scl"][:, 96:112]
    sc_w2 = b["scl"][:, 112:128]
    B_.barrier(V)
    B_.op(V, "tensor_copy", b["mrun"][:], b["c_idpair"][:])
    B_.op(V, "memset", b["brun"][:], 0.0, inc=("sV", 1))

    for cc in range(NCH):
        B_.barrier("sync")
        B_.dma("sync", b["qfm"][:], sp["q"][cc])
        B_.dma("sync", b["kfm"][:], sp["k"][cc])
        B_.barrier("tensor")
        for i_, h in enumerate([x for x in range(H) if x % 2 == 0] + [x for x in range(H) if x % 2 == 1]):
            B_.tp(pX[:, h * d:(h + 1) * d],
                  b["kfm"][half(h):half(h) + 64, (h >> 1) * C:(h >> 1) * C + C],
                  b["c_idpair"][half(h):half(h) + 64, 0:64], tpos=(half(h), 0),
                  inc=("sT", 1) if i_ in (7, H - 1) else None)
            if i_ == 7:
                B_.wait("tensor", "sT")
        for i_, h in enumerate([x for x in range(H) if x % 2 == 0] + [x for x in range(H) if x % 2 == 1]):
            B_.tp(pY[:, h * d:(h + 1) * d],
                  b["qfm"][half(h):half(h) + 64, (h >> 1) * C:(h >> 1) * C + C],
                  b["c_idpair"][half(h):half(h) + 64, 0:64],
                  inc=("sT", 1) if i_ in (7, H - 1) else None, tpos=(half(h), 0))
            if i_ == 7:
                B_.wait("tensor", "sT")
        B_.wait(V, "sT")
        B_.op(V, "tensor_copy", b["ktm"][:], pX[:, :H * d])
        B_.op(V, "tensor_copy", b["qtm"][:], pY[:, :H * d], inc=("sV", 1))
        B_.barrier("tensor", ("sV",))
        B_.tp(pX[:, 0:16], b["beta"][:, cc * C:(cc + 1) * C],
              b["c_idpair"][0:16, 0:16], inc=("sT", 1))
        B_.wait(V, "sT")
        B_.op(V, "tensor_copy", sc_bcol, pX[:, 0:16], inc=("sV", 1))
        B_.wait(SC, "sV")
        B_.act(b["sqs"][:], b["ktm"][:], AF.Square, inc=("sS", 1))
        B_.wait(V, "sS")
        B_.op(V, "reduce_sum", sc_ssk.unsqueeze(2),
              b["sqs"][:].rearrange("p (h j) -> p h j", h=H), axis=AX.X,
              inc=("sV", 1))
        B_.wait(V, "sV")
        B_.wait(SC, "sV")
        B_.act(b["sqs"][:], b["qtm"][:], AF.Square, inc=("sS", 1))
        B_.wait(V, "sS")
        B_.op(V, "reduce_sum", sc_ssq.unsqueeze(2),
              b["sqs"][:].rearrange("p (h j) -> p h j", h=H), axis=AX.X,
              inc=("sF", 1))
        B_.wait(V, "sF")
        B_.op(V, "tensor_scalar_add", sc_tmp, sc_ssk, 1e-6)
        rsqrt(sc_tmp, sc_bsc, sc_bb, sc_w2)
        B_.op(V, "tensor_scalar", sc_tmp, sc_ssq, 64.0, 64e-6, ALU.mult,
              ALU.add)
        rsqrt(sc_tmp, b["asc"][:, cc * H:(cc + 1) * H], sc_bb, sc_w2)
        B_.wait("sync", "sV")
        B_.wait("sync", "sS")
        B_.dma("sync", b["vtm"][:], sp["vtm"][cc])
        B_.op(V, "tensor_mul", sc_bb, sc_bcol, sc_bsc)
        B_.op(V, "tensor_mul", sc_bb2, sc_bb, sc_bsc)
        for h in range(H):
            B_.op(V, "tensor_scalar_mul", b["kdd"][:, h * d:(h + 1) * d],
                  b["ktm"][:, h * d:(h + 1) * d], sc_bb2[:, h:h + 1])
        B_.wait(V, "sD")
        for h in range(H):
            B_.op(V, "tensor_scalar_mul", b["rb"][:, h * C:h * C + 64],
                  b["vtm"][:, h * d:(h + 1) * d], sc_bb[:, h:h + 1])
        B_.op(V, "tensor_copy",
              b["rb"][:].rearrange("p (h t) -> p h t", h=H)[:, :, 64:128],
              b["kdd"][:].rearrange("p (h j) -> p h j", h=H), inc=("sV", 1))
        B_.barrier("tensor", ("sV",))
        for h in range(H):
            B_.mm(pX[half(h):half(h) + 64, (h >> 1) * C:(h >> 1) * C + C],
                  b["kdd"][:, h * d:(h + 1) * d], id128,
                  inc=("sT", 1) if h == H - 1 else None,
                  tpos=(0, half(h)))
        B_.wait(V, "sT")
        B_.op(V, "tensor_copy", b["kddf"][:], pX[:, :8 * C], inc=("sV", 1))
        B_.barrier("tensor", ("sV",))
        for i_, h in enumerate([x for x in range(H) if x % 2 == 0] + [x for x in range(H) if x % 2 == 1]):
            B_.mm(pX[:, h * C:(h + 1) * C],
                  b["kddf"][half(h):half(h) + 64, (h >> 1) * C:(h >> 1) * C + C],
                  b["kfm"][half(h):half(h) + 64, (h >> 1) * C:(h >> 1) * C + C],
                  tpos=(half(h), 0),
                  inc=("sT", 1) if i_ in (7, H - 1) else None)
            if i_ == 7:
                B_.wait("tensor", "sT")
        for i_, h in enumerate([x for x in range(H) if x % 2 == 0] + [x for x in range(H) if x % 2 == 1]):
            B_.mm(pY[:, h * C:(h + 1) * C],
                  b["kfm"][half(h):half(h) + 64, (h >> 1) * C:(h >> 1) * C + C],
                  b["kddf"][half(h):half(h) + 64, (h >> 1) * C:(h >> 1) * C + C],
                  inc=("sT", 1) if i_ in (7, H - 1) else None,
                  tpos=(half(h), 0))
            if i_ == 7:
                B_.wait("tensor", "sT")
        B_.wait(V, "sT")
        B_.op(V, "tensor_mul", b["Xb"][:], pX[:, :], b["c_trilS"][:])
        B_.op(V, "tensor_mul", b["Xtb"][:], pY[:, :], b["c_triuS"][:],
              inc=("sV", 1))
        B_.barrier("tensor", ("sV",))
        for h in range(H):
            B_.mm(pX[:, h * C:(h + 1) * C], b["Xtb"][:, h * C:(h + 1) * C],
                  b["rb"][:, h * C:(h + 1) * C],
                  inc=("sT", 1) if h == H - 1 else None)
        B_.wait(V, "sT")
        B_.op(V, "tensor_sub", b["rb"][:], b["rb"][:], pX[:, :], inc=("sV", 1))
        cur, curT, nxt, nxtT = "Xb", "Xtb", "Pb", "Ptb"
        for lvl in range(6):
            B_.barrier("tensor", ("sV",))
            if lvl < 5:
                for h in range(H):
                    B_.mm(pX[:, h * C:(h + 1) * C],
                          b[curT][:, h * C:(h + 1) * C],
                          b[cur][:, h * C:(h + 1) * C])
            for h in range(H):
                B_.mm(pY[:, h * C:(h + 1) * C], b[cur][:, h * C:(h + 1) * C],
                      b[curT][:, h * C:(h + 1) * C],
                      inc=("sT", 1) if h == H - 1 else None)
            B_.wait(V, "sT")
            if lvl < 5:
                B_.op(V, "tensor_copy", b[nxt][:], pX[:, :])
            B_.op(V, "tensor_copy", b[nxtT][:], pY[:, :], inc=("sV", 1))
            B_.barrier("tensor", ("sV",))
            for h in range(H):
                B_.mm(pX[:, h * C:(h + 1) * C], b[nxtT][:, h * C:(h + 1) * C],
                      b["rb"][:, h * C:(h + 1) * C],
                      inc=("sT", 1) if h == H - 1 else None)
            B_.wait(V, "sT")
            B_.op(V, "tensor_add", b["rb"][:], b["rb"][:], pX[:, :],
                  inc=("sV", 1))
            cur, curT, nxt, nxtT = nxt, nxtT, cur, curT

        def rV(h):
            return b["rb"][:, h * C:h * C + 64]

        def rK(h):
            return b["rb"][:, h * C + 64:h * C + 128]

        B_.barrier("tensor", ("sV",))
        for i_, h in enumerate([x for x in range(H) if x % 2 == 0] + [x for x in range(H) if x % 2 == 1]):
            B_.mm(pX[:, h * C:(h + 1) * C],
                  b["kfm"][half(h):half(h) + 64, (h >> 1) * C:(h >> 1) * C + C],
                  b["qfm"][half(h):half(h) + 64, (h >> 1) * C:(h >> 1) * C + C],
                  inc=("sT", 1) if i_ in (7, H - 1) else None,
                  tpos=(half(h), 0))
            if i_ == 7:
                B_.wait("tensor", "sT")
        B_.wait(V, "sT")
        B_.op(V, "tensor_mul", b["Gtb"][:], pX[:, :], b["c_triuI"][:],
              inc=("sV", 1))
        B_.barrier("tensor", ("sV",))
        for h in range(H):
            B_.mm(pY[:, h * d:(h + 1) * d], b["Gtb"][:, h * C:(h + 1) * C],
                  rV(h))
        for h in range(H):
            B_.mm(pX[half(h):half(h) + 64, (h >> 1) * C:(h >> 1) * C + C],
                  rK(h), b["Gtb"][:, h * C:(h + 1) * C],
                  inc=("sT", 1) if h == H - 1 else None,
                  tpos=(0, half(h)))
        B_.wait(V, "sT")
        B_.op(V, "tensor_copy", b["stg"][:], pY[:, :H * d])
        B_.op(V, "tensor_sub", b["stg2"][:], b["qfm"][:], pX[:, :8 * C],
              inc=("sV", 1))
        B_.wait("sync", "sV")
        B_.dma("sync", sp["gg"][cc], b["stg"][:])
        B_.dma("sync", sp["zt"][cc], b["stg2"][:])
        B_.barrier("tensor")
        for h in range(H):
            B_.mm(pX[half(h):half(h) + 64, (h >> 1) * d:(h >> 1) * d + d],
                  rK(h), b["ktm"][:, h * d:(h + 1) * d], tpos=(0, half(h)))
        for h in range(H):
            B_.mm(pY[half(h):half(h) + 64, (h >> 1) * d:(h >> 1) * d + d],
                  rV(h), b["ktm"][:, h * d:(h + 1) * d],
                  inc=("sT", 1) if h == H - 1 else None, tpos=(0, half(h)))
        B_.wait(V, "sT")
        B_.op(V, "tensor_sub", b["stg"][:, :8 * d], b["c_idpair"][:],
              pX[:, :8 * d])
        B_.op(V, "tensor_copy", b["stg2"][:, :8 * d], pY[:, :8 * d],
              inc=("sV", 1))
        B_.wait("sync", "sV")
        B_.dma("sync", sp["mt"][cc], b["stg"][:, :8 * d])
        B_.dma("sync", sp["bt"][cc], b["stg2"][:, :8 * d])
        B_.barrier("tensor", ("sV",))
        for i_, h in enumerate([x for x in range(H) if x % 2 == 0] + [x for x in range(H) if x % 2 == 1]):
            hb, hp = half(h), (h >> 1) * d
            B_.mm(pX[hb:hb + 64, hp:hp + d], b["stg"][hb:hb + 64, hp:hp + d],
                  b["mrun"][hb:hb + 64, hp:hp + d], tpos=(hb, hb))
            B_.mm(pY[hb:hb + 64, hp:hp + d], b["stg"][hb:hb + 64, hp:hp + d],
                  b["brun"][hb:hb + 64, hp:hp + d], start=True, stop=False,
                  tpos=(hb, hb))
            B_.mm(pY[hb:hb + 64, hp:hp + d], b["stg2"][hb:hb + 64, hp:hp + d],
                  b["c_idpair"][hb:hb + 64, hp:hp + d], start=False, stop=True,
                  inc=("sT", 1) if i_ in (7, H - 1) else None, tpos=(hb, hb))
            if i_ == 7:
                B_.wait("tensor", "sT")
        B_.wait(V, "sT")
        B_.op(V, "tensor_copy", b["mrun"][:], pX[:, :8 * d])
        B_.op(V, "tensor_copy", b["brun"][:], pY[:, :8 * d], inc=("sV", 1))

    if cut(5):
        return
    B_.wait("sync", "sV")
    B_.dma("sync", sp["cc_in"][0:1, 0:65536]
           .rearrange("a (p f) -> (a p) f", p=128), b["mrun"][:])
    B_.dma("sync", sp["cc_in"][0:1, 65536:131072]
           .rearrange("a (p f) -> (a p) f", p=128), b["brun"][:])
    B_.wait(G, "sD")
    B_._emit(G, lambda e: e.collective_compute(
        "AllGather", ALU.bypass, replica_groups=[[0, 1, 2, 3], [4, 5, 6, 7]],
        ins=[sp["cc_in"][:].opt()], outs=[sp["cc_out"][:].opt()]),
        ("sC", 1))

    def wquant_stream(src_ap_quarters, nquarters, sw_col, absacc=True):
        for qq in range(nquarters):
            B_.barrier("sync")
            B_.dma("sync", b["wstage"][:].rearrange("p (c f) -> p c f", c=8),
                   src_ap_quarters(qq))
            B_.wait(V, "sD")
            B_.op(V, "reduce_sum", b["rowm"][:, qq:qq + 1].unsqueeze(1),
                  b["wstage"][:].unsqueeze(1), axis=AX.X,
                  apply_absolute_value=True, inc=("sV", 1))

    if cut(6):
        return
    load_wpair("wo", "WCh", "WCl")

    B_.wait("sync", "sC", 1)
    B_.barrier(V)
    B_.op(V, "memset", b["Sb"][:], 0.0, inc=("sV", 1))
    for j in range(3):
        B_.barrier("sync", ("sV", "sT"))
        B_.dma("sync", b["Mg"][:], sp["cc_out"][j:j + 1, 0:65536]
               .rearrange("a (p f) -> (a p) f", p=128))
        B_.dma("sync", b["Bg"][:], sp["cc_out"][j:j + 1, 65536:131072]
               .rearrange("a (p f) -> (a p) f", p=128))
        B_.wait(V, "sD")
        B_.op(V, "tensor_scalar", b["Mg"][:], b["Mg"][:], b["c_pm"][:, j:j + 1],
              None, ALU.mult, ALU.bypass)
        B_.op(V, "scalar_tensor_tensor", b["Mg"][:], b["c_idpair"][:],
              b["c_pm1"][:, j:j + 1], b["Mg"][:], ALU.mult, ALU.add)
        B_.op(V, "tensor_scalar", b["Bg"][:], b["Bg"][:], b["c_pm"][:, j:j + 1],
              None, ALU.mult, ALU.bypass, inc=("sV", 1))
        B_.barrier("tensor", ("sV",))
        for i_, h in enumerate([x for x in range(H) if x % 2 == 0] + [x for x in range(H) if x % 2 == 1]):
            hb, hp = half(h), (h >> 1) * d
            off = 0 if hb == 0 else 512
            B_.mm(pX[hb:hb + 64, off + hp:off + hp + d],
                  b["Mg"][hb:hb + 64, hp:hp + d],
                  b["c_idpair"][hb:hb + 64, 0:64],
                  inc=("sT", 1) if i_ in (7, H - 1) else None, tpos=(hb, hb))
        B_.wait(V, "sT")
        B_.op(V, "tensor_copy", b["Mg"][0:64, :], pX[0:64, 0:512])
        B_.op(V, "tensor_copy", b["Mg"][64:128, :], pX[64:128, 512:1024],
              inc=("sV", 1))
        B_.barrier("tensor", ("sV",))
        for i_, h in enumerate([x for x in range(H) if x % 2 == 0] + [x for x in range(H) if x % 2 == 1]):
            hb, hp = half(h), (h >> 1) * d
            off = 0 if hb == 0 else 512
            B_.mm(pY[hb:hb + 64, off + hp:off + hp + d],
                  b["Mg"][hb:hb + 64, hp:hp + d],
                  b["Sb"][hb:hb + 64, hp:hp + d], start=True, stop=False,
                  tpos=(hb, hb))
            B_.mm(pY[hb:hb + 64, off + hp:off + hp + d],
                  b["c_idpair"][hb:hb + 64, hp:hp + d],
                  b["Bg"][hb:hb + 64, hp:hp + d], start=False, stop=True,
                  inc=("sT", 1) if i_ in (7, H - 1) else None, tpos=(hb, hb))
        B_.wait(V, "sT")
        B_.op(V, "tensor_copy", b["Sb"][0:64, :], pY[0:64, 0:512])
        B_.op(V, "tensor_copy", b["Sb"][64:128, :], pY[64:128, 512:1024],
              inc=("sV", 1))

    if cut(7):
        return
    B_.barrier("sync", ("sV", "sT"))
    B_.dma("sync", b["c_nwrep"][:], io["nwrep"][:])
    sc_rs = b["scl"][:, 0:16]
    sc_t3 = b["scl"][:, 16:32]
    sc_w3 = b["scl"][:, 32:48]
    sc_w4 = b["scl"][:, 48:64]
    for cc in range(int(os.environ.get("KCC", NCH))):
        ztb = b["Xb"]
        ggb = b["Xtb"]
        mtb = b["Pb"]
        gfb = b["Ptb"]
        B_.barrier("sync")
        B_.dma("sync", ztb[:, :8 * C], sp["zt"][cc])
        B_.dma("sync", ggb[:, :H * d], sp["gg"][cc])
        B_.dma("sync", mtb[:, 0:8 * d], sp["mt"][cc])
        B_.dma("sync", mtb[:, 8 * d:16 * d], sp["bt"][cc])
        B_.dma("sync", gfb[:, :8 * C], sp["gate"][cc])
        B_.wait("tensor", "sD")
        B_.barrier("tensor", ("sV", "sS"))
        for i_, h in enumerate([x for x in range(H) if x % 2 == 0] + [x for x in range(H) if x % 2 == 1]):
            hb = half(h)
            B_.mm(pX[:, h * d:(h + 1) * d],
                  ztb[hb:hb + 64, (h >> 1) * C:(h >> 1) * C + C],
                  b["Sb"][hb:hb + 64, (h >> 1) * d:(h >> 1) * d + d],
                  inc=("sT", 1) if i_ in (7, H - 1) else None, tpos=(hb, 0))
            if i_ == 7:
                B_.wait("tensor", "sT")
        if cc + 1 < NCH:
            for i_, h in enumerate([x for x in range(H) if x % 2 == 0] + [x for x in range(H) if x % 2 == 1]):
                hb, hp = half(h), (h >> 1) * d
                off = 0 if hb == 0 else 512
                B_.mm(pY[hb:hb + 64, off + hp:off + hp + d],
                      mtb[hb:hb + 64, hp:hp + d],
                      b["Sb"][hb:hb + 64, hp:hp + d], start=True, stop=False,
                      tpos=(hb, hb))
                B_.mm(pY[hb:hb + 64, off + hp:off + hp + d],
                      mtb[hb:hb + 64, 8 * d + hp:8 * d + hp + d],
                      b["c_idpair"][hb:hb + 64, 0:64], start=False, stop=True,
                      inc=("sT", 1) if i_ in (7, H - 1) else None, tpos=(hb, hb))
        B_.wait(V, "sT")
        B_.op(V, "tensor_add", b["rb"][:, :H * d], pX[:, :H * d], ggb[:, :H * d])
        if cc + 1 < NCH:
            B_.op(V, "tensor_copy", b["Sb"][0:64, :], pY[0:64, 0:512])
            B_.op(V, "tensor_copy", b["Sb"][64:128, :], pY[64:128, 512:1024],
                  inc=("sV", 1))
        for h in range(H):
            B_.op(V, "tensor_scalar_mul", b["rb"][:, h * d:(h + 1) * d],
                  b["rb"][:, h * d:(h + 1) * d],
                  b["asc"][:, cc * H + h:cc * H + h + 1],
                  inc=("sV", 1) if h == H - 1 else None)
        B_.wait(SC, "sV")
        B_.act(b["sqs"][:], b["rb"][:, :H * d], AF.Square, inc=("sS", 1))
        B_.wait(V, "sS")
        B_.op(V, "reduce_sum", sc_t3.unsqueeze(2),
              b["sqs"][:].rearrange("p (h j) -> p h j", h=H), axis=AX.X,
              inc=("sF", 1))
        B_.wait(V, "sF")
        B_.op(V, "tensor_scalar", sc_t3, sc_t3, 1.0 / d, 1e-5, ALU.mult,
              ALU.add)
        rsqrt(sc_t3, sc_rs, sc_w3, sc_w4)
        for h in range(H):
            B_.op(V, "scalar_tensor_tensor", b["rb"][:, h * d:(h + 1) * d],
                  b["rb"][:, h * d:(h + 1) * d], sc_rs[:, h:h + 1],
                  b["c_nwrep"][:, h * d:(h + 1) * d], ALU.mult, ALU.mult,
                  inc=("sV", 1) if h == H - 1 else None)
        B_.barrier("tensor", ("sV",))
        for fc in range(8):
            B_.tp(pX[:, fc * 128:(fc + 1) * 128],
                  gfb[:, fc * 128:(fc + 1) * 128], id128,
                  inc=("sT", 1) if fc == 7 else None)
        B_.wait(SC, "sT")
        B_.act(b["qfm"][:, :1024], pX[:, :1024], AF.Silu, inc=("sS", 1))
        B_.wait(V, "sS")
        B_.op(V, "tensor_mul", b["kfm"][:, :1024], b["rb"][:, :H * d],
              b["qfm"][:, :1024])
        B_.op(V, "tensor_copy", b["hif"][:], b["kfm"][:, :1024])
        B_.op(V, "tensor_sub", b["lof"][:], b["kfm"][:, :1024], b["hif"][:],
              inc=("sV", 1))
        B_.barrier("tensor", ("sV",))
        for fc in range(8):
            B_.tp(pXb[:, fc * 128:(fc + 1) * 128],
                  b["hif"][:, fc * 128:(fc + 1) * 128], id128b)
        for fc in range(8):
            B_.tp(pYb[:, fc * 128:(fc + 1) * 128],
                  b["lof"][:, fc * 128:(fc + 1) * 128], id128b,
                  inc=("sT", 1) if fc == 7 else None)
        B_.wait(V, "sT")
        B_.op(V, "tensor_copy",
              b["oghi"][:].rearrange("p (c f) -> p c f", c=16)[:, 0:8, (cc % 4) * 128:(cc % 4 + 1) * 128],
              pXb[:, :1024].rearrange("p (c f) -> p c f", c=8))
        B_.op(V, "tensor_copy",
              b["oglo"][:].rearrange("p (c f) -> p c f", c=16)[:, 0:8, (cc % 4) * 128:(cc % 4 + 1) * 128],
              pYb[:, :1024].rearrange("p (c f) -> p c f", c=8), inc=("sV", 1))
        if cc % 4 == 3:
            sb4 = cc // 4
            for m in range(8):
                po = pX[:, (m % 4) * 512:(m % 4) * 512 + 512]
                B_.barrier("tensor", ("sV",))
                for kk in range(8):
                    B_.mm(po, b["WCh"][:, kk * 1024 + m * 128:kk * 1024 + (m + 1) * 128],
                          b["oghi"][:, kk * 512:(kk + 1) * 512],
                          start=(kk == 0), stop=False)
                    B_.mm(po, b["WCl"][:, kk * 1024 + m * 128:kk * 1024 + (m + 1) * 128],
                          b["oghi"][:, kk * 512:(kk + 1) * 512],
                          start=False, stop=False)
                    B_.mm(po, b["WCh"][:, kk * 1024 + m * 128:kk * 1024 + (m + 1) * 128],
                          b["oglo"][:, kk * 512:(kk + 1) * 512],
                          start=False, stop=(kk == 7),
                          inc=("sT", 1) if kk == 7 else None)
                B_.wait(V, "sT")
                B_.op(V, "tensor_copy", b["yfm"][:, m * 512:(m + 1) * 512], po,
                      inc=("sV", 1))
            for tt4 in range(4):
                gt = sb4 * 4 + tt4
                B_.barrier("tensor", ("sV", "sS"))
                for mc in range(8):
                    B_.tp(pY[:, mc * 128:(mc + 1) * 128],
                          b["yfm"][:, mc * 512 + tt4 * 128:mc * 512 + (tt4 + 1) * 128],
                          id128, inc=("sT", 1) if mc == 7 else None)
                B_.wait(V, "sT")
                B_.wait(V, "sD")
                B_.op(V, "tensor_copy", b["x1t"][:], pY[:, :1024], inc=("sV", 1))
                ln_core(b["x1t"][:], out_f32=b["lnt"][:])
                B_.barrier("sync", ("sV", "sS"))
                B_.dma("sync", b["xt"][:],
                       io["x_seg"][128 + gt * 128:128 + (gt + 1) * 128, :])
                B_.wait(V, "sD")
                B_.wait(V, "sS")
                B_.op(V, "tensor_add", b["x1t"][:], b["lnt"][:], b["xt"][:],
                      inc=("sV", 1))
                B_.wait("sync", "sV")
                B_.dma("sync", sp["x1"][gt], b["x1t"][:])
                if gt == 0:
                    B_.dma("sync", sp["cb_in"][0:1, 0:D], b["x1t"][0:1, :])
                if gt == NT - 1:
                    B_.dma("sync", sp["cb_in"][0:1, D:2 * D],
                           b["x1t"][127:128, :])

    B_.wait(G, "sD")
    B_._emit(G, lambda e: e.collective_compute(
        "AllGather", ALU.bypass, replica_groups=[[0, 1, 2, 3], [4, 5, 6, 7]],
        ins=[sp["cb_in"][:].opt()], outs=[sp["cb_out"][:].opt()]),
        ("sC", 1))

    if cut(8):
        return
    def w1q(qq):
        return io["fw1"][qq * 1024:(qq + 1) * 1024, :] \
            .rearrange("(c p) f -> p c f", p=128)

    wquant_stream(w1q, 4, 0)
    B_.barrier("tensor", ("sV",))
    B_.mm(pX[0:1, 0:4], b["c_ones"][:, 0:1], b["rowm"][:, 0:4], inc=("sT", 1))
    B_.wait(V, "sT")
    B_.op(V, "reduce_sum", b["ffs"][0:1, 0:1].unsqueeze(1),
          pX[0:1, 0:4].unsqueeze(1), axis=AX.X, inc=("sF", 1))
    B_.wait(V, "sF")
    B_.op(V, "tensor_scalar_mul", b["ffs"][0:1, 1:2], b["ffs"][0:1, 0:1],
          1.0 / (DH * D))
    B_.op(V, "tensor_scalar_max", b["ffs"][0:1, 1:2], b["ffs"][0:1, 1:2],
          1e-5)
    B_.op(V, "reciprocal", b["ffs"][0:1, 2:3], b["ffs"][0:1, 1:2],
          inc=("sV", 1))
    B_.barrier("tensor", ("sV",))
    B_.mm(pX[:, 0:1], b["onesrow"][:], b["ffs"][0:1, 2:3], inc=("sT", 1))
    B_.mm(pX[:, 1:2], b["onesrow"][:], b["ffs"][0:1, 1:2], inc=("sT", 1))
    B_.wait(V, "sT")
    B_.op(V, "tensor_copy", b["ffsb"][:, 0:2], pX[:, 0:2], inc=("sV", 1))

    W1T = [("WAh", 0), ("WAh", 1), ("WAl", 0), ("WAl", 1),
           ("WBh", 0), ("WBh", 1), ("WBl", 0), ("WBl", 1)]

    for qq in range(4):
        B_.barrier("sync")
        B_.dma("sync", b["wstage"][:].rearrange("p (c f) -> p c f", c=8),
               w1q(qq))
        B_.wait(V, "sD")
        for hh2 in range(4):
            sl = slice(hh2 * 2048, (hh2 + 1) * 2048)
            B_.op(V, "tensor_scalar", b["wstage"][:, sl], b["wstage"][:, sl],
                  b["ffsb"][:, 0:1], None, ALU.mult, ALU.bypass)
            B_.op(V, "tensor_scalar", b["wstage"][:, sl], b["wstage"][:, sl],
                  -1.0, 1.0, ALU.max, ALU.min)
            B_.op(V, "tensor_copy", b["wi32"][:], b["wstage"][:, sl])
            B_.op(V, "tensor_copy", b["wq16"][:, sl], b["wi32"][:],
                  inc=("sV", 1))
        for dc in range(8):
            buf, plane = W1T[dc]
            B_.barrier("tensor", ("sV",))
            for jc in range(8):
                B_.tp(pXb[:, jc * 128:(jc + 1) * 128],
                      b["wq16"][:, jc * 1024 + dc * 128:jc * 1024 + (dc + 1) * 128],
                      id128b, inc=("sT", 1) if jc == 7 else None)
            B_.wait(V, "sT")
            B_.op(V, "tensor_copy",
                  b[buf][:, plane * 4096 + qq * 1024:plane * 4096 + (qq + 1) * 1024],
                  pXb[:, 0:1024], inc=("sV", 1))

    def w1qT_sl(dc, jc):
        buf, plane = W1T[dc]
        return b[buf][:, plane * 4096 + jc * 128:plane * 4096 + (jc + 1) * 128]


    FB = 128
    B_.barrier(V)
    B_.op(V, "memset", b["hscr"][:], 1.0)
    B_.op(V, "memset", b["zscr"][:], 1.0, inc=("sV", 1))

    if cut(8):
        return
    B_.dma("sync", b["c_fcw"][:].rearrange("p (j t) -> p j t", j=32),
           io["fcw"][:].rearrange("(j p) t -> p j t", p=128))
    B_.dma("sync", b["c_fcb"][:].unsqueeze(2),
           io["fcb"][:].rearrange("(j p) o -> p j o", p=128))
    B_.wait(SC, "sD")
    for tt in range(NT):
        B_.barrier("sync")
        B_.dma("sync", b["x1t"][:], sp["x1"][tt])
        B_.wait(V, "sD")
        B_.wait(SC, "sD")
        ln_core(b["x1t"][:], out_f32=b["lnt"][:])
        B_.wait(V, "sS")
        B_.op(V, "reduce_max", b["rowm"][:, 0:1].unsqueeze(1),
              b["lnt"][:].unsqueeze(1), axis=AX.X, apply_absolute_value=True,
              inc=("sF", 1))
        B_.wait(V, "sF")
        B_.op(V, "tensor_scalar_max", b["rowm"][:, 1:2], b["rowm"][:, 0:1],
              1e-5)
        B_.op(V, "reciprocal", b["rowm"][:, 2:3], b["rowm"][:, 1:2],
              inc=("sF", 1))
        B_.wait(V, "sF")
        B_.op(V, "tensor_scalar", b["sqs"][:], b["lnt"][:],
              b["rowm"][:, 2:3], 127.0, ALU.mult, ALU.mult)
        B_.op(V, "tensor_copy", b["xi32"][:], b["sqs"][:])
        B_.op(V, "tensor_copy", b["xit"][:], b["xi32"][:])
        B_.op(V, "tensor_scalar", b["rowm"][:, 3:4], b["rowm"][:, 1:2],
              b["ffsb"][:, 1:2], 1.0 / 127.0, ALU.mult, ALU.mult,
              inc=("sV", 1))
        B_.barrier("tensor", ("sV",))
        for fc in range(8):
            B_.tp(pXb[:, fc * 128:(fc + 1) * 128],
                  b["xit"][:, fc * 128:(fc + 1) * 128], id128b)
        B_.mm(pY[0:1, 0:128], b["rowm"][:, 3:4], id128, inc=("sT", 1))
        B_.wait(V, "sT")
        B_.op(V, "tensor_copy", b["hqt"][:, 0:1024], pXb[:, 0:1024])
        B_.op(V, "tensor_copy", b["hscr"][0:1, 1 + tt * 128:1 + (tt + 1) * 128],
              pY[0:1, 0:128], inc=("sV", 1))
        B_.wait("sync", "sV")
        B_.dma("sync", sp["xi"][tt], b["hqt"][:, 0:1024])

    B_.wait("sync", "sC", 2)
    B_.barrier("sync", ("sV", "sS", "sT"))
    B_.dma("sync", b["cball"][:], sp["cb_out"][:])
    B_.wait("tensor", "sD")
    B_.mm(pX[0:1, 0:512], b["c_sel"][:, 0:1], b["cball"][:, D:D + 512])
    B_.mm(pX[0:1, 512:1024], b["c_sel"][:, 0:1], b["cball"][:, D + 512:2 * D])
    B_.mm(pX[0:1, 1024:1536], b["c_sel"][:, 1:2], b["cball"][:, 0:512])
    B_.mm(pX[0:1, 1536:2048], b["c_sel"][:, 1:2], b["cball"][:, 512:D],
          inc=("sT", 1))
    B_.wait(V, "sT")
    B_.op(V, "tensor_copy", b["brow"][:], pX[0:1, 0:2048], inc=("sV", 1))
    for s in range(2):
        xrow = b["brow"][0:1, s * D:(s + 1) * D]
        ln_core(xrow, out_f32=b["lnt"][0:1, :])
        B_.wait(V, "sS")
        B_.op(V, "reduce_max", b["rowm"][0:1, 4:5].unsqueeze(1),
              b["lnt"][0:1, :].unsqueeze(1), axis=AX.X,
              apply_absolute_value=True, inc=("sF", 1))
        B_.wait(V, "sF")
        B_.op(V, "tensor_scalar_max", b["rowm"][0:1, 4:5], b["rowm"][0:1, 4:5],
              1e-5)
        B_.op(V, "reciprocal", b["rowm"][0:1, 5:6], b["rowm"][0:1, 4:5],
              inc=("sF", 1))
        B_.wait(V, "sF")
        B_.op(V, "tensor_scalar", b["sqs"][0:1, :], b["lnt"][0:1, :],
              b["rowm"][0:1, 5:6], 127.0, ALU.mult, ALU.mult)
        B_.op(V, "tensor_copy", b["xi32"][0:1, :], b["sqs"][0:1, :])
        B_.op(V, "tensor_copy", b["xit"][0:1, :], b["xi32"][0:1, :])
        B_.op(V, "tensor_scalar", b["rowm"][0:1, 6:7], b["rowm"][0:1, 4:5],
              b["ffs"][0:1, 1:2], 1.0 / 127.0, ALU.mult, ALU.mult,
              inc=("sV", 1))
        B_.barrier("tensor", ("sV",))
        for fc in range(8):
            B_.mm(pY[:, fc:fc + 1], b["xit"][0:1, fc * 128:(fc + 1) * 128],
                  id128b[0:1, 0:1],
                  inc=("sT", 1) if fc == 7 else None)
        B_.mm(pY[:, 8:9], b["onesrow"][:], b["rowm"][0:1, 6:7], inc=("sT", 1))
        B_.wait(V, "sT")
        B_.op(V, "tensor_copy", b["xif_h"][:], pY[:, 0:8])
        B_.op(V, "tensor_copy", b["rowm"][:, 6:7], pY[:, 8:9], inc=("sV", 1))
        B_.barrier("tensor", ("sV",))
        for jc in range(32):
            for dc in range(8):
                B_.mm(pX[:, jc:jc + 1], w1qT_sl(dc, jc),
                      b["xif_h"][:, dc:dc + 1], start=(dc == 0),
                      stop=(dc == 7),
                      inc=("sT", 1) if (jc == 31 and dc == 7) else None)
        B_.wait(V, "sT")
        B_.op(V, "tensor_scalar", b["hhalo"][:, s * 32:(s + 1) * 32],
              pX[:, 0:32], b["rowm"][:, 6:7], b["c_hm"][:, s:s + 1],
              ALU.mult, ALU.mult, inc=("sV", 1))

    if cut(9):
        return
    for fb in range(int(os.environ.get("KFB", NT))):
        B_.barrier("tensor", ("sV", "sS"))
        B_.mm(pX[:, 0:FB + 2], b["onesrow"][:],
              b["hscr"][0:1, fb * FB:fb * FB + FB + 2], inc=("sT", 1))
        B_.wait(V, "sT")
        B_.op(V, "tensor_copy", b["hscp"][:], pX[:, 0:FB + 2], inc=("sV", 1))
        B_.barrier("sync", ("sV",))
        if fb == 0:
            B_.dma("sync", b["xib"][:, 0:1024], sp["xi"][0])
        if fb + 1 < NT:
            B_.dma("sync", b["xib"][:, ((fb + 1) % 3) * 1024:
                                     ((fb + 1) % 3) * 1024 + 1024],
                   sp["xi"][fb + 1])
        B_.wait(V, "sD")
        xiwr = b["xiw"][:].rearrange("p (c f) -> p c f", c=8)
        cur = b["xib"][:, (fb % 3) * 1024:(fb % 3) * 1024 + 1024] \
            .rearrange("p (c f) -> p c f", c=8)
        B_.op(V, "tensor_copy", xiwr[:, :, 1:FB + 1], cur)
        if fb == 0:
            B_.op(V, "memset", xiwr[:, :, 0:1], 0.0)
        else:
            prev = b["xib"][:, ((fb - 1) % 3) * 1024:
                            ((fb - 1) % 3) * 1024 + 1024] \
                .rearrange("p (c f) -> p c f", c=8)
            B_.op(V, "tensor_copy", xiwr[:, :, 0:1], prev[:, :, 127:128])
        if fb == NT - 1:
            B_.op(V, "memset", xiwr[:, :, FB + 1:FB + 2], 0.0, inc=("sV", 1))
        else:
            nxt = b["xib"][:, ((fb + 1) % 3) * 1024:
                           ((fb + 1) % 3) * 1024 + 1024] \
                .rearrange("p (c f) -> p c f", c=8)
            B_.op(V, "tensor_copy", xiwr[:, :, FB + 1:FB + 2],
                  nxt[:, :, 0:1], inc=("sV", 1))
        B_.barrier("tensor", ("sV",))
        for grp in range(4):
            B_.barrier("tensor", ("sV",))
            for j2 in range(8):
                jc = grp * 8 + j2
                po = (pX if j2 < 4 else pY)[:, (j2 % 4) * 512:(j2 % 4) * 512 + FB + 2]
                for dc in range(8):
                    B_.mm(po, w1qT_sl(dc, jc),
                          b["xiw"][:, dc * (FB + 2):(dc + 1) * (FB + 2)],
                          start=(dc == 0), stop=(dc == 7),
                          inc=("sT", 1) if (j2 == 7 and dc == 7) else None)
            B_.wait(V, "sT")
            for j2 in range(8):
                jc = grp * 8 + j2
                po = (pX if j2 < 4 else pY)[:, (j2 % 4) * 512:(j2 % 4) * 512 + FB + 2]
                B_.op(V, "tensor_mul",
                      b["hraw"][:, jc * (FB + 2):(jc + 1) * (FB + 2)],
                      po, b["hscp"][:],
                      inc=("sV", 1) if j2 == 7 else None)
        if fb == 0:
            B_.op(V, "tensor_copy",
                  b["hraw"][:].rearrange("p (j f) -> p j f", j=32)[:, :, 0:1],
                  b["hhalo"][:, 0:32].unsqueeze(2), inc=("sV", 1))
        if fb == NT - 1:
            B_.op(V, "tensor_copy",
                  b["hraw"][:].rearrange("p (j f) -> p j f", j=32)
                  [:, :, FB + 1:FB + 2],
                  b["hhalo"][:, 32:64].unsqueeze(2), inc=("sV", 1))
        B_.barrier(SC, ("sV",))
        for jc in range(32):
            B_.act(b["hcs"][:, jc * FB:(jc + 1) * FB],
                   b["hraw"][:, jc * (FB + 2) + 1:jc * (FB + 2) + 1 + FB],
                   AF.Identity, scale=b["c_fcw"][:, jc * 3 + 1:jc * 3 + 2],
                   bias=b["c_fcb"][:, jc:jc + 1],
                   inc=("sS", 1) if jc == 31 else None)
        B_.wait(V, "sS")
        for j in (0, 2):
            for jc in range(32):
                B_.op(V, "scalar_tensor_tensor",
                      b["hcs"][:, jc * FB:(jc + 1) * FB],
                      b["hraw"][:, jc * (FB + 2) + j:jc * (FB + 2) + j + FB],
                      b["c_fcw"][:, jc * 3 + j:jc * 3 + j + 1],
                      b["hcs"][:, jc * FB:(jc + 1) * FB],
                      ALU.mult, ALU.add,
                      inc=("sV", 1) if (j == 2 and jc == 31) else None)
        B_.wait(SC, "sV")
        B_.act(b["hcs"][:], b["hcs"][:], AF.Silu, inc=("sS", 1))
        for rnd in range(2):
            B_.barrier("tensor", ("sV", "sS"))
            for jj in range(16):
                jc = rnd * 16 + jj
                po = pX if jj < 8 else pY
                B_.tp(po[:, (jj % 8) * 128:(jj % 8 + 1) * 128],
                      b["hcs"][:, jc * FB:(jc + 1) * FB], id128,
                      inc=("sT", 1) if jj == 15 else None)
            B_.wait(V, "sT")
            B_.op(V, "tensor_copy",
                  b["htt"][:, rnd * 2048:rnd * 2048 + 1024], pX[:, 0:1024])
            B_.op(V, "tensor_copy",
                  b["htt"][:, rnd * 2048 + 1024:rnd * 2048 + 2048],
                  pY[:, 0:1024], inc=("sV", 1))
        B_.op(V, "reduce_max", b["rowm"][:, 4:5].unsqueeze(1),
              b["htt"][:].unsqueeze(1), axis=AX.X, apply_absolute_value=True,
              inc=("sF", 1))
        B_.wait(V, "sF")
        B_.op(V, "tensor_scalar_max", b["rowm"][:, 5:6], b["rowm"][:, 4:5],
              1e-5)
        B_.op(V, "reciprocal", b["rowm"][:, 6:7], b["rowm"][:, 5:6],
              inc=("sF", 1))
        B_.wait(V, "sF")
        B_.op(V, "tensor_scalar", b["htt"][:], b["htt"][:],
              b["rowm"][:, 6:7], 127.0, ALU.mult, ALU.mult)
        for g in range(2):
            B_.op(V, "tensor_copy", b["hi32"][:],
                  b["htt"][:, g * 2048:(g + 1) * 2048])
            B_.op(V, "tensor_copy", b["hqt"][:, g * 2048:(g + 1) * 2048],
                  b["hi32"][:])
        B_.op(V, "tensor_scalar_mul", b["rowm"][:, 7:8], b["rowm"][:, 5:6],
              1.0 / 127.0, inc=("sV", 1))
        B_.barrier("tensor", ("sV",))
        B_.mm(pX[0:1, 1536:1664], b["rowm"][:, 7:8], id128, inc=("sT", 1))
        B_.wait(V, "sT")
        B_.op(V, "tensor_copy",
              b["zscr"][0:1, fb * FB:(fb + 1) * FB], pX[0:1, 1536:1664],
              inc=("sV", 1))
        for rnd in range(2):
            B_.barrier("tensor", ("sV",))
            for jj in range(16):
                jc = rnd * 16 + jj
                pob = pXb if jj < 8 else pYb
                B_.tp(pob[:, (jj % 8) * 128:(jj % 8 + 1) * 128],
                      b["hqt"][:, jc * 128:(jc + 1) * 128], id128b,
                      inc=("sT", 1) if jj == 15 else None)
            B_.wait(V, "sT")
            for g in range(2):
                pob = pXb if g == 0 else pYb
                B_.op(V, "tensor_copy",
                      b["hqf"][:].rearrange("p (j f) -> p j f", j=32)
                      [:, rnd * 16 + g * 8:rnd * 16 + (g + 1) * 8, :],
                      pob[:, 0:1024].rearrange("p (j f) -> p j f", j=8),
                      inc=("sV", 1) if g == 1 else None)
        B_.wait("sync", "sV")
        B_.dma("sync", sp["hqf"][fb], b["hqf"][:])

    def w2chunk(mc):
        return io["fw2"][mc * 128:(mc + 1) * 128, :]

    for mc in range(8):
        B_.barrier("sync")
        B_.dma("sync", b["wstage"][:, :DH], w2chunk(mc))
        B_.wait(V, "sD")
        B_.op(V, "reduce_sum", b["rowm"][:, mc:mc + 1].unsqueeze(1),
              b["wstage"][:, :DH].unsqueeze(1), axis=AX.X,
              apply_absolute_value=True, inc=("sV", 1))
    B_.barrier("tensor", ("sV",))
    B_.mm(pX[0:1, 0:8], b["c_ones"][:, 0:1], b["rowm"][:, 0:8], inc=("sT", 1))
    B_.wait(V, "sT")
    B_.op(V, "reduce_sum", b["ffs"][0:1, 3:4].unsqueeze(1),
          pX[0:1, 0:8].unsqueeze(1), axis=AX.X, inc=("sF", 1))
    B_.wait(V, "sF")
    B_.op(V, "tensor_scalar_mul", b["ffs"][0:1, 4:5], b["ffs"][0:1, 3:4],
          1.0 / (DH * D))
    B_.op(V, "tensor_scalar_max", b["ffs"][0:1, 4:5], b["ffs"][0:1, 4:5], 1e-5)
    B_.op(V, "reciprocal", b["ffs"][0:1, 5:6], b["ffs"][0:1, 4:5],
          inc=("sV", 1))
    B_.barrier("tensor", ("sV",))
    B_.mm(pX[:, 0:1], b["onesrow"][:], b["ffs"][0:1, 5:6])
    B_.mm(pX[:, 1:2], b["onesrow"][:], b["ffs"][0:1, 4:5], inc=("sT", 1))
    B_.wait(V, "sT")
    B_.op(V, "tensor_copy", b["ffsb"][:, 2:4], pX[:, 0:2], inc=("sV", 1))

    W2T = [("WCh", 0), ("WCh", 1), ("WCl", 0), ("WCl", 1),
           ("WDh", 0), ("WDh", 1), ("WDl", 0), ("WDl", 1)]

    def w2qT_sl(jc, mc):
        bufn = ("WCh", "WCl", "WDh", "WDl")[jc // 8]
        off = (jc % 8) * 1024 + mc * 128
        return b[bufn][:, off:off + 128]

    for mc in range(8):
        B_.barrier("sync")
        B_.dma("sync", b["wstage"][:, :DH], w2chunk(mc))
        B_.wait(V, "sD")
        for hh2 in range(2):
            sl = slice(hh2 * 2048, (hh2 + 1) * 2048)
            B_.op(V, "tensor_scalar", b["wstage"][:, sl], b["wstage"][:, sl],
                  b["ffsb"][:, 2:3], None, ALU.mult, ALU.bypass)
            B_.op(V, "tensor_scalar", b["wstage"][:, sl], b["wstage"][:, sl],
                  -1.0, 1.0, ALU.max, ALU.min)
            B_.op(V, "tensor_copy", b["wi32"][:], b["wstage"][:, sl])
            B_.op(V, "tensor_copy", b["hqt"][:, sl], b["wi32"][:],
                  inc=("sV", 1))
        for rnd in range(2):
            B_.barrier("tensor", ("sV",))
            for jj in range(16):
                jc = rnd * 16 + jj
                pob = pXb if jj < 8 else pYb
                B_.tp(pob[:, (jj % 8) * 128:(jj % 8 + 1) * 128],
                      b["hqt"][:, jc * 128:(jc + 1) * 128], id128b,
                      inc=("sT", 1) if jj == 15 else None)
            B_.wait(V, "sT")
            for g4 in range(2):
                pob = pXb if g4 == 0 else pYb
                base_jc = rnd * 16 + g4 * 8
                bufn = ("WCh", "WCl", "WDh", "WDl")[base_jc // 8]
                B_.op(V, "tensor_copy",
                      b[bufn][:].rearrange("p (j f) -> p j f", j=8)
                      [:, :, mc * 128:(mc + 1) * 128],
                      pob[:, 0:1024].rearrange("p (j f) -> p j f", j=8),
                      inc=("sV", 1) if g4 == 1 else None)

    if cut(10):
        return
    for fb in range(NT):
        B_.barrier("sync")
        B_.dma("sync", b["hqf"][:], sp["hqf"][fb])
        B_.barrier("tensor", ("sV", "sS"))
        B_.mm(pX[:, 0:FB], b["onesrow"][:],
              b["zscr"][0:1, fb * FB:(fb + 1) * FB], inc=("sT", 1))
        B_.wait(V, "sT")
        B_.op(V, "tensor_scalar", b["zscp"][:], pX[:, 0:FB],
              b["ffsb"][:, 3:4], None, ALU.mult, ALU.bypass, inc=("sV", 1))
        if _PS < 2:
            continue
        B_.barrier("tensor", ("sV", "sD"))
        for mc in range(8):
            po = (pX if mc < 4 else pY)[:, (mc % 4) * 512:(mc % 4) * 512 + FB]
            for jc in range(32):
                B_.mm(po, w2qT_sl(jc, mc), b["hqf"][:, jc * FB:(jc + 1) * FB],
                      start=(jc == 0), stop=(jc == 31),
                      inc=("sT", 1) if (mc == 7 and jc == 31) else None)
        B_.wait(V, "sT")
        for mc in range(8):
            po = (pX if mc < 4 else pY)[:, (mc % 4) * 512:(mc % 4) * 512 + FB]
            B_.op(V, "tensor_mul", b["zf"][:, mc * FB:(mc + 1) * FB], po,
                  b["zscp"][:], inc=("sV", 1) if mc == 7 else None)
        if _PS < 3:
            continue
        B_.barrier("tensor", ("sV", "sS"))
        for mc in range(8):
            B_.tp(pX[:, mc * 128:(mc + 1) * 128],
                  b["zf"][:, mc * FB:(mc + 1) * FB], id128,
                  inc=("sT", 1) if mc == 7 else None)
        B_.wait(V, "sT")
        B_.op(V, "tensor_copy", b["xt"][:], pX[:, 0:1024], inc=("sV", 1))
        ln_core(b["xt"][:], out_f32=b["lnt"][:])
        B_.barrier("sync", ("sV", "sS"))
        B_.dma("sync", b["x1t"][:], sp["x1"][fb])
        B_.wait(V, "sD")
        B_.wait(V, "sS")
        B_.op(V, "tensor_add", b["praw"][:, 0:1024], b["lnt"][:],
              b["x1t"][:], inc=("sV", 1))
        B_.wait("sync", "sV")
        B_.dma("sync", out[fb * 128:(fb + 1) * 128, :], b["praw"][:, 0:1024])

    _dbg_tap()

    B_.wait("sync", "sD")
    B_.wait("sync", "sG")


_NC_CACHE = {}


def _get_nc():
    if "nc" not in _NC_CACHE:
        _NC_CACHE["nc"] = build_nc()
    return _NC_CACHE["nc"]


def kernel(**inputs):
    x = np.asarray(inputs["L0_attn_x"], dtype=np.float32)
    f32 = np.float32

    def c_(a):
        return np.ascontiguousarray(np.asarray(a, dtype=f32))

    base = {
        "wq": c_(inputs["L0_attn_Wq"]), "wk": c_(inputs["L0_attn_Wk"]),
        "wv": c_(inputs["L0_attn_Wv"]), "wg": c_(inputs["L0_attn_Wg"]),
        "wo": c_(inputs["L0_attn_Wo"]), "wb": c_(inputs["L0_attn_Wb"]),
        "cqw": c_(inputs["L0_attn_cq_w"]),
        "cqb": c_(inputs["L0_attn_cq_b"]).reshape(D, 1),
        "ckw": c_(inputs["L0_attn_ck_w"]),
        "ckb": c_(inputs["L0_attn_ck_b"]).reshape(D, 1),
        "cvw": c_(inputs["L0_attn_cv_w"]),
        "cvb": c_(inputs["L0_attn_cv_b"]).reshape(D, 1),
        "nwrep": np.ascontiguousarray(
            np.tile(np.asarray(inputs["L0_attn_nw"], f32).reshape(1, d),
                    (128, H))),
        "fw1": c_(inputs["L0_ffn_w1"]),
        "fw2": c_(inputs["L0_ffn_w2"]),
        "fcw": c_(inputs["L0_ffn_conv_w"]).reshape(DH, 3),
        "fcb": c_(inputs["L0_ffn_conv_b"]).reshape(DH, 1),
        "ident128": np.eye(128, dtype=f32),
        "identpair": np.ascontiguousarray(
            np.tile(np.concatenate([np.eye(d, dtype=f32)] * 2, axis=0),
                    (1, 8))),
        "m_trilS": np.ascontiguousarray(
            np.tile(np.tril(np.ones((128, 128), f32), -1), (1, H))),
        "m_triuS": np.ascontiguousarray(
            np.tile(np.triu(np.ones((128, 128), f32), 1), (1, H))),
        "m_triuI": np.ascontiguousarray(
            np.tile(np.triu(np.ones((128, 128), f32), 0), (1, H))),
        "onescol": np.ones((128, 2), f32),
    }
    in_maps = []
    for core in range(8):
        bidx, pos = core // 4, core % 4
        seg0 = pos * R
        halo3 = (x[bidx, seg0 - 3:seg0] if pos > 0
                 else np.zeros((3, D), f32))
        x_seg = np.concatenate(
            [np.zeros((125, D), f32), halo3, x[bidx, seg0:seg0 + R]], axis=0)
        pm = np.zeros((128, 4), f32)
        pm[:, :3] = [1.0 if j < pos else 0.0 for j in range(3)]
        pm1 = np.zeros((128, 4), f32)
        pm1[:, :3] = 1.0 - pm[:, :3]
        hm = np.zeros((128, 2), f32)
        hm[:, 0] = 1.0 if pos > 0 else 0.0
        hm[:, 1] = 1.0 if pos < 3 else 0.0
        cs = np.zeros((4, 2), f32)
        if pos > 0:
            cs[pos - 1, 0] = 1.0
        if pos < 3:
            cs[pos + 1, 1] = 1.0
        m = dict(base)
        m["x_seg"] = np.ascontiguousarray(x_seg)
        m["pmask"] = pm
        m["pmask1"] = pm1
        m["hmask"] = hm
        m["csel"] = cs
        in_maps.append(m)

    nc = _get_nc()
    res = run_bass_kernel_spmd(nc, in_maps, core_ids=list(range(8)))
    outp = np.empty((B, T, D), dtype=np.float32)
    for core in range(8):
        bidx, pos = core // 4, core % 4
        outp[bidx, pos * R:(pos + 1) * R, :] = res.results[core]["out"]
    return outp
